# version 03
# baseline (speedup 1.0000x reference)
import contextlib
import numpy as np
import ml_dtypes
import concourse.bass as bass
import concourse.mybir as mybir
from concourse.bass_utils import run_bass_kernel_spmd

F32 = mybir.dt.float32
BF16 = mybir.dt.bfloat16
I32 = mybir.dt.int32
AF = mybir.ActivationFunctionType
ALU = mybir.AluOpType

D = 1024
E = 2048
EH = 1024
NST = 64
FH = 3072
DT = D // 128
ET = E // 128
EHT = EH // 128
FHT = FH // 128
EPS = 1e-5

_SIZES_F = [("ident", 128), ("wi", (ET + EHT) * DT * 128), ("wdt", EHT * ET * 128),
            ("wbc", 1 * ET * 128), ("wout", EHT * D),
            ("wgu", 2 * ET * DT * 128), ("wdn", DT * ET * 128)]
_SIZES_I = [("fgu", 2 * FHT * DT * 128), ("fdn", DT * FHT * 128)]
XF_COLS = sum(s for _, s in _SIZES_F)
XI_COLS = sum(s for _, s in _SIZES_I)
WOFF = {}
_o = 0
for _n, _s in _SIZES_F:
    WOFF[_n] = _o
    _o += _s
for _n, _s in _SIZES_I:
    WOFF[_n] = _o
    _o += _s
OFF_CW, OFF_CB, OFF_DTB, OFF_WIN, OFF_WFN, OFF_AL = 0, 64, 80, 88, 96, 104
OFF_K = 168
CONSTS = (1e-5, 1.0, -1.0, -0.5)
SM_COLS = OFF_K + len(CONSTS)


def _tileize(w, kt, mt):
    a = np.asarray(w).reshape(kt, 128, mt, 128).transpose(2, 1, 0, 3)
    return np.ascontiguousarray(
        a.reshape(mt, 128, kt * 128).transpose(1, 0, 2).reshape(128, mt * kt * 128))


def _rhs_tiles(w, kt, f):
    a = np.asarray(w).reshape(kt, 128, f)
    return np.ascontiguousarray(a.transpose(1, 0, 2).reshape(128, kt * f))


def _interleave_m(a, b):
    Dd, M = a.shape
    mt = M // 128
    o = np.empty((Dd, 2 * M), a.dtype)
    for m in range(mt):
        o[:, (2 * m) * 128:(2 * m + 1) * 128] = a[:, m * 128:(m + 1) * 128]
        o[:, (2 * m + 1) * 128:(2 * m + 2) * 128] = b[:, m * 128:(m + 1) * 128]
    return o


def pack_core(inp, b, j):
    f32 = np.float32
    x = np.asarray(inp["L0_attn_x"], f32)[b]
    T = x.shape[0]
    xT = np.ascontiguousarray(x.T)
    xo = np.ascontiguousarray(xT[:, j * (T // 2):(j + 1) * (T // 2)])

    perm = np.concatenate([np.arange(j * EH, (j + 1) * EH),
                           np.arange((1 - j) * EH, (2 - j) * EH)])
    Wi = np.asarray(inp["L0_attn_in_proj_W"], f32)
    cols = np.concatenate([perm, E + j * EH + np.arange(EH)])
    parts_f = [
        np.eye(128, dtype=f32),
        _tileize(Wi[:, cols], DT, ET + EHT),
        _tileize(np.asarray(inp["L0_attn_dt_W"], f32)[perm][:, j * EH:(j + 1) * EH],
                 ET, EHT),
        _tileize(np.concatenate([np.asarray(inp["L0_attn_B_W"], f32)[perm],
                                 np.asarray(inp["L0_attn_C_W"], f32)[perm]], axis=1),
                 ET, 1),
        _rhs_tiles(np.asarray(inp["L0_attn_out_proj_W"], f32)[j * EH:(j + 1) * EH],
                   EHT, D),
        _tileize(_interleave_m(np.asarray(inp["L0_attn_gate_W"], f32),
                               np.asarray(inp["L0_attn_up_W"], f32)), DT, 2 * ET),
        _tileize(np.asarray(inp["L0_attn_down_W"], f32), ET, DT),
    ]
    wf = np.concatenate(parts_f, axis=1)
    parts_i = [
        _tileize(_interleave_m(np.asarray(inp["L0_ffn_w_gate"], np.int32).T,
                               np.asarray(inp["L0_ffn_w_up"], np.int32).T),
                 DT, 2 * FHT),
        _tileize(np.asarray(inp["L0_ffn_w_down"], np.int32).T, FHT, DT),
    ]
    wi = np.concatenate(parts_i, axis=1)

    cw = np.asarray(inp["L0_attn_conv_W"], f32)[:, 0, :].T[perm]
    cwp = cw.reshape(ET, 128, 4).transpose(1, 0, 2).reshape(128, ET * 4)
    cb = np.asarray(inp["L0_attn_conv_b"], f32)[perm].reshape(ET, 128).T
    dtb = np.asarray(inp["L0_attn_dt_b"], f32)[j * EH:(j + 1) * EH].reshape(EHT, 128).T
    win = np.asarray(inp["L0_attn_input_norm_w"], f32).reshape(DT, 128).T
    wfn = np.asarray(inp["L0_attn_ffn_norm_w"], f32).reshape(DT, 128).T
    alog = np.broadcast_to(np.asarray(inp["L0_attn_A_log"], f32)[None, :],
                           (128, NST)).copy()
    kc = np.broadcast_to(np.asarray(CONSTS, f32)[None, :], (128, len(CONSTS))).copy()
    smalls = np.concatenate([cwp, cb, dtb, win, wfn, alog, kc], axis=1).astype(f32)
    assert smalls.shape[1] == SM_COLS
    return {"x": xT, "x_own": xo, "wf32": np.ascontiguousarray(wf),
            "wi32": np.ascontiguousarray(wi), "smalls": smalls}


class Op:
    __slots__ = ("eng", "waits", "inc", "amt", "fn")

    def __init__(self, eng, waits, inc, amt, fn):
        self.eng, self.waits, self.inc, self.amt, self.fn = eng, waits, inc, amt, fn


class Prog:
    ENG = ("sync", "tensor", "vector", "scalar", "gpsimd")

    def __init__(self):
        self.ops = []
        self.cnt = {}
        self.seen = {e: {} for e in self.ENG}

    def reg(self, s):
        self.cnt[s.name] = 0
        return s

    def emit(self, eng, fn, waits=(), inc=None, amt=1):
        waits = list(waits)
        if eng in ("vector", "scalar") and inc is not None:
            waits.append((inc, self.cnt[inc.name]))
        kept = []
        for (s, c) in waits:
            if c > 0 and self.seen[eng].get(s.name, 0) < c:
                kept.append((s, c))
                self.seen[eng][s.name] = c
        self.ops.append(Op(eng, kept, inc, amt, fn))
        if inc is not None:
            self.cnt[inc.name] += amt
            return self.cnt[inc.name]
        return None

    def now(self, s):
        return self.cnt[s.name]


def build_nc(T=8192, FC=1024, FS=512, NG=4, debug=False, dump=False, stop_after=None):
    NCH = T // FC
    NSUB = T // FS
    SPC = FC // FS
    TH = T // 2
    THS = TH // FS
    NGR = NST // NG

    nc = bass.Bass("TRN2", target_bir_lowering=False, debug=debug)

    x_in = nc.dram_tensor("x", [D, T], F32, kind="ExternalInput").ap()
    xown = nc.dram_tensor("x_own", [D, TH], F32, kind="ExternalInput").ap()
    wf32 = nc.dram_tensor("wf32", [128, XF_COLS], F32, kind="ExternalInput").ap()
    wi32 = nc.dram_tensor("wi32", [128, XI_COLS], I32, kind="ExternalInput").ap()
    smalls = nc.dram_tensor("smalls", [128, SM_COLS], F32, kind="ExternalInput").ap()
    out = nc.dram_tensor("out", [D, TH], F32, kind="ExternalOutput").ap()

    wbf_t = nc.dram_tensor("wbf", [128, XF_COLS + XI_COLS], BF16)
    partial_t = nc.dram_tensor("partial", [T, D], BF16)
    s1own_t = nc.dram_tensor("s1own", [TH, D], BF16)
    gdram_t = nc.dram_tensor("gdram", [D, TH], BF16)
    wbf = wbf_t.ap()
    partial = partial_t.ap()
    s1own = s1own_t.ap()
    gdram = gdram_t.ap()

    P = Prog()
    dumps = {}

    with contextlib.ExitStack() as st:
        def sb(name, shape, dt):
            return st.enter_context(nc.sbuf_tensor(name, shape, dt))

        def psb(name, shape):
            return st.enter_context(nc.psum_tensor(name, shape, F32))

        def sem(name):
            return P.reg(st.enter_context(nc.semaphore(name)))

        xsub = sb("xsub", [128, DT * FS], F32)
        normed = sb("normed", [128, DT * FS], BF16)
        WSLOT = 3072
        wst = sb("wst", [128, 2 * WSLOT], BF16)
        xzs = [sb(f"xz{i}", [128, ET * (FS + 3)], BF16) for i in range(2)]
        chunkbuf = sb("chunkbuf", [128, max(3 * EHT * FC, FHT * FS)], BF16)
        zsil = chunkbuf[:, 0:EHT * FC]
        dtc = chunkbuf[:, EHT * FC:2 * EHT * FC]
        xch = chunkbuf[:, 2 * EHT * FC:3 * EHT * FC]
        h1b = chunkbuf[:, 0:FHT * FS]
        xco = sb("xco", [128, EHT * FS], BF16)
        bcb = sb("bcb", [128, FC], BF16)
        bcgbuf = sb("bcgbuf", [128, max(4 * NG * FC, 4 * DT * FS)], BF16)
        bcg = [bcgbuf[:, 0:2 * NG * FC], bcgbuf[:, 2 * NG * FC:4 * NG * FC]]
        x32a = bcgbuf[:, 0:2 * DT * FS].bitcast(F32)
        x32b = bcgbuf[:, 2 * DT * FS:4 * DT * FS].bitcast(F32)
        dabuf = sb("dabuf", [128, max(2 * FC, DT * FS // 2)], F32)
        dAb = [dabuf[:, 0:FC], dabuf[:, FC:2 * FC]]
        sfin = dabuf[:, 0:DT * FS // 2].bitcast(BF16)
        scanb = sb("scanb", [128, max(4 * FC, DT * FS)], BF16)
        ub = [scanb[:, 0:FC], scanb[:, FC:2 * FC]]
        hb = scanb[:, 2 * FC:3 * FC]
        ycb = scanb[:, 3 * FC:4 * FC]
        gbuf = scanb[:, 0:DT * FS]
        cvtf = bcgbuf[:, 0:2 * DT * FS].bitcast(F32)
        cvto = scanb[:, 0:min(4096, 4 * FC)]
        tre = sb("tre", [128, max(2 * FC, 2 * FS)], BF16)
        sgbuf = tre[:, 0:FS]
        upbuf = tre[:, FS:2 * FS]
        ybuf = sb("ybuf", [128, EHT * FC], BF16)
        hst = sb("hst", [128, EHT * NST], F32)
        ptile = sb("ptile", [128, 4 * 128], BF16)
        spt = sb("spt", [128, 2 * FS], F32)
        sp2 = spt
        rows = sb("rows", [1, 2 * FS], F32)
        rowb = sb("rowb", [1, FS], BF16)
        onesc = sb("onesc", [128, 1], BF16)
        onesr = sb("onesr", [1, 128], BF16)
        cst = sb("cst", [128, SM_COLS], F32)
        identb = sb("identb", [128, 128], BF16)
        tailb = [sb(f"tail{i}", [128, ET], BF16) for i in range(2)]

        pA = psb("pA", [128, FS])
        pB = psb("pB", [128, FS])
        pC = psb("pC", [128, FS])
        pDm = psb("pD", [128, FS])
        pS1 = psb("pS1", [1, FS])
        pBC = psb("pBC", [128, FS])
        pM = psb("pM", [128, FS])
        pSc = psb("pSc", [128, FS])

        sIn = [sem(f"sIn{i}") for i in range(4)]
        sG = [sem(f"sG{i}") for i in range(4)]
        sScq = [sem(f"sScq{i}") for i in range(4)]
        sPe = sem("sPe")
        sV = sem("sV")
        sA = sem("sA")
        sCC = sem("sCC")

        EM = P.emit
        NOW = P.now
        for ki, kv in enumerate(CONSTS):
            nc.const_aps.register(F32, kv, cst[:, OFF_K + ki:OFF_K + ki + 1])

        QSEM = {"sync": sIn, "gpsimd": sG, "scalar": sScq}
        QROT = {"sync": 0, "gpsimd": 0, "scalar": 0}

        def DMA(eng, fn, waits=()):
            q = QSEM[eng][QROT[eng] % 4]
            QROT[eng] += 1
            w = list(waits) + [(q, P.cnt[q.name])]
            EM(eng, fn, waits=w, inc=q, amt=16)
            return (q, P.cnt[q.name])

        def qsnap(eng):
            return [(s, P.cnt[s.name]) for s in QSEM[eng]]

        def DUMP(name, ap, waits):
            if not dump:
                return
            t = nc.dram_tensor("dbg_" + name, list(ap.shape), ap.dtype)
            dumps[name] = t
            DMA("gpsimd", lambda e, ap=ap, t=t: e.dma_start(out=t.ap(), in_=ap),
                waits=waits)
        pfree = {id(pA): (sV, 0), id(pB): (sV, 0), id(pC): (sV, 0), id(pDm): (sV, 0),
                 id(pS1): (sV, 0), id(pBC): (sV, 0), id(pM): (sV, 0), id(pSc): (sV, 0)}

        c_sm = DMA("sync", lambda e: e.dma_start(out=cst[:], in_=smalls))
        EM("vector", lambda e: e.memset(onesc[:], 1.0), inc=sV)
        EM("vector", lambda e: e.memset(onesr[:], 1.0), inc=sV)
        EM("vector", lambda e: e.memset(hst[:], 0.0), inc=sV)
        c_A1 = EM("scalar", lambda e: e.activation(
            out=cst[:, OFF_AL:OFF_AL + NST], in_=cst[:, OFF_AL:OFF_AL + NST],
            func=AF.Exp), waits=[c_sm], inc=sA)
        c_A = EM("vector", lambda e: e.tensor_scalar(
            out=cst[:, OFF_AL:OFF_AL + NST], in0=cst[:, OFF_AL:OFF_AL + NST],
            scalar1=-1.0, scalar2=None, op0=ALU.mult),
            waits=[(sA, c_A1)], inc=sV)

        CH = min(4096, 4 * FC)
        prev_cv = 0
        for (tot, srcap, isint) in ((XF_COLS, wf32, False), (XI_COLS, wi32, True)):
            nchk = (tot + CH - 1) // CH
            base = 0 if not isint else XF_COLS
            for i in range(nchk):
                c0, w = i * CH, min((i + 1) * CH, tot) - i * CH
                st_toks = qsnap("gpsimd")
                if isint:
                    ld = DMA("sync", lambda e, c0=c0, w=w, srcap=srcap: e.dma_start(
                        out=cvtf[:, 0:w].bitcast(I32),
                        in_=srcap[:, c0:c0 + w]),
                        waits=[(sV, prev_cv)])
                    cv = EM("vector", lambda e, w=w: e.tensor_scalar(
                        out=cvto[:, 0:w], in0=cvtf[:, 0:w].bitcast(I32),
                        scalar1=1.0 / 127.0, scalar2=None, op0=ALU.mult),
                        waits=[ld] + st_toks, inc=sV)
                else:
                    ld = DMA("sync", lambda e, c0=c0, w=w, srcap=srcap: e.dma_start(
                        out=cvtf[:, 0:w], in_=srcap[:, c0:c0 + w]),
                        waits=[(sV, prev_cv)])
                    cv = EM("vector", lambda e, w=w: e.tensor_copy(
                        out=cvto[:, 0:w], in_=cvtf[:, 0:w]),
                        waits=[ld] + st_toks, inc=sV)
                prev_cv = cv
                DMA("gpsimd", lambda e, c0=c0, w=w, base=base: e.dma_start(
                    out=wbf[:, base + c0:base + c0 + w], in_=cvto[:, 0:w]),
                    waits=[(sV, cv)])
        c_wready = qsnap("gpsimd")
        c_id = DMA("sync", lambda e: e.dma_start(
            out=identb[:], in_=wbf[:, WOFF["ident"]:WOFF["ident"] + 128]),
            waits=c_wready)

        x_r = x_in.rearrange("(dt p) t -> p dt t", p=128)
        xo_r = xown.rearrange("(dt p) t -> p dt t", p=128)

        def mmwait(pp, extra):
            w = list(extra)
            w.append(pfree[id(pp)])
            return w

        def evac_done(pp, sem_, cnt):
            pfree[id(pp)] = (sem_, cnt)

        def emit_ln(src_r, t0, f, wcol, want_f32=None, nb_dst=None):
            nb = nb_dst if nb_dst is not None else normed
            ld = DMA("sync", lambda e: e.dma_start(
                out=xsub[:].rearrange("p (dt t) -> p dt t", t=f),
                in_=src_r[:, :, t0:t0 + f]),
                waits=[(sV, P.cnt[sV.name]), (sA, P.cnt[sA.name])])
            pe_snap = P.cnt[sPe.name]
            cps = []
            for d in range(DT):
                cps.append(EM("vector", lambda e, d=d: e.tensor_copy(
                    out=nb[:, d * f:(d + 1) * f], in_=xsub[:, d * f:(d + 1) * f]),
                    waits=[ld, (sPe, pe_snap)], inc=sV))
            sqs = []
            for d in range(DT):
                sqs.append(EM("scalar", lambda e, d=d: e.activation(
                    out=xco[:, d * f:(d + 1) * f], in_=xsub[:, d * f:(d + 1) * f],
                    func=AF.Square), waits=[ld, (sPe, pe_snap)], inc=sA))
            m1 = m2 = None
            for d in range(DT):
                m1 = EM("tensor", lambda e, d=d: e.matmul(
                    pS1[:, 0:f], lhsT=onesc[:], rhs=nb[:, d * f:(d + 1) * f],
                    start=(d == 0), stop=(d == DT - 1)),
                    waits=mmwait(pS1, [(sV, cps[d])]) if d == 0 else [(sV, cps[d])],
                    inc=(sPe if d == DT - 1 else None))
            cm0 = EM("vector", lambda e: e.tensor_scalar(
                out=rows[:, 0:f], in0=pS1[0:1, 0:f], scalar1=1.0 / D, scalar2=None,
                op0=ALU.mult), waits=[(sPe, m1)], inc=sV)
            evac_done(pS1, sV, cm0)
            for d in range(DT):
                m2 = EM("tensor", lambda e, d=d: e.matmul(
                    pS1[:, 0:f], lhsT=onesc[:], rhs=xco[:, d * f:(d + 1) * f],
                    start=(d == 0), stop=(d == DT - 1)),
                    waits=mmwait(pS1, [(sA, sqs[d])]) if d == 0 else [(sA, sqs[d])],
                    inc=(sPe if d == DT - 1 else None))
            cvm = EM("vector", lambda e: e.tensor_copy(
                out=rowb[:, 0:f], in_=rows[:, 0:f]), inc=sV)
            mb = EM("tensor", lambda e: e.matmul(
                pM[:, 0:f], lhsT=onesr[:], rhs=rowb[:, 0:f], start=True, stop=True),
                waits=mmwait(pM, [(sV, cvm)]), inc=sPe)
            EM("vector", lambda e: e.tensor_tensor(
                out=rows[:, f:2 * f], in0=rows[:, 0:f], in1=rows[:, 0:f],
                op=ALU.mult), inc=sV)
            cv1 = EM("vector", lambda e: e.scalar_tensor_tensor(
                out=rows[:, f:2 * f], in0=pS1[0:1, 0:f], scalar=1.0 / D,
                in1=rows[:, f:2 * f], op0=ALU.mult, op1=ALU.subtract),
                waits=[(sPe, m2)], inc=sV)
            evac_done(pS1, sV, cv1)
            EM("scalar", lambda e: e.activation(
                out=rows[:, f:2 * f], in_=rows[:, f:2 * f], func=AF.Ln, bias=EPS),
                waits=[(sV, cv1)], inc=sA)
            ca = EM("scalar", lambda e: e.activation(
                out=rows[:, 0:f], in_=rows[:, f:2 * f], func=AF.Exp,
                scale=-0.5), waits=[(sV, cvm)], inc=sA)
            if wcol is not None:
                EM("scalar", lambda e: e.activation(
                    out=rows[:, f:2 * f], in_=rows[:, f:2 * f], func=AF.Exp,
                    scale=-1.0), inc=sA)
                ca2 = EM("vector", lambda e: e.tensor_scalar(
                    out=rows[:, f:2 * f], in0=rows[:, f:2 * f], scalar1=-EPS,
                    scalar2=1.0, op0=ALU.mult, op1=ALU.add),
                    waits=[(sA, NOW(sA))], inc=sV)
                EM("scalar", lambda e: e.activation(
                    out=rows[:, f:2 * f], in_=rows[:, f:2 * f], func=AF.Ln,
                    bias=EPS), waits=[(sV, ca2)], inc=sA)
                EM("scalar", lambda e: e.activation(
                    out=rows[:, f:2 * f], in_=rows[:, f:2 * f], func=AF.Exp,
                    scale=-0.5), inc=sA)
                cv3 = EM("vector", lambda e: e.tensor_tensor(
                    out=rows[:, f:2 * f], in0=rows[:, 0:f],
                    in1=rows[:, f:2 * f], op=ALU.mult),
                    waits=[(sA, NOW(sA))], inc=sV)
            else:
                cv3 = EM("vector", lambda e: e.tensor_copy(
                    out=rows[:, f:2 * f], in_=rows[:, 0:f]),
                    waits=[(sA, ca)], inc=sV)
            cv4 = EM("vector", lambda e: e.tensor_copy(
                out=rowb[:, 0:f], in_=rows[:, f:2 * f]),
                waits=[(sPe, mb)], inc=sV)
            sc = EM("tensor", lambda e: e.matmul(
                pSc[:, 0:f], lhsT=onesr[:], rhs=rowb[:, 0:f], start=True,
                stop=True), waits=mmwait(pSc, [(sV, cv4)]), inc=sPe)
            last = None
            sa_snap = P.cnt[sA.name]
            for d in range(DT):
                EM("vector", lambda e, d=d: e.tensor_tensor(
                    out=spt[:, 0:f], in0=xsub[:, d * f:(d + 1) * f], in1=pM[:, 0:f],
                    op=ALU.subtract), waits=[(sPe, sc), (sA, sa_snap)], inc=sV)
                if want_f32 is not None:
                    EM("vector", lambda e, d=d: e.tensor_tensor(
                        out=want_f32[:, d * f:(d + 1) * f], in0=spt[:, 0:f],
                        in1=pSc[:, 0:f], op=ALU.mult), inc=sV)
                if wcol is not None:
                    last = EM("vector", lambda e, d=d: e.scalar_tensor_tensor(
                        out=nb[:, d * f:(d + 1) * f], in0=spt[:, 0:f],
                        scalar=cst[:, wcol + d:wcol + d + 1], in1=pSc[:, 0:f],
                        op0=ALU.mult, op1=ALU.mult), inc=sV)
                else:
                    last = EM("vector", lambda e, d=d: e.tensor_tensor(
                        out=nb[:, d * f:(d + 1) * f], in0=spt[:, 0:f],
                        in1=pSc[:, 0:f], op=ALU.mult), inc=sV)
            evac_done(pM, sV, last)
            evac_done(pSc, sV, last)
            return last

        wload_slot_free = [(sPe, 0), (sPe, 0)]

        def w_load(region, m, kt, slot):
            off = WOFF[region] + m * kt * 128
            return DMA("sync", lambda e: e.dma_start(
                out=wst[:, slot * WSLOT:slot * WSLOT + kt * 128],
                in_=wbf[:, off:off + kt * 128]),
                waits=[wload_slot_free[slot]] + c_wready)

        def wtile(slot, k):
            return wst[:, slot * WSLOT + k * 128:slot * WSLOT + (k + 1) * 128]

        def mm_chain(region, m, kt, rhs_fn, pp, slot, extra_waits=()):
            lw = w_load(region, m, kt, slot)
            last = None
            for k in range(kt):
                w = [lw] + list(extra_waits)
                if k == 0:
                    w = mmwait(pp, w)
                last = EM("tensor",
                          lambda e, k=k, pp=pp, slot=slot, rhs_fn=rhs_fn: e.matmul(
                              pp[:], lhsT=wtile(slot, k), rhs=rhs_fn(k),
                              start=(k == 0), stop=(k == kt - 1)),
                          waits=w, inc=(sPe if k == kt - 1 else None))
            wload_slot_free[slot] = (sPe, last)
            return last

        c_conv_done = [0] * NSUB
        c_z_read = 0
        c_scan_done_v = 0
        c_scan_done_a = 0
        c_ln_q = {}

        def emit_A_xz(s):
            sc = s % SPC
            xz = xzs[s % 2]
            gate = [(sV, c_conv_done[s - 2])] if s >= 2 else []
            c_n = emit_ln(x_r, s * FS, FS, OFF_WIN)
            c_ln_q[s] = c_n
            for m in range(ET):
                pp = (pA, pB)[m % 2]
                slot = m % 2
                mm = mm_chain("wi", m, DT, lambda k: normed[:, k * FS:(k + 1) * FS],
                              pp, slot, extra_waits=[(sV, c_n)])
                ca = EM("scalar", lambda e, m=m, pp=pp, xz=xz, sc=sc: e.activation(
                    out=xz[:, m * (FS + 3) + 1:m * (FS + 3) + 1 + FS],
                    in_=pp[:], func=AF.Copy),
                    waits=[(sPe, mm)] + gate, inc=sA)
                evac_done(pp, sA, ca)
            return NOW(sA)

        def emit_A_z(s):
            sc = s % SPC
            for mz in range(EHT):
                m = ET + mz
                pp = (pA, pB)[m % 2]
                slot = m % 2
                mm = mm_chain("wi", m, DT, lambda k: normed[:, k * FS:(k + 1) * FS],
                              pp, slot, extra_waits=[(sV, c_ln_q[s]), (sV, c_z_read)])
                ca = EM("scalar", lambda e, mz=mz, pp=pp, sc=sc: e.activation(
                    out=zsil[:, mz * FC + sc * FS:mz * FC + (sc + 1) * FS],
                    in_=pp[:], func=AF.Silu),
                    waits=[(sPe, mm)], inc=sA)
                evac_done(pp, sA, ca)
            return NOW(sA)

        def emit_patch(s, c_next_xz):
            xz = xzs[s % 2]
            tb = tailb[s % 2]
            tbp = tailb[(s - 1) % 2]
            last = None
            for m in range(ET):
                b0 = m * (FS + 3)
                if s == NSUB - 1:
                    last = EM("vector", lambda e, b0=b0: e.memset(
                        xz[:, b0 + 1 + FS:b0 + 3 + FS], 0.0),
                        waits=[(sA, c_next_xz)], inc=sV)
                else:
                    xzn = xzs[(s + 1) % 2]
                    last = EM("vector", lambda e, b0=b0, xzn=xzn: e.tensor_copy(
                        out=xz[:, b0 + 1 + FS:b0 + 3 + FS],
                        in_=xzn[:, b0 + 1:b0 + 3]),
                        waits=[(sA, c_next_xz)], inc=sV)
                last = EM("vector", lambda e, b0=b0, m=m, tb=tb: e.tensor_copy(
                    out=tb[:, m:m + 1], in_=xz[:, b0 + FS:b0 + FS + 1]), inc=sV)
                if s == 0:
                    last = EM("vector", lambda e, b0=b0: e.memset(
                        xz[:, b0:b0 + 1], 0.0), inc=sV)
                else:
                    last = EM("vector", lambda e, b0=b0, m=m, tbp=tbp: e.tensor_copy(
                        out=xz[:, b0:b0 + 1], in_=tbp[:, m:m + 1]), inc=sV)
            return last

        def emit_conv(s, c_patch):
            sc = s % SPC
            xz = xzs[s % 2]
            sil_counts = []
            for et in range(ET):
                b0 = et * (FS + 3)
                sl = et % 2
                EM("vector", lambda e, b0=b0, et=et, sl=sl: e.tensor_scalar(
                    out=spt[:, sl * FS:(sl + 1) * FS], in0=xz[:, b0:b0 + FS],
                    scalar1=cst[:, OFF_CW + et * 4:OFF_CW + et * 4 + 1],
                    scalar2=None, op0=ALU.mult),
                    waits=[(sV, c_patch), (sA, sil_counts[-2] if et >= 2 else 0),
                           (sA, c_scan_done_a)],
                    inc=sV)
                for w in range(1, 4):
                    EM("vector", lambda e, b0=b0, et=et, w=w, sl=sl:
                       e.scalar_tensor_tensor(
                           out=spt[:, sl * FS:(sl + 1) * FS],
                           in0=xz[:, b0 + w:b0 + w + FS],
                           scalar=cst[:, OFF_CW + et * 4 + w:OFF_CW + et * 4 + w + 1],
                           in1=spt[:, sl * FS:(sl + 1) * FS],
                           op0=ALU.mult, op1=ALU.add), inc=sV)
                cvv = NOW(sV)
                if et < EHT:
                    dst = xch[:, et * FC + sc * FS:et * FC + (sc + 1) * FS]
                else:
                    dst = xco[:, (et - EHT) * FS:(et - EHT + 1) * FS]
                sil_counts.append(EM("scalar", lambda e, dst=dst, et=et, sl=sl:
                                     e.activation(out=dst,
                                                  in_=spt[:, sl * FS:(sl + 1) * FS],
                                                  func=AF.Silu,
                                                  bias=cst[:, OFF_CB + et:OFF_CB + et + 1]),
                                     waits=[(sV, cvv), (sPe, P.cnt[sPe.name])],
                                     inc=sA))
            c_conv_done[s] = NOW(sV)
            return sil_counts[-1]

        def xc_rhs(sc):
            def f(k):
                if k < EHT:
                    return xch[:, k * FC + sc * FS:k * FC + (sc + 1) * FS]
                return xco[:, (k - EHT) * FS:(k - EHT + 1) * FS]
            return f

        def emit_dtbc(s, c_conv):
            sc = s % SPC
            for m in range(EHT):
                pp = (pC, pDm)[m % 2]
                slot = m % 2
                mm = mm_chain("wdt", m, ET, xc_rhs(sc), pp, slot,
                              extra_waits=[(sA, c_conv)])
                EM("scalar", lambda e, pp=pp, m=m: e.activation(
                    out=sp2[:, 0:FS], in_=pp[:], func=AF.Abs,
                    bias=cst[:, OFF_DTB + m:OFF_DTB + m + 1]),
                    waits=[(sPe, mm), (sV, NOW(sV))], inc=sA)
                EM("scalar", lambda e: e.activation(
                    out=sp2[:, 0:FS], in_=sp2[:, 0:FS], func=AF.Exp, scale=-1.0),
                    inc=sA)
                EM("scalar", lambda e: e.activation(
                    out=sp2[:, 0:FS], in_=sp2[:, 0:FS], func=AF.Ln, bias=1.0),
                    inc=sA)
                ca = EM("scalar", lambda e, pp=pp, m=m: e.activation(
                    out=sp2[:, FS:2 * FS], in_=pp[:], func=AF.Relu,
                    bias=cst[:, OFF_DTB + m:OFF_DTB + m + 1]), inc=sA)
                evac_done(pp, sA, ca)
                EM("vector", lambda e: e.tensor_tensor(
                    out=sp2[:, 0:FS], in0=sp2[:, 0:FS], in1=sp2[:, FS:2 * FS],
                    op=ALU.add), waits=[(sA, ca)], inc=sV)
                EM("vector", lambda e, m=m, sc=sc: e.tensor_scalar(
                    out=dtc[:, m * FC + sc * FS:m * FC + (sc + 1) * FS],
                    in0=sp2[:, 0:FS], scalar1=0.1, scalar2=0.001,
                    op0=ALU.min, op1=ALU.max),
                    waits=[(sA, c_scan_done_a)], inc=sV)
            mm = mm_chain("wbc", 0, ET, xc_rhs(sc), pC, 0,
                          extra_waits=[(sA, c_conv)])
            ca = EM("scalar", lambda e, sc=sc: e.activation(
                out=bcb[:, sc * FS:(sc + 1) * FS], in_=pC[:], func=AF.Copy),
                waits=[(sPe, mm), (sPe, c_bcb_free[0])], inc=sA)
            evac_done(pC, sA, ca)
            return NOW(sV), ca

        def emit_scan(c, c_dt_v, c_bc_a):
            nonlocal c_scan_done_v, c_scan_done_a
            for et in range(EHT):
                EM("vector", lambda e, et=et: e.tensor_tensor(
                    out=xch[:, et * FC:(et + 1) * FC],
                    in0=dtc[:, et * FC:(et + 1) * FC],
                    in1=xch[:, et * FC:(et + 1) * FC], op=ALU.mult),
                    waits=[(sA, c_bc_a), (sV, c_dt_v)] + qsnap("gpsimd"),
                    inc=sV)
            c_grp_prev = 0
            c_scan_i = {}
            for g in range(NGR):
                n0 = g * NG
                bc_ready = []
                for ni in range(NG):
                    n = n0 + ni
                    for (ci, row) in ((0, n), (1, NST + n)):
                        dst_base = (ci * 4 + ni) * FC
                        for half in range(FC // FS):
                            mmb = EM("tensor", lambda e, row=row, half=half:
                                     e.matmul(
                                         pBC[:],
                                         lhsT=identb[:, row:row + 1].rearrange(
                                             "p (o f) -> p o f", o=1
                                         ).broadcast_to([128, 1, 128]),
                                         rhs=bcb[:, half * FS:(half + 1) * FS],
                                         start=True, stop=True),
                                     waits=mmwait(pBC, [(sA, c_bc_a),
                                                        (sV, c_grp_prev),
                                                        c_id]),
                                     inc=sPe)
                            cev = EM("scalar", lambda e, dst_base=dst_base,
                                     half=half: e.activation(
                                         out=bcgbuf[:, dst_base + half * FS:
                                                    dst_base + (half + 1) * FS],
                                         in_=pBC[:], func=AF.Copy),
                                     waits=[(sPe, mmb)], inc=sA)
                            evac_done(pBC, sA, cev)
                    bc_ready.append(NOW(sA))
                for et in range(EHT):
                    for ni in range(NG):
                        n = n0 + ni
                        idx = g * EHT * NG + et * NG + ni
                        da = dAb[idx % 2]
                        uu = ub[idx % 2]
                        ce = EM("scalar", lambda e, da=da, et=et, n=n: e.activation(
                            out=da, in_=dtc[:, et * FC:(et + 1) * FC],
                            func=AF.Exp,
                            scale=cst[:, OFF_AL + n:OFF_AL + n + 1]),
                            waits=[(sV, c_dt_v), (sV, c_A),
                                   (sV, c_scan_i.get(idx - 2, 0))], inc=sA)
                        cu = EM("vector", lambda e, uu=uu, et=et, ni=ni:
                                e.tensor_tensor(
                                    out=uu, in0=xch[:, et * FC:(et + 1) * FC],
                                    in1=bcgbuf[:, ni * FC:(ni + 1) * FC],
                                    op=ALU.mult),
                                waits=[(sA, bc_ready[ni])], inc=sV)
                        csc = EM("vector", lambda e, da=da, uu=uu, et=et, n=n:
                                 e.tensor_tensor_scan(
                                     out=hb, data0=da, data1=uu,
                                     initial=hst[:, et * NST + n:et * NST + n + 1],
                                     op0=ALU.mult, op1=ALU.add),
                                 waits=[(sA, ce)], inc=sV)
                        c_scan_i[idx] = csc
                        EM("vector", lambda e, et=et, n=n: e.tensor_copy(
                            out=hst[:, et * NST + n:et * NST + n + 1],
                            in_=hb[:, FC - 1:FC]), inc=sV)
                        if c == 0 and g == 0 and et == 0 and ni == 0:
                            DUMP("da00", da, [(sA, ce)])
                            DUMP("u00", uu, [(sV, cu)])
                            DUMP("h00", hb, [(sV, csc)])
                        cslice = bcgbuf[:, (4 + ni) * FC:(4 + ni + 1) * FC]
                        ycdst = ycb if ni % 2 == 0 else tre[:, (ni // 2) * FC:(ni // 2 + 1) * FC]
                        EM("vector", lambda e, cs=cslice, d=ycdst: e.tensor_tensor(
                            out=d, in0=hb, in1=cs, op=ALU.mult), inc=sV)
                        if ni % 2 == 1:
                            EM("vector", lambda e, sl=ni // 2: e.tensor_tensor(
                                out=tre[:, sl * FC:(sl + 1) * FC], in0=ycb,
                                in1=tre[:, sl * FC:(sl + 1) * FC], op=ALU.add),
                                inc=sV)
                        if ni == NG - 1:
                            EM("vector", lambda e: e.tensor_tensor(
                                out=ycb, in0=tre[:, 0:FC], in1=tre[:, FC:2 * FC],
                                op=ALU.add), inc=sV)
                            ydst = ybuf[:, et * FC:(et + 1) * FC]
                            if g == 0:
                                EM("vector", lambda e, yd=ydst: e.tensor_copy(
                                    out=yd, in_=ycb),
                                    waits=[(sPe, c_ybuf_free[0])], inc=sV)
                            else:
                                EM("vector", lambda e, yd=ydst: e.tensor_tensor(
                                    out=yd, in0=yd, in1=ycb, op=ALU.add),
                                    inc=sV)
                c_grp_prev = NOW(sV)
            c_bcb_free[0] = P.cnt[sPe.name]
            c_scan_done_v = NOW(sV)
            c_scan_done_a = NOW(sA)
            return c_scan_done_v

        def emit_yzop(c, c_scan_v):
            nonlocal c_z_read
            for et in range(EHT):
                EM("vector", lambda e, et=et: e.tensor_tensor(
                    out=xch[:, et * FC:(et + 1) * FC],
                    in0=ybuf[:, et * FC:(et + 1) * FC],
                    in1=zsil[:, et * FC:(et + 1) * FC], op=ALU.mult),
                    waits=[(sV, c_scan_v), (sA, NOW(sA))] + qsnap("gpsimd"),
                    inc=sV)
            c_yz = NOW(sV)
            c_z_read = c_yz
            wo_r = wbf[:, WOFF["wout"]:WOFF["wout"] + EHT * D].rearrange(
                "p (k d) -> p k d", d=D)
            npc = D // 128
            for pc_ in range(npc):
                slot = pc_ % 2
                lw = DMA("sync", lambda e, pc_=pc_, slot=slot: e.dma_start(
                    out=wst[:, slot * WSLOT:slot * WSLOT + EHT * 128].rearrange(
                        "p (k f) -> p k f", f=128),
                    in_=wo_r[:, :, pc_ * 128:(pc_ + 1) * 128]),
                    waits=[wload_slot_free[slot]] + c_wready)
                lastmm = None
                for tt in range(FC // 128):
                    pp = (pA, pB)[tt % 2]
                    last = None
                    for k in range(EHT):
                        w = [(sV, c_yz), lw]
                        if k == 0:
                            w = mmwait(pp, w)
                        last = EM("tensor", lambda e, pp=pp, k=k, tt=tt, slot=slot:
                                  e.matmul(
                                      pp[:, 0:128],
                                      lhsT=xch[:, k * FC + tt * 128:k * FC + (tt + 1) * 128],
                                      rhs=wst[:, slot * WSLOT + k * 128:slot * WSLOT + (k + 1) * 128],
                                      start=(k == 0), stop=(k == EHT - 1)),
                                  waits=w, inc=(sPe if k == EHT - 1 else None))
                    lastmm = last
                    psl = (pc_ * (FC // 128) + tt) % 4
                    cv = EM("vector", lambda e, pp=pp, psl=psl: e.tensor_copy(
                        out=ptile[:, psl * 128:(psl + 1) * 128], in_=pp[:, 0:128]),
                        waits=[(sPe, last), c_pt_free[psl]], inc=sV)
                    evac_done(pp, sV, cv)
                    cst_ = DMA("gpsimd", lambda e, tt=tt, c=c, pc_=pc_, psl=psl:
                               e.dma_start(
                        out=partial[c * FC + tt * 128:c * FC + (tt + 1) * 128,
                                    pc_ * 128:(pc_ + 1) * 128],
                        in_=ptile[:, psl * 128:(psl + 1) * 128]),
                        waits=[(sV, cv)])
                    c_pt_free[psl] = cst_
                wload_slot_free[slot] = (sPe, lastmm)
            c_ybuf_free[0] = P.cnt[sPe.name]

        c_pt_free = [(sV, 0)] * 4
        c_bcb_free = [0]
        c_ybuf_free = [0]

        pend = None
        for s in range(NSUB if stop_after != "prologue" else 0):
            c_axz = emit_A_xz(s)
            if s >= 1:
                sp = s - 1
                c_p = emit_patch(sp, c_axz)
                c_conv = emit_conv(sp, c_p)
                c_dt_v, c_bc_a = emit_dtbc(sp, c_conv)
                if (sp + 1) % SPC == 0:
                    cch = sp // SPC
                    if cch == 0:
                        DUMP("xch0", xch[:], [(sV, c_dt_v), (sA, c_bc_a)])
                        DUMP("dtc0", dtc[:], [(sV, c_dt_v)])
                        DUMP("bcb0", bcb[:], [(sA, c_bc_a)])
                    c_sv = emit_scan(cch, c_dt_v, c_bc_a)
                    if cch == 0:
                        DUMP("ybuf0", ybuf[:], [(sV, c_sv)])
                        DUMP("hst0", hst[:], [(sV, c_sv)])
                        DUMP("zsil0", zsil[:], [(sA, P.cnt[sA.name])])
                    emit_yzop(cch, c_sv)
            emit_A_z(s)
        if stop_after != "prologue":
            sp = NSUB - 1
            c_p = emit_patch(sp, NOW(sA))
            c_conv = emit_conv(sp, c_p)
            c_dt_v, c_bc_a = emit_dtbc(sp, c_conv)
            c_sv = emit_scan(NCH - 1, c_dt_v, c_bc_a)
            emit_yzop(NCH - 1, c_sv)
        c_pi_st = qsnap("gpsimd")

        c_g_free = [(sV, 0), (sV, 0)]
        for su in range(THS if stop_after in (None, "g", "cc", "pii") else 0):
            c_n = emit_ln(xo_r, su * FS, FS, None)
            c_h1_prev = 0
            for mh in range(FHT):
                mmg = mm_chain("fgu", 2 * mh, DT,
                               lambda k: normed[:, k * FS:(k + 1) * FS], pA, 0,
                               extra_waits=[(sV, c_n)])
                mmu = mm_chain("fgu", 2 * mh + 1, DT,
                               lambda k: normed[:, k * FS:(k + 1) * FS], pB, 1,
                               extra_waits=[(sV, c_n)])
                ca1 = EM("scalar", lambda e: e.activation(
                    out=sgbuf, in_=pA[:], func=AF.Silu),
                    waits=[(sPe, mmg), (sV, c_h1_prev)], inc=sA)
                ca2 = EM("scalar", lambda e: e.activation(
                    out=upbuf, in_=pB[:], func=AF.Copy),
                    waits=[(sPe, mmu)], inc=sA)
                evac_done(pA, sA, ca1)
                evac_done(pB, sA, ca2)
                c_h1_prev = EM("vector", lambda e, mh=mh: e.tensor_tensor(
                    out=h1b[:, mh * FS:(mh + 1) * FS], in0=sgbuf,
                    in1=upbuf, op=ALU.mult),
                    waits=[(sA, ca2), (sPe, P.cnt[sPe.name])], inc=sV)
            c_h1 = NOW(sV)
            for md in range(DT):
                pp = (pC, pDm)[md % 2]
                sl = md % 2
                mm = mm_chain("fdn", md, FHT,
                              lambda k: h1b[:, k * FS:(k + 1) * FS], pp, md % 2,
                              extra_waits=[(sV, c_h1)])
                ca = EM("scalar", lambda e, pp=pp, sl=sl: e.activation(
                    out=gbuf[:, sl * FS:(sl + 1) * FS], in_=pp[:], func=AF.Silu),
                    waits=[(sPe, mm), c_g_free[sl]], inc=sA)
                evac_done(pp, sA, ca)
                cg = DMA("gpsimd", lambda e, md=md, su=su, sl=sl: e.dma_start(
                    out=gdram.rearrange("(dt p) t -> p dt t", p=128)[
                        :, md, su * FS:(su + 1) * FS],
                    in_=gbuf[:, sl * FS:(sl + 1) * FS]),
                    waits=[(sA, ca)])
                c_g_free[sl] = cg

        c_cc = 0
        if stop_after in (None, "cc", "pii"):
            c_cc = EM("gpsimd", lambda e: e.collective_compute(
                "ReduceScatter", ALU.add,
                replica_groups=[[0, 1], [2, 3], [4, 5], [6, 7]],
                ins=[partial.opt()], outs=[s1own.opt()]),
                waits=c_pi_st, inc=sCC, amt=1)

        s1_r = s1own
        if stop_after not in (None, "pii"):
            for su in range(THS):
                DMA("gpsimd", lambda e, su=su: e.dma_start(
                    out=out.rearrange("(dt p) t -> p dt t", p=128)[
                        :, :, su * FS:(su + 1) * FS],
                    in_=xsub[:].rearrange("p (dt t) -> p dt t", t=FS)),
                    waits=[(sV, P.cnt[sV.name])])
        for su in range(THS if stop_after in (None, "pii") else 0):
            c_n = emit_ln(xo_r, su * FS, FS, None, want_f32=x32a)
            lds = []
            sv_snap2 = P.cnt[sV.name]
            for d in range(DT):
                lds.append(DMA("scalar", lambda e, d=d, su=su: e.dma_start(
                    out=sfin[:, d * FS:(d + 1) * FS],
                    in_=s1_r[su * FS:(su + 1) * FS, d * 128:(d + 1) * 128],
                    transpose=True),
                    waits=[(sCC, c_cc), (sV, sv_snap2)]))
            cmid = None
            for d in range(DT):
                cmid = EM("vector", lambda e, d=d: e.tensor_tensor(
                    out=x32b[:, d * FS:(d + 1) * FS], in0=x32a[:, d * FS:(d + 1) * FS],
                    in1=sfin[:, d * FS:(d + 1) * FS], op=ALU.add),
                    waits=[lds[d], (sV, c_n)], inc=sV)
            sqs = []
            for d in range(DT):
                sqs.append(EM("scalar", lambda e, d=d: e.activation(
                    out=xco[:, d * FS:(d + 1) * FS], in_=x32b[:, d * FS:(d + 1) * FS],
                    func=AF.Square), waits=[(sV, cmid)], inc=sA))
            m2 = None
            for d in range(DT):
                m2 = EM("tensor", lambda e, d=d: e.matmul(
                    pS1[:, 0:FS], lhsT=onesc[:], rhs=xco[:, d * FS:(d + 1) * FS],
                    start=(d == 0), stop=(d == DT - 1)),
                    waits=mmwait(pS1, [(sA, sqs[d])]) if d == 0 else [(sA, sqs[d])],
                    inc=(sPe if d == DT - 1 else None))
            cv1 = EM("vector", lambda e: e.tensor_scalar(
                out=rows[:, 0:FS], in0=pS1[0:1, 0:FS], scalar1=1.0 / D, scalar2=None,
                op0=ALU.mult), waits=[(sPe, m2)], inc=sV)
            evac_done(pS1, sV, cv1)
            EM("scalar", lambda e: e.activation(
                out=rows[:, FS:2 * FS], in_=rows[:, 0:FS], func=AF.Ln, bias=EPS),
                waits=[(sV, cv1)], inc=sA)
            ca = EM("scalar", lambda e: e.activation(
                out=rows[:, FS:2 * FS], in_=rows[:, FS:2 * FS], func=AF.Exp,
                scale=-0.5), inc=sA)
            cv2 = EM("vector", lambda e: e.tensor_copy(
                out=rowb[:, 0:FS], in_=rows[:, FS:2 * FS]),
                waits=[(sA, ca)], inc=sV)
            mb = EM("tensor", lambda e: e.matmul(
                pSc[:, 0:FS], lhsT=onesr[:], rhs=rowb[:, 0:FS], start=True,
                stop=True), waits=mmwait(pSc, [(sV, cv2)]), inc=sPe)
            cn2 = None
            for d in range(DT):
                cn2 = EM("vector", lambda e, d=d: e.scalar_tensor_tensor(
                    out=normed[:, d * FS:(d + 1) * FS],
                    in0=x32b[:, d * FS:(d + 1) * FS],
                    scalar=cst[:, OFF_WFN + d:OFF_WFN + d + 1], in1=pSc[:, 0:FS],
                    op0=ALU.mult, op1=ALU.mult), waits=[(sPe, mb)], inc=sV)
            evac_done(pSc, sV, cn2)
            c_h1_prev = 0
            for mh in range(ET):
                mmg = mm_chain("wgu", 2 * mh, DT,
                               lambda k: normed[:, k * FS:(k + 1) * FS], pA, 0,
                               extra_waits=[(sV, cn2)])
                mmu = mm_chain("wgu", 2 * mh + 1, DT,
                               lambda k: normed[:, k * FS:(k + 1) * FS], pB, 1,
                               extra_waits=[(sV, cn2)])
                ca1 = EM("scalar", lambda e: e.activation(
                    out=sgbuf, in_=pA[:], func=AF.Silu),
                    waits=[(sPe, mmg), (sV, c_h1_prev)], inc=sA)
                ca2 = EM("scalar", lambda e: e.activation(
                    out=upbuf, in_=pB[:], func=AF.Copy),
                    waits=[(sPe, mmu)], inc=sA)
                evac_done(pA, sA, ca1)
                evac_done(pB, sA, ca2)
                c_h1_prev = EM("vector", lambda e, mh=mh: e.tensor_tensor(
                    out=h1b[:, mh * FS:(mh + 1) * FS], in0=sgbuf,
                    in1=upbuf, op=ALU.mult),
                    waits=[(sA, ca2), (sPe, P.cnt[sPe.name])], inc=sV)
            c_h1 = NOW(sV)
            ldg = DMA("scalar", lambda e, su=su: e.dma_start(
                out=gbuf.rearrange("p (dt t) -> p dt t", t=FS),
                in_=gdram.rearrange("(dt p) t -> p dt t", p=128)[
                    :, :, su * FS:(su + 1) * FS]),
                waits=qsnap("gpsimd") + [(sV, c_h1)])
            for md in range(DT):
                pp = (pC, pDm)[md % 2]
                mm = mm_chain("wdn", md, ET,
                              lambda k: h1b[:, k * FS:(k + 1) * FS], pp, md % 2,
                              extra_waits=[(sV, c_h1)])
                cs = EM("vector", lambda e, pp=pp, md=md: e.tensor_tensor(
                    out=sfin[:, md * FS:(md + 1) * FS], in0=pp[:],
                    in1=x32b[:, md * FS:(md + 1) * FS], op=ALU.add),
                    waits=[(sPe, mm)], inc=sV)
                evac_done(pp, sV, cs)
                EM("vector", lambda e, md=md: e.tensor_tensor(
                    out=spt[:, 0:FS], in0=sfin[:, md * FS:(md + 1) * FS],
                    in1=gbuf[:, md * FS:(md + 1) * FS], op=ALU.mult),
                    waits=[ldg], inc=sV)
                cfo = EM("vector", lambda e, md=md: e.tensor_tensor(
                    out=x32a[:, md * FS:(md + 1) * FS],
                    in0=xsub[:, md * FS:(md + 1) * FS], in1=spt[:, 0:FS],
                    op=ALU.add), inc=sV)
            c_fin = NOW(sV)
            DMA("gpsimd", lambda e, su=su: e.dma_start(
                out=out.rearrange("(dt p) t -> p dt t", p=128)[
                    :, :, su * FS:(su + 1) * FS],
                in_=x32a[:].rearrange("p (dt t) -> p dt t", t=FS)),
                waits=[(sV, c_fin)])
        EM("gpsimd", lambda e: e.nop(),
           waits=qsnap("gpsimd") + qsnap("sync") + qsnap("scalar"))

        nc._dumps = dumps
        qs = {e: [] for e in Prog.ENG}
        for op in P.ops:
            qs[op.eng].append(op)

        def runner(ops):
            def run(eng):
                for op in ops:
                    for (s, c) in op.waits:
                        eng.wait_ge(s, c)
                    ins = op.fn(eng)
                    if op.inc is not None:
                        ins.then_inc(op.inc, op.amt)
            return run

        with nc.Block() as block:
            block.sync(runner(qs["sync"]))
            block.tensor(runner(qs["tensor"]))
            block.vector(runner(qs["vector"]))
            block.scalar(runner(qs["scalar"]))
            block.gpsimd(runner(qs["gpsimd"]))

    return nc


def kernel(**inputs):
    T = int(np.asarray(inputs["L0_attn_x"]).shape[1])
    nc = build_nc(T=T)
    in_maps = []
    for core in range(8):
        b, j = core // 2, core % 2
        in_maps.append(pack_core(inputs, b, j))
    res = run_bass_kernel_spmd(nc, in_maps, core_ids=list(range(8)))
    x = np.asarray(inputs["L0_attn_x"])
    B, T_, D_ = x.shape
    out = np.empty((B, T_, D_), np.float32)
    for core in range(8):
        b, j = core // 2, core % 2
        o = res.results[core]["out"]
        out[b, j * (T_ // 2):(j + 1) * (T_ // 2), :] = o.T
    return out


# version 05
# speedup vs baseline: 1.1549x; 1.1549x over previous
import contextlib
import numpy as np
import ml_dtypes
import concourse.bass as bass
import concourse.mybir as mybir
from concourse.bass_utils import run_bass_kernel_spmd

F32 = mybir.dt.float32
BF16 = mybir.dt.bfloat16
I32 = mybir.dt.int32
AF = mybir.ActivationFunctionType
ALU = mybir.AluOpType

D = 1024
E = 2048
EH = 1024
NST = 64
FH = 3072
DT = D // 128
ET = E // 128
EHT = EH // 128
FHT = FH // 128
EPS = 1e-5

_SIZES_F = [("ident", 128), ("wi", (ET + EHT) * DT * 128), ("wdt", EHT * ET * 128),
            ("wbc", 1 * ET * 128), ("wout", EHT * D),
            ("wgu", 2 * ET * DT * 128), ("wdn", DT * ET * 128)]
_SIZES_I = [("fgu", 2 * FHT * DT * 128), ("fdn", DT * FHT * 128)]
XF_COLS = sum(s for _, s in _SIZES_F)
XI_COLS = sum(s for _, s in _SIZES_I)
WOFF = {}
_o = 0
for _n, _s in _SIZES_F:
    WOFF[_n] = _o
    _o += _s
for _n, _s in _SIZES_I:
    WOFF[_n] = _o
    _o += _s
OFF_CW, OFF_CB, OFF_DTB, OFF_WIN, OFF_WFN, OFF_AL = 0, 64, 80, 88, 96, 104
OFF_K = 168
CONSTS = (1e-5, 1.0, -1.0, -0.5)
SM_COLS = OFF_K + len(CONSTS)


def _tileize(w, kt, mt):
    a = np.asarray(w).reshape(kt, 128, mt, 128).transpose(2, 1, 0, 3)
    return np.ascontiguousarray(
        a.reshape(mt, 128, kt * 128).transpose(1, 0, 2).reshape(128, mt * kt * 128))


def _rhs_tiles(w, kt, f):
    a = np.asarray(w).reshape(kt, 128, f)
    return np.ascontiguousarray(a.transpose(1, 0, 2).reshape(128, kt * f))


def _interleave_m(a, b):
    Dd, M = a.shape
    mt = M // 128
    o = np.empty((Dd, 2 * M), a.dtype)
    for m in range(mt):
        o[:, (2 * m) * 128:(2 * m + 1) * 128] = a[:, m * 128:(m + 1) * 128]
        o[:, (2 * m + 1) * 128:(2 * m + 2) * 128] = b[:, m * 128:(m + 1) * 128]
    return o


def pack_core(inp, b, j):
    f32 = np.float32
    x = np.asarray(inp["L0_attn_x"], f32)[b]
    T = x.shape[0]
    xT = np.ascontiguousarray(x.T)
    xo = np.ascontiguousarray(xT[:, j * (T // 2):(j + 1) * (T // 2)])

    perm = np.concatenate([np.arange(j * EH, (j + 1) * EH),
                           np.arange((1 - j) * EH, (2 - j) * EH)])
    Wi = np.asarray(inp["L0_attn_in_proj_W"], f32)
    cols = np.concatenate([perm, E + j * EH + np.arange(EH)])
    parts_f = [
        np.eye(128, dtype=f32),
        _tileize(Wi[:, cols], DT, ET + EHT),
        _tileize(np.asarray(inp["L0_attn_dt_W"], f32)[perm][:, j * EH:(j + 1) * EH],
                 ET, EHT),
        _tileize(np.concatenate([np.asarray(inp["L0_attn_B_W"], f32)[perm],
                                 np.asarray(inp["L0_attn_C_W"], f32)[perm]], axis=1),
                 ET, 1),
        _rhs_tiles(np.asarray(inp["L0_attn_out_proj_W"], f32)[j * EH:(j + 1) * EH],
                   EHT, D),
        _tileize(_interleave_m(np.asarray(inp["L0_attn_gate_W"], f32),
                               np.asarray(inp["L0_attn_up_W"], f32)), DT, 2 * ET),
        _tileize(np.asarray(inp["L0_attn_down_W"], f32), ET, DT),
    ]
    wf = np.concatenate(parts_f, axis=1)
    parts_i = [
        _tileize(_interleave_m(np.asarray(inp["L0_ffn_w_gate"], np.int32).T,
                               np.asarray(inp["L0_ffn_w_up"], np.int32).T),
                 DT, 2 * FHT),
        _tileize(np.asarray(inp["L0_ffn_w_down"], np.int32).T, FHT, DT),
    ]
    wi = np.concatenate(parts_i, axis=1)

    cw = np.asarray(inp["L0_attn_conv_W"], f32)[:, 0, :].T[perm]
    cwp = cw.reshape(ET, 128, 4).transpose(1, 0, 2).reshape(128, ET * 4)
    cb = np.asarray(inp["L0_attn_conv_b"], f32)[perm].reshape(ET, 128).T
    dtb = np.asarray(inp["L0_attn_dt_b"], f32)[j * EH:(j + 1) * EH].reshape(EHT, 128).T
    win = np.asarray(inp["L0_attn_input_norm_w"], f32).reshape(DT, 128).T
    wfn = np.asarray(inp["L0_attn_ffn_norm_w"], f32).reshape(DT, 128).T
    alog = np.broadcast_to(np.asarray(inp["L0_attn_A_log"], f32)[None, :],
                           (128, NST)).copy()
    kc = np.broadcast_to(np.asarray(CONSTS, f32)[None, :], (128, len(CONSTS))).copy()
    smalls = np.concatenate([cwp, cb, dtb, win, wfn, alog, kc], axis=1).astype(f32)
    assert smalls.shape[1] == SM_COLS
    return {"x": xT, "x_own": xo, "wf32": np.ascontiguousarray(wf),
            "wi32": np.ascontiguousarray(wi), "smalls": smalls}


class Op:
    __slots__ = ("eng", "waits", "inc", "amt", "fn")

    def __init__(self, eng, waits, inc, amt, fn):
        self.eng, self.waits, self.inc, self.amt, self.fn = eng, waits, inc, amt, fn


class Prog:
    ENG = ("sync", "tensor", "vector", "scalar", "gpsimd")

    def __init__(self):
        self.ops = []
        self.cnt = {}
        self.seen = {e: {} for e in self.ENG}

    def reg(self, s):
        self.cnt[s.name] = 0
        return s

    def emit(self, eng, fn, waits=(), inc=None, amt=1, selfw=True):
        waits = list(waits)
        if eng in ("vector", "scalar") and selfw and inc is not None:
            waits.append((inc, self.cnt[inc.name]))
        kept = []
        for (s, c) in waits:
            if c > 0 and self.seen[eng].get(s.name, 0) < c:
                kept.append((s, c))
                self.seen[eng][s.name] = c
        self.ops.append(Op(eng, kept, inc, amt, fn))
        if inc is not None:
            self.cnt[inc.name] += amt
            return self.cnt[inc.name]
        return None

    def now(self, s):
        return self.cnt[s.name]


def build_nc(T=8192, FC=1024, FS=512, NG=4, debug=False, dump=False, stop_after=None):
    NCH = T // FC
    NSUB = T // FS
    SPC = FC // FS
    TH = T // 2
    THS = TH // FS
    NGR = NST // NG

    nc = bass.Bass("TRN2", target_bir_lowering=False, debug=debug)

    x_in = nc.dram_tensor("x", [D, T], F32, kind="ExternalInput").ap()
    xown = nc.dram_tensor("x_own", [D, TH], F32, kind="ExternalInput").ap()
    wf32 = nc.dram_tensor("wf32", [128, XF_COLS], F32, kind="ExternalInput").ap()
    wi32 = nc.dram_tensor("wi32", [128, XI_COLS], I32, kind="ExternalInput").ap()
    smalls = nc.dram_tensor("smalls", [128, SM_COLS], F32, kind="ExternalInput").ap()
    out = nc.dram_tensor("out", [D, TH], F32, kind="ExternalOutput").ap()

    wbf_t = nc.dram_tensor("wbf", [128, XF_COLS + XI_COLS], BF16)
    partial_t = nc.dram_tensor("partial", [T, D], BF16)
    s1own_t = nc.dram_tensor("s1own", [TH, D], BF16)
    gdram_t = nc.dram_tensor("gdram", [D, TH], BF16)
    wbf = wbf_t.ap()
    partial = partial_t.ap()
    s1own = s1own_t.ap()
    gdram = gdram_t.ap()

    P = Prog()
    dumps = {}

    with contextlib.ExitStack() as st:
        def sb(name, shape, dt):
            return st.enter_context(nc.sbuf_tensor(name, shape, dt))

        def psb(name, shape):
            return st.enter_context(nc.psum_tensor(name, shape, F32))

        def sem(name):
            return P.reg(st.enter_context(nc.semaphore(name)))

        xsub = sb("xsub", [128, DT * FS], F32)
        normed = sb("normed", [128, DT * FS], BF16)
        WSLOT = 3072
        wst = sb("wst", [128, 2 * WSLOT], BF16)
        xzs = [sb(f"xz{i}", [128, ET * (FS + 3)], BF16) for i in range(2)]
        chunkbuf = sb("chunkbuf", [128, max(3 * EHT * FC, FHT * FS)], BF16)
        zsil = chunkbuf[:, 0:EHT * FC]
        dtc = chunkbuf[:, EHT * FC:2 * EHT * FC]
        xch = chunkbuf[:, 2 * EHT * FC:3 * EHT * FC]
        h1b = chunkbuf[:, 0:FHT * FS]
        xco = sb("xco", [128, EHT * FS], BF16)
        bcb = sb("bcb", [128, FC], BF16)
        bcgbuf = sb("bcgbuf", [128, max(4 * NG * FC, 4 * DT * FS)], BF16)
        bcg = [bcgbuf[:, 0:2 * NG * FC], bcgbuf[:, 2 * NG * FC:4 * NG * FC]]
        x32a = bcgbuf[:, 0:2 * DT * FS].bitcast(F32)
        x32b = bcgbuf[:, 2 * DT * FS:4 * DT * FS].bitcast(F32)
        dabuf = sb("dabuf", [128, max(2 * FC, DT * FS // 2)], F32)
        dAb = [dabuf[:, 0:FC], dabuf[:, FC:2 * FC]]
        sfin = dabuf[:, 0:DT * FS // 2].bitcast(BF16)
        scanb = sb("scanb", [128, max(4 * FC, DT * FS)], BF16)
        ub = [scanb[:, 0:FC], scanb[:, FC:2 * FC]]
        hbs = [scanb[:, 2 * FC:3 * FC], scanb[:, 3 * FC:4 * FC]]
        ycbs = ub
        gbuf = scanb[:, 0:DT * FS]
        cvtf = bcgbuf[:, 0:2 * DT * FS].bitcast(F32)
        cvto = scanb[:, 0:min(4096, 4 * FC)]
        tre = sb("tre", [128, max(2 * FC, 2 * FS)], BF16)
        sgbuf = tre[:, 0:FS]
        upbuf = tre[:, FS:2 * FS]
        ybuf = sb("ybuf", [128, EHT * FC], BF16)
        hst = sb("hst", [128, EHT * NST], F32)
        ptile = sb("ptile", [128, 4 * 128], BF16)
        spt = sb("spt", [128, 2 * FS], F32)
        sp2 = spt
        rows = sb("rows", [1, 2 * FS], F32)
        rowb = sb("rowb", [1, FS], BF16)
        onesc = sb("onesc", [128, 1], BF16)
        onesr = sb("onesr", [1, 128], BF16)
        cst = sb("cst", [128, SM_COLS], F32)
        identb = sb("identb", [128, 128], BF16)
        tailb = [sb(f"tail{i}", [128, ET], BF16) for i in range(2)]

        pA = psb("pA", [128, FS])
        pB = psb("pB", [128, FS])
        pC = psb("pC", [128, FS])
        pDm = psb("pD", [128, FS])
        pS1 = psb("pS1", [1, FS])
        pBC = psb("pBC", [128, FS])
        pM = psb("pM", [128, FS])
        pSc = psb("pSc", [128, FS])

        sIn = [sem(f"sIn{i}") for i in range(4)]
        sG = [sem(f"sG{i}") for i in range(4)]
        sScq = [sem(f"sScq{i}") for i in range(4)]
        sPe = sem("sPe")
        sV = sem("sV")
        sA = sem("sA")
        sCC = sem("sCC")
        sPl = sem("sPl")

        EM = P.emit
        NOW = P.now
        for ki, kv in enumerate(CONSTS):
            nc.const_aps.register(F32, kv, cst[:, OFF_K + ki:OFF_K + ki + 1])

        QSEM = {"sync": sIn, "gpsimd": sG, "scalar": sScq}
        QROT = {"sync": 0, "gpsimd": 0, "scalar": 0}

        def DMA(eng, fn, waits=()):
            q = QSEM[eng][QROT[eng] % 4]
            QROT[eng] += 1
            w = list(waits) + [(q, P.cnt[q.name])]
            EM(eng, fn, waits=w, inc=q, amt=16)
            return (q, P.cnt[q.name])

        def qsnap(eng):
            return [(s, P.cnt[s.name]) for s in QSEM[eng]]

        def DUMP(name, ap, waits):
            if not dump:
                return
            t = nc.dram_tensor("dbg_" + name, list(ap.shape), ap.dtype)
            dumps[name] = t
            DMA("gpsimd", lambda e, ap=ap, t=t: e.dma_start(out=t.ap(), in_=ap),
                waits=waits)
        pfree = {id(pA): (sV, 0), id(pB): (sV, 0), id(pC): (sV, 0), id(pDm): (sV, 0),
                 id(pS1): (sV, 0), id(pBC): (sV, 0), id(pM): (sV, 0), id(pSc): (sV, 0)}

        c_sm = DMA("sync", lambda e: e.dma_start(out=cst[:], in_=smalls))
        EM("vector", lambda e: e.memset(onesc[:], 1.0), inc=sV)
        EM("vector", lambda e: e.memset(onesr[:], 1.0), inc=sV)
        EM("vector", lambda e: e.memset(hst[:], 0.0), inc=sV)
        c_A1 = EM("scalar", lambda e: e.activation(
            out=cst[:, OFF_AL:OFF_AL + NST], in_=cst[:, OFF_AL:OFF_AL + NST],
            func=AF.Exp), waits=[c_sm], inc=sA)
        c_A = EM("vector", lambda e: e.tensor_scalar(
            out=cst[:, OFF_AL:OFF_AL + NST], in0=cst[:, OFF_AL:OFF_AL + NST],
            scalar1=-1.0, scalar2=None, op0=ALU.mult),
            waits=[(sA, c_A1)], inc=sV)

        CH = min(4096, 4 * FC)
        prev_cv = 0
        for (tot, srcap, isint) in ((XF_COLS, wf32, False), (XI_COLS, wi32, True)):
            nchk = (tot + CH - 1) // CH
            base = 0 if not isint else XF_COLS
            for i in range(nchk):
                c0, w = i * CH, min((i + 1) * CH, tot) - i * CH
                st_toks = qsnap("gpsimd")
                if isint:
                    ld = DMA("sync", lambda e, c0=c0, w=w, srcap=srcap: e.dma_start(
                        out=cvtf[:, 0:w].bitcast(I32),
                        in_=srcap[:, c0:c0 + w]),
                        waits=[(sV, prev_cv)])
                    cv = EM("vector", lambda e, w=w: e.tensor_scalar(
                        out=cvto[:, 0:w], in0=cvtf[:, 0:w].bitcast(I32),
                        scalar1=1.0 / 127.0, scalar2=None, op0=ALU.mult),
                        waits=[ld] + st_toks, inc=sV)
                else:
                    ld = DMA("sync", lambda e, c0=c0, w=w, srcap=srcap: e.dma_start(
                        out=cvtf[:, 0:w], in_=srcap[:, c0:c0 + w]),
                        waits=[(sV, prev_cv)])
                    cv = EM("vector", lambda e, w=w: e.tensor_copy(
                        out=cvto[:, 0:w], in_=cvtf[:, 0:w]),
                        waits=[ld] + st_toks, inc=sV)
                prev_cv = cv
                DMA("gpsimd", lambda e, c0=c0, w=w, base=base: e.dma_start(
                    out=wbf[:, base + c0:base + c0 + w], in_=cvto[:, 0:w]),
                    waits=[(sV, cv)])
        c_wready = qsnap("gpsimd")
        c_id = DMA("sync", lambda e: e.dma_start(
            out=identb[:], in_=wbf[:, WOFF["ident"]:WOFF["ident"] + 128]),
            waits=c_wready)

        x_r = x_in.rearrange("(dt p) t -> p dt t", p=128)
        xo_r = xown.rearrange("(dt p) t -> p dt t", p=128)

        def mmwait(pp, extra):
            w = list(extra)
            w.append(pfree[id(pp)])
            return w

        def evac_done(pp, sem_, cnt):
            pfree[id(pp)] = (sem_, cnt)

        def emit_ln(src_r, t0, f, wcol, want_f32=None, nb_dst=None):
            nb = nb_dst if nb_dst is not None else normed
            ld = DMA("sync", lambda e: e.dma_start(
                out=xsub[:].rearrange("p (dt t) -> p dt t", t=f),
                in_=src_r[:, :, t0:t0 + f]),
                waits=[(sV, P.cnt[sV.name]), (sA, P.cnt[sA.name])])
            pe_snap = P.cnt[sPe.name]
            cps = []
            for d in range(DT):
                cps.append(EM("vector", lambda e, d=d: e.tensor_copy(
                    out=nb[:, d * f:(d + 1) * f], in_=xsub[:, d * f:(d + 1) * f]),
                    waits=[ld, (sPe, pe_snap)], inc=sV))
            sqs = []
            for d in range(DT):
                sqs.append(EM("scalar", lambda e, d=d: e.activation(
                    out=xco[:, d * f:(d + 1) * f], in_=xsub[:, d * f:(d + 1) * f],
                    func=AF.Square), waits=[ld, (sPe, pe_snap)], inc=sA))
            m1 = m2 = None
            for d in range(DT):
                m1 = EM("tensor", lambda e, d=d: e.matmul(
                    pS1[:, 0:f], lhsT=onesc[:], rhs=nb[:, d * f:(d + 1) * f],
                    start=(d == 0), stop=(d == DT - 1)),
                    waits=mmwait(pS1, [(sV, cps[d])]) if d == 0 else [(sV, cps[d])],
                    inc=(sPe if d == DT - 1 else None))
            cm0 = EM("vector", lambda e: e.tensor_scalar(
                out=rows[:, 0:f], in0=pS1[0:1, 0:f], scalar1=1.0 / D, scalar2=None,
                op0=ALU.mult), waits=[(sPe, m1)], inc=sV)
            evac_done(pS1, sV, cm0)
            for d in range(DT):
                m2 = EM("tensor", lambda e, d=d: e.matmul(
                    pS1[:, 0:f], lhsT=onesc[:], rhs=xco[:, d * f:(d + 1) * f],
                    start=(d == 0), stop=(d == DT - 1)),
                    waits=mmwait(pS1, [(sA, sqs[d])]) if d == 0 else [(sA, sqs[d])],
                    inc=(sPe if d == DT - 1 else None))
            cvm = EM("vector", lambda e: e.tensor_copy(
                out=rowb[:, 0:f], in_=rows[:, 0:f]), inc=sV)
            mb = EM("tensor", lambda e: e.matmul(
                pM[:, 0:f], lhsT=onesr[:], rhs=rowb[:, 0:f], start=True, stop=True),
                waits=mmwait(pM, [(sV, cvm)]), inc=sPe)
            EM("vector", lambda e: e.tensor_tensor(
                out=rows[:, f:2 * f], in0=rows[:, 0:f], in1=rows[:, 0:f],
                op=ALU.mult), inc=sV)
            cv1 = EM("vector", lambda e: e.scalar_tensor_tensor(
                out=rows[:, f:2 * f], in0=pS1[0:1, 0:f], scalar=1.0 / D,
                in1=rows[:, f:2 * f], op0=ALU.mult, op1=ALU.subtract),
                waits=[(sPe, m2)], inc=sV)
            evac_done(pS1, sV, cv1)
            EM("scalar", lambda e: e.activation(
                out=rows[:, f:2 * f], in_=rows[:, f:2 * f], func=AF.Ln, bias=EPS),
                waits=[(sV, cv1)], inc=sA)
            ca = EM("scalar", lambda e: e.activation(
                out=rows[:, 0:f], in_=rows[:, f:2 * f], func=AF.Exp,
                scale=-0.5), waits=[(sV, cvm)], inc=sA)
            if wcol is not None:
                EM("scalar", lambda e: e.activation(
                    out=rows[:, f:2 * f], in_=rows[:, f:2 * f], func=AF.Exp,
                    scale=-1.0), inc=sA)
                ca2 = EM("vector", lambda e: e.tensor_scalar(
                    out=rows[:, f:2 * f], in0=rows[:, f:2 * f], scalar1=-EPS,
                    scalar2=1.0, op0=ALU.mult, op1=ALU.add),
                    waits=[(sA, NOW(sA))], inc=sV)
                EM("scalar", lambda e: e.activation(
                    out=rows[:, f:2 * f], in_=rows[:, f:2 * f], func=AF.Ln,
                    bias=EPS), waits=[(sV, ca2)], inc=sA)
                EM("scalar", lambda e: e.activation(
                    out=rows[:, f:2 * f], in_=rows[:, f:2 * f], func=AF.Exp,
                    scale=-0.5), inc=sA)
                cv3 = EM("vector", lambda e: e.tensor_tensor(
                    out=rows[:, f:2 * f], in0=rows[:, 0:f],
                    in1=rows[:, f:2 * f], op=ALU.mult),
                    waits=[(sA, NOW(sA))], inc=sV)
            else:
                cv3 = EM("vector", lambda e: e.tensor_copy(
                    out=rows[:, f:2 * f], in_=rows[:, 0:f]),
                    waits=[(sA, ca)], inc=sV)
            cv4 = EM("vector", lambda e: e.tensor_copy(
                out=rowb[:, 0:f], in_=rows[:, f:2 * f]),
                waits=[(sPe, mb)], inc=sV)
            sc = EM("tensor", lambda e: e.matmul(
                pSc[:, 0:f], lhsT=onesr[:], rhs=rowb[:, 0:f], start=True,
                stop=True), waits=mmwait(pSc, [(sV, cv4)]), inc=sPe)
            last = None
            sa_snap = P.cnt[sA.name]
            for d in range(DT):
                EM("vector", lambda e, d=d: e.tensor_tensor(
                    out=spt[:, 0:f], in0=xsub[:, d * f:(d + 1) * f], in1=pM[:, 0:f],
                    op=ALU.subtract), waits=[(sPe, sc), (sA, sa_snap)], inc=sV)
                if want_f32 is not None:
                    EM("vector", lambda e, d=d: e.tensor_tensor(
                        out=want_f32[:, d * f:(d + 1) * f], in0=spt[:, 0:f],
                        in1=pSc[:, 0:f], op=ALU.mult), inc=sV)
                if wcol is not None:
                    last = EM("vector", lambda e, d=d: e.scalar_tensor_tensor(
                        out=nb[:, d * f:(d + 1) * f], in0=spt[:, 0:f],
                        scalar=cst[:, wcol + d:wcol + d + 1], in1=pSc[:, 0:f],
                        op0=ALU.mult, op1=ALU.mult), inc=sV)
                else:
                    last = EM("vector", lambda e, d=d: e.tensor_tensor(
                        out=nb[:, d * f:(d + 1) * f], in0=spt[:, 0:f],
                        in1=pSc[:, 0:f], op=ALU.mult), inc=sV)
            evac_done(pM, sV, last)
            evac_done(pSc, sV, last)
            return last

        wload_slot_free = [(sPe, 0), (sPe, 0)]

        def w_load(region, m, kt, slot):
            off = WOFF[region] + m * kt * 128
            return DMA("sync", lambda e: e.dma_start(
                out=wst[:, slot * WSLOT:slot * WSLOT + kt * 128],
                in_=wbf[:, off:off + kt * 128]),
                waits=[wload_slot_free[slot]] + c_wready)

        def wtile(slot, k):
            return wst[:, slot * WSLOT + k * 128:slot * WSLOT + (k + 1) * 128]

        def mm_chain(region, m, kt, rhs_fn, pp, slot, extra_waits=()):
            lw = w_load(region, m, kt, slot)
            last = None
            for k in range(kt):
                w = [lw] + list(extra_waits)
                if k == 0:
                    w = mmwait(pp, w)
                last = EM("tensor",
                          lambda e, k=k, pp=pp, slot=slot, rhs_fn=rhs_fn: e.matmul(
                              pp[:], lhsT=wtile(slot, k), rhs=rhs_fn(k),
                              start=(k == 0), stop=(k == kt - 1)),
                          waits=w, inc=(sPe if k == kt - 1 else None))
            wload_slot_free[slot] = (sPe, last)
            return last

        c_conv_done = [0] * NSUB
        c_z_read = 0
        c_scan_done_v = 0
        c_scan_done_a = 0
        c_ln_q = {}

        def emit_A_xz(s):
            sc = s % SPC
            xz = xzs[s % 2]
            gate = [(sV, c_conv_done[s - 2])] if s >= 2 else []
            c_n = emit_ln(x_r, s * FS, FS, OFF_WIN)
            c_ln_q[s] = c_n
            for m in range(ET):
                pp = (pA, pB)[m % 2]
                slot = m % 2
                mm = mm_chain("wi", m, DT, lambda k: normed[:, k * FS:(k + 1) * FS],
                              pp, slot, extra_waits=[(sV, c_n)])
                ca = EM("scalar", lambda e, m=m, pp=pp, xz=xz, sc=sc: e.activation(
                    out=xz[:, m * (FS + 3) + 1:m * (FS + 3) + 1 + FS],
                    in_=pp[:], func=AF.Copy),
                    waits=[(sPe, mm)] + gate, inc=sA)
                evac_done(pp, sA, ca)
            return NOW(sA)

        def emit_A_z(s):
            sc = s % SPC
            for mz in range(EHT):
                m = ET + mz
                pp = (pA, pB)[m % 2]
                slot = m % 2
                mm = mm_chain("wi", m, DT, lambda k: normed[:, k * FS:(k + 1) * FS],
                              pp, slot, extra_waits=[(sV, c_ln_q[s]), (sV, c_z_read)])
                ca = EM("scalar", lambda e, mz=mz, pp=pp, sc=sc: e.activation(
                    out=zsil[:, mz * FC + sc * FS:mz * FC + (sc + 1) * FS],
                    in_=pp[:], func=AF.Silu),
                    waits=[(sPe, mm)], inc=sA)
                evac_done(pp, sA, ca)
            return NOW(sA)

        def emit_patch(s, c_next_xz):
            xz = xzs[s % 2]
            tb = tailb[s % 2]
            tbp = tailb[(s - 1) % 2]
            last = None
            for m in range(ET):
                b0 = m * (FS + 3)
                if s == NSUB - 1:
                    last = EM("vector", lambda e, b0=b0: e.memset(
                        xz[:, b0 + 1 + FS:b0 + 3 + FS], 0.0),
                        waits=[(sA, c_next_xz)], inc=sV)
                else:
                    xzn = xzs[(s + 1) % 2]
                    last = EM("vector", lambda e, b0=b0, xzn=xzn: e.tensor_copy(
                        out=xz[:, b0 + 1 + FS:b0 + 3 + FS],
                        in_=xzn[:, b0 + 1:b0 + 3]),
                        waits=[(sA, c_next_xz)], inc=sV)
                last = EM("vector", lambda e, b0=b0, m=m, tb=tb: e.tensor_copy(
                    out=tb[:, m:m + 1], in_=xz[:, b0 + FS:b0 + FS + 1]), inc=sV)
                if s == 0:
                    last = EM("vector", lambda e, b0=b0: e.memset(
                        xz[:, b0:b0 + 1], 0.0), inc=sV)
                else:
                    last = EM("vector", lambda e, b0=b0, m=m, tbp=tbp: e.tensor_copy(
                        out=xz[:, b0:b0 + 1], in_=tbp[:, m:m + 1]), inc=sV)
            return last

        def emit_conv(s, c_patch):
            sc = s % SPC
            xz = xzs[s % 2]
            sil_counts = []
            for et in range(ET):
                b0 = et * (FS + 3)
                sl = et % 2
                EM("vector", lambda e, b0=b0, et=et, sl=sl: e.tensor_scalar(
                    out=spt[:, sl * FS:(sl + 1) * FS], in0=xz[:, b0:b0 + FS],
                    scalar1=cst[:, OFF_CW + et * 4:OFF_CW + et * 4 + 1],
                    scalar2=None, op0=ALU.mult),
                    waits=[(sV, c_patch), (sA, sil_counts[-2] if et >= 2 else 0),
                           (sA, c_scan_done_a)],
                    inc=sV)
                for w in range(1, 4):
                    EM("vector", lambda e, b0=b0, et=et, w=w, sl=sl:
                       e.scalar_tensor_tensor(
                           out=spt[:, sl * FS:(sl + 1) * FS],
                           in0=xz[:, b0 + w:b0 + w + FS],
                           scalar=cst[:, OFF_CW + et * 4 + w:OFF_CW + et * 4 + w + 1],
                           in1=spt[:, sl * FS:(sl + 1) * FS],
                           op0=ALU.mult, op1=ALU.add), inc=sV)
                cvv = NOW(sV)
                if et < EHT:
                    dst = xch[:, et * FC + sc * FS:et * FC + (sc + 1) * FS]
                else:
                    dst = xco[:, (et - EHT) * FS:(et - EHT + 1) * FS]
                sil_counts.append(EM("scalar", lambda e, dst=dst, et=et, sl=sl:
                                     e.activation(out=dst,
                                                  in_=spt[:, sl * FS:(sl + 1) * FS],
                                                  func=AF.Silu,
                                                  bias=cst[:, OFF_CB + et:OFF_CB + et + 1]),
                                     waits=[(sV, cvv), (sPe, P.cnt[sPe.name])],
                                     inc=sA))
            c_conv_done[s] = NOW(sV)
            return sil_counts[-1]

        def xc_rhs(sc):
            def f(k):
                if k < EHT:
                    return xch[:, k * FC + sc * FS:k * FC + (sc + 1) * FS]
                return xco[:, (k - EHT) * FS:(k - EHT + 1) * FS]
            return f

        def emit_dtbc(s, c_conv):
            sc = s % SPC
            for m in range(EHT):
                pp = (pC, pDm)[m % 2]
                slot = m % 2
                mm = mm_chain("wdt", m, ET, xc_rhs(sc), pp, slot,
                              extra_waits=[(sA, c_conv)])
                EM("scalar", lambda e, pp=pp, m=m: e.activation(
                    out=sp2[:, 0:FS], in_=pp[:], func=AF.Abs,
                    bias=cst[:, OFF_DTB + m:OFF_DTB + m + 1]),
                    waits=[(sPe, mm), (sV, NOW(sV))], inc=sA)
                EM("scalar", lambda e: e.activation(
                    out=sp2[:, 0:FS], in_=sp2[:, 0:FS], func=AF.Exp, scale=-1.0),
                    inc=sA)
                EM("scalar", lambda e: e.activation(
                    out=sp2[:, 0:FS], in_=sp2[:, 0:FS], func=AF.Ln, bias=1.0),
                    inc=sA)
                ca = EM("scalar", lambda e, pp=pp, m=m: e.activation(
                    out=sp2[:, FS:2 * FS], in_=pp[:], func=AF.Relu,
                    bias=cst[:, OFF_DTB + m:OFF_DTB + m + 1]), inc=sA)
                evac_done(pp, sA, ca)
                EM("vector", lambda e: e.tensor_tensor(
                    out=sp2[:, 0:FS], in0=sp2[:, 0:FS], in1=sp2[:, FS:2 * FS],
                    op=ALU.add), waits=[(sA, ca)], inc=sV)
                EM("vector", lambda e, m=m, sc=sc: e.tensor_scalar(
                    out=dtc[:, m * FC + sc * FS:m * FC + (sc + 1) * FS],
                    in0=sp2[:, 0:FS], scalar1=0.1, scalar2=0.001,
                    op0=ALU.min, op1=ALU.max),
                    waits=[(sA, c_scan_done_a)], inc=sV)
            mm = mm_chain("wbc", 0, ET, xc_rhs(sc), pC, 0,
                          extra_waits=[(sA, c_conv)])
            ca = EM("scalar", lambda e, sc=sc: e.activation(
                out=bcb[:, sc * FS:(sc + 1) * FS], in_=pC[:], func=AF.Copy),
                waits=[(sPe, mm), (sPe, c_bcb_free[0])], inc=sA)
            evac_done(pC, sA, ca)
            return NOW(sV), ca

        def emit_scan(c, c_dt_v, c_bc_a):
            nonlocal c_scan_done_v, c_scan_done_a
            for et in range(EHT):
                EM("vector", lambda e, et=et: e.tensor_tensor(
                    out=xch[:, et * FC:(et + 1) * FC],
                    in0=dtc[:, et * FC:(et + 1) * FC],
                    in1=xch[:, et * FC:(et + 1) * FC], op=ALU.mult),
                    waits=[(sA, c_bc_a), (sV, c_dt_v)] + qsnap("gpsimd"),
                    inc=sV)
            c_w = NOW(sV)
            c_grp_prev = 0
            scan_tok = {}
            u_tok = {}
            yc_tok = {}
            add_tok = {}
            pool_i = {}
            idx = 0
            for g in range(NGR):
                n0 = g * NG
                bc_ready = []
                for ni in range(NG):
                    n = n0 + ni
                    for (ci, row) in ((0, n), (1, NST + n)):
                        dst_base = (ci * 4 + ni) * FC
                        for half in range(FC // FS):
                            mmb = EM("tensor", lambda e, row=row, half=half:
                                     e.matmul(
                                         pBC[:],
                                         lhsT=identb[:, row:row + 1].rearrange(
                                             "p (o f) -> p o f", o=1
                                         ).broadcast_to([128, 1, 128]),
                                         rhs=bcb[:, half * FS:(half + 1) * FS],
                                         start=True, stop=True),
                                     waits=mmwait(pBC, [(sA, c_bc_a),
                                                        (sV, c_grp_prev),
                                                        c_id]),
                                     inc=sPe)
                            cev = EM("scalar", lambda e, dst_base=dst_base,
                                     half=half: e.activation(
                                         out=bcgbuf[:, dst_base + half * FS:
                                                    dst_base + (half + 1) * FS],
                                         in_=pBC[:], func=AF.Copy),
                                     waits=[(sPe, mmb)], inc=sA)
                            evac_done(pBC, sA, cev)
                    bc_ready.append(NOW(sA))
                for ep in range(EHT // 2):
                    for ni in range(NG):
                        n = n0 + ni
                        iA, iB = idx, idx + 1
                        idx += 2
                        ets = (2 * ep, 2 * ep + 1)
                        for st in (0, 1):
                            EM("scalar", lambda e, st=st, et=ets[st], n=n:
                               e.activation(
                                   out=dAb[st], in_=dtc[:, et * FC:(et + 1) * FC],
                                   func=AF.Exp,
                                   scale=cst[:, OFF_AL + n:OFF_AL + n + 1]),
                               waits=[(sV, c_dt_v), (sV, c_A),
                                      (sV, scan_tok.get((iA, iB)[st] - 2, 0))],
                               inc=sA)
                        c_exp = NOW(sA)
                        for st in (0, 1):
                            i_ = (iA, iB)[st]
                            u_tok[i_] = EM("vector", lambda e, st=st,
                                           et=ets[st], ni=ni: e.tensor_tensor(
                                               out=ub[st],
                                               in0=xch[:, et * FC:(et + 1) * FC],
                                               in1=bcgbuf[:, ni * FC:(ni + 1) * FC],
                                               op=ALU.mult),
                                           waits=[(sA, bc_ready[ni]),
                                                  (sV, c_w),
                                                  (sV, add_tok.get(i_ - 2, 0)),
                                                  (sV, scan_tok.get(i_ - 2, 0))],
                                           inc=sV, selfw=False)
                        for st in (0, 1):
                            i_ = (iA, iB)[st]
                            scan_tok[i_] = EM("vector", lambda e, st=st,
                                              et=ets[st], n=n: e.tensor_tensor_scan(
                                                  out=hbs[st], data0=dAb[st],
                                                  data1=ub[st],
                                                  initial=hst[:, et * NST + n:
                                                              et * NST + n + 1],
                                                  op0=ALU.mult, op1=ALU.add),
                                              waits=[(sA, c_exp),
                                                     (sV, u_tok[i_]),
                                                     (sV, yc_tok.get(i_ - 2, 0)),
                                                     (sPl, pool_i.get(i_ - 2, 0)),
                                                     (sPl, pool_tok.get(
                                                         (ets[st], n, (c + 1) % 2), 0))],
                                              inc=sV, selfw=False)
                        for st in (0, 1):
                            i_ = (iA, iB)[st]
                            pool_i[i_] = pool_tok[(ets[st], n, c % 2)] = EM(
                                "gpsimd", lambda e, st=st, et=ets[st], n=n:
                                e.tensor_copy(
                                    out=hst[:, et * NST + n:et * NST + n + 1],
                                    in_=hbs[st][:, FC - 1:FC]),
                                waits=[(sV, scan_tok[i_])], inc=sPl)
                        for st in (0, 1):
                            i_ = (iA, iB)[st]
                            dst = tre[:, st * FC:(st + 1) * FC] if ni == 0 else ycbs[st]
                            yc_tok[i_] = EM("vector", lambda e, st=st, ni=ni,
                                            dst=dst: e.tensor_tensor(
                                                out=dst, in0=hbs[st],
                                                in1=bcgbuf[:, (4 + ni) * FC:
                                                           (4 + ni + 1) * FC],
                                                op=ALU.mult),
                                            waits=[(sV, scan_tok[i_]),
                                                   (sA, bc_ready[ni]),
                                                   (sV, add_tok.get(i_ - 2, 0))],
                                            inc=sV, selfw=False)
                        if ni > 0:
                            for st in (0, 1):
                                i_ = (iA, iB)[st]
                                add_tok[i_] = EM("vector", lambda e, st=st:
                                                 e.tensor_tensor(
                                                     out=tre[:, st * FC:(st + 1) * FC],
                                                     in0=ycbs[st],
                                                     in1=tre[:, st * FC:(st + 1) * FC],
                                                     op=ALU.add),
                                                 waits=[(sV, yc_tok[i_])],
                                                 inc=sV, selfw=False)
                        if ni == NG - 1:
                            for st in (0, 1):
                                i_ = (iA, iB)[st]
                                et = ets[st]
                                ydst = ybuf[:, et * FC:(et + 1) * FC]
                                wt = [(sV, add_tok[i_])]
                                if g == 0:
                                    wt.append((sPe, c_ybuf_free[0]))
                                    add_tok[i_] = EM("vector", lambda e, yd=ydst, st=st:
                                       e.tensor_copy(out=yd, in_=tre[:, st * FC:(st + 1) * FC]),
                                       waits=wt, inc=sV, selfw=False)
                                else:
                                    add_tok[i_] = EM("vector", lambda e, yd=ydst, st=st:
                                       e.tensor_tensor(
                                           out=yd, in0=yd,
                                           in1=tre[:, st * FC:(st + 1) * FC],
                                           op=ALU.add),
                                       waits=wt, inc=sV, selfw=False)
                c_grp_prev = NOW(sV)
            c_bcb_free[0] = P.cnt[sPe.name]
            c_scan_done_v = NOW(sV)
            c_scan_done_a = NOW(sA)
            return c_scan_done_v

        def emit_yzop(c, c_scan_v):
            nonlocal c_z_read
            for et in range(EHT):
                EM("vector", lambda e, et=et: e.tensor_tensor(
                    out=xch[:, et * FC:(et + 1) * FC],
                    in0=ybuf[:, et * FC:(et + 1) * FC],
                    in1=zsil[:, et * FC:(et + 1) * FC], op=ALU.mult),
                    waits=[(sV, c_scan_v), (sA, NOW(sA))] + qsnap("gpsimd"),
                    inc=sV)
            c_yz = NOW(sV)
            c_z_read = c_yz
            wo_r = wbf[:, WOFF["wout"]:WOFF["wout"] + EHT * D].rearrange(
                "p (k d) -> p k d", d=D)
            npc = D // 128
            for pc_ in range(npc):
                slot = pc_ % 2
                lw = DMA("sync", lambda e, pc_=pc_, slot=slot: e.dma_start(
                    out=wst[:, slot * WSLOT:slot * WSLOT + EHT * 128].rearrange(
                        "p (k f) -> p k f", f=128),
                    in_=wo_r[:, :, pc_ * 128:(pc_ + 1) * 128]),
                    waits=[wload_slot_free[slot]] + c_wready)
                lastmm = None
                for tt in range(FC // 128):
                    pp = (pA, pB)[tt % 2]
                    last = None
                    for k in range(EHT):
                        w = [(sV, c_yz), lw]
                        if k == 0:
                            w = mmwait(pp, w)
                        last = EM("tensor", lambda e, pp=pp, k=k, tt=tt, slot=slot:
                                  e.matmul(
                                      pp[:, 0:128],
                                      lhsT=xch[:, k * FC + tt * 128:k * FC + (tt + 1) * 128],
                                      rhs=wst[:, slot * WSLOT + k * 128:slot * WSLOT + (k + 1) * 128],
                                      start=(k == 0), stop=(k == EHT - 1)),
                                  waits=w, inc=(sPe if k == EHT - 1 else None))
                    lastmm = last
                    psl = (pc_ * (FC // 128) + tt) % 4
                    cv = EM("vector", lambda e, pp=pp, psl=psl: e.tensor_copy(
                        out=ptile[:, psl * 128:(psl + 1) * 128], in_=pp[:, 0:128]),
                        waits=[(sPe, last), c_pt_free[psl]], inc=sV)
                    evac_done(pp, sV, cv)
                    cst_ = DMA("gpsimd", lambda e, tt=tt, c=c, pc_=pc_, psl=psl:
                               e.dma_start(
                        out=partial[c * FC + tt * 128:c * FC + (tt + 1) * 128,
                                    pc_ * 128:(pc_ + 1) * 128],
                        in_=ptile[:, psl * 128:(psl + 1) * 128]),
                        waits=[(sV, cv)])
                    c_pt_free[psl] = cst_
                wload_slot_free[slot] = (sPe, lastmm)
            c_ybuf_free[0] = P.cnt[sPe.name]

        c_pt_free = [(sV, 0)] * 4
        pool_tok = {}
        c_bcb_free = [0]
        c_ybuf_free = [0]

        pend = None
        for s in range(NSUB if stop_after != "prologue" else 0):
            c_axz = emit_A_xz(s)
            if s >= 1:
                sp = s - 1
                c_p = emit_patch(sp, c_axz)
                c_conv = emit_conv(sp, c_p)
                c_dt_v, c_bc_a = emit_dtbc(sp, c_conv)
                if (sp + 1) % SPC == 0:
                    cch = sp // SPC
                    if cch == 0:
                        DUMP("xch0", xch[:], [(sV, c_dt_v), (sA, c_bc_a)])
                        DUMP("dtc0", dtc[:], [(sV, c_dt_v)])
                        DUMP("bcb0", bcb[:], [(sA, c_bc_a)])
                    c_sv = emit_scan(cch, c_dt_v, c_bc_a)
                    if cch == 0:
                        DUMP("ybuf0", ybuf[:], [(sV, c_sv)])
                        DUMP("hst0", hst[:], [(sV, c_sv)])
                        DUMP("zsil0", zsil[:], [(sA, P.cnt[sA.name])])
                    emit_yzop(cch, c_sv)
            emit_A_z(s)
        if stop_after != "prologue":
            sp = NSUB - 1
            c_p = emit_patch(sp, NOW(sA))
            c_conv = emit_conv(sp, c_p)
            c_dt_v, c_bc_a = emit_dtbc(sp, c_conv)
            c_sv = emit_scan(NCH - 1, c_dt_v, c_bc_a)
            emit_yzop(NCH - 1, c_sv)
        c_pi_st = qsnap("gpsimd")

        c_g_free = [(sV, 0), (sV, 0)]
        for su in range(THS if stop_after in (None, "g", "cc", "pii") else 0):
            c_n = emit_ln(xo_r, su * FS, FS, None)
            c_h1_prev = 0
            for mh in range(FHT):
                mmg = mm_chain("fgu", 2 * mh, DT,
                               lambda k: normed[:, k * FS:(k + 1) * FS], pA, 0,
                               extra_waits=[(sV, c_n)])
                mmu = mm_chain("fgu", 2 * mh + 1, DT,
                               lambda k: normed[:, k * FS:(k + 1) * FS], pB, 1,
                               extra_waits=[(sV, c_n)])
                ca1 = EM("scalar", lambda e: e.activation(
                    out=sgbuf, in_=pA[:], func=AF.Silu),
                    waits=[(sPe, mmg), (sV, c_h1_prev)], inc=sA)
                ca2 = EM("scalar", lambda e: e.activation(
                    out=upbuf, in_=pB[:], func=AF.Copy),
                    waits=[(sPe, mmu)], inc=sA)
                evac_done(pA, sA, ca1)
                evac_done(pB, sA, ca2)
                c_h1_prev = EM("vector", lambda e, mh=mh: e.tensor_tensor(
                    out=h1b[:, mh * FS:(mh + 1) * FS], in0=sgbuf,
                    in1=upbuf, op=ALU.mult),
                    waits=[(sA, ca2), (sPe, P.cnt[sPe.name])], inc=sV)
            c_h1 = NOW(sV)
            for md in range(DT):
                pp = (pC, pDm)[md % 2]
                sl = md % 2
                mm = mm_chain("fdn", md, FHT,
                              lambda k: h1b[:, k * FS:(k + 1) * FS], pp, md % 2,
                              extra_waits=[(sV, c_h1)])
                ca = EM("scalar", lambda e, pp=pp, sl=sl: e.activation(
                    out=gbuf[:, sl * FS:(sl + 1) * FS], in_=pp[:], func=AF.Silu),
                    waits=[(sPe, mm), c_g_free[sl]], inc=sA)
                evac_done(pp, sA, ca)
                cg = DMA("gpsimd", lambda e, md=md, su=su, sl=sl: e.dma_start(
                    out=gdram.rearrange("(dt p) t -> p dt t", p=128)[
                        :, md, su * FS:(su + 1) * FS],
                    in_=gbuf[:, sl * FS:(sl + 1) * FS]),
                    waits=[(sA, ca)])
                c_g_free[sl] = cg

        c_cc = 0
        if stop_after in (None, "cc", "pii"):
            c_cc = EM("gpsimd", lambda e: e.collective_compute(
                "ReduceScatter", ALU.add,
                replica_groups=[[0, 1], [2, 3], [4, 5], [6, 7]],
                ins=[partial.opt()], outs=[s1own.opt()]),
                waits=c_pi_st, inc=sCC, amt=1)

        s1_r = s1own
        if stop_after not in (None, "pii"):
            for su in range(THS):
                DMA("gpsimd", lambda e, su=su: e.dma_start(
                    out=out.rearrange("(dt p) t -> p dt t", p=128)[
                        :, :, su * FS:(su + 1) * FS],
                    in_=xsub[:].rearrange("p (dt t) -> p dt t", t=FS)),
                    waits=[(sV, P.cnt[sV.name])])
        for su in range(THS if stop_after in (None, "pii") else 0):
            c_n = emit_ln(xo_r, su * FS, FS, None, want_f32=x32a)
            lds = []
            sv_snap2 = P.cnt[sV.name]
            for d in range(DT):
                lds.append(DMA("scalar", lambda e, d=d, su=su: e.dma_start(
                    out=sfin[:, d * FS:(d + 1) * FS],
                    in_=s1_r[su * FS:(su + 1) * FS, d * 128:(d + 1) * 128],
                    transpose=True),
                    waits=[(sCC, c_cc), (sV, sv_snap2)]))
            cmid = None
            for d in range(DT):
                cmid = EM("vector", lambda e, d=d: e.tensor_tensor(
                    out=x32b[:, d * FS:(d + 1) * FS], in0=x32a[:, d * FS:(d + 1) * FS],
                    in1=sfin[:, d * FS:(d + 1) * FS], op=ALU.add),
                    waits=[lds[d], (sV, c_n)], inc=sV)
            sqs = []
            for d in range(DT):
                sqs.append(EM("scalar", lambda e, d=d: e.activation(
                    out=xco[:, d * FS:(d + 1) * FS], in_=x32b[:, d * FS:(d + 1) * FS],
                    func=AF.Square), waits=[(sV, cmid)], inc=sA))
            m2 = None
            for d in range(DT):
                m2 = EM("tensor", lambda e, d=d: e.matmul(
                    pS1[:, 0:FS], lhsT=onesc[:], rhs=xco[:, d * FS:(d + 1) * FS],
                    start=(d == 0), stop=(d == DT - 1)),
                    waits=mmwait(pS1, [(sA, sqs[d])]) if d == 0 else [(sA, sqs[d])],
                    inc=(sPe if d == DT - 1 else None))
            cv1 = EM("vector", lambda e: e.tensor_scalar(
                out=rows[:, 0:FS], in0=pS1[0:1, 0:FS], scalar1=1.0 / D, scalar2=None,
                op0=ALU.mult), waits=[(sPe, m2)], inc=sV)
            evac_done(pS1, sV, cv1)
            EM("scalar", lambda e: e.activation(
                out=rows[:, FS:2 * FS], in_=rows[:, 0:FS], func=AF.Ln, bias=EPS),
                waits=[(sV, cv1)], inc=sA)
            ca = EM("scalar", lambda e: e.activation(
                out=rows[:, FS:2 * FS], in_=rows[:, FS:2 * FS], func=AF.Exp,
                scale=-0.5), inc=sA)
            cv2 = EM("vector", lambda e: e.tensor_copy(
                out=rowb[:, 0:FS], in_=rows[:, FS:2 * FS]),
                waits=[(sA, ca)], inc=sV)
            mb = EM("tensor", lambda e: e.matmul(
                pSc[:, 0:FS], lhsT=onesr[:], rhs=rowb[:, 0:FS], start=True,
                stop=True), waits=mmwait(pSc, [(sV, cv2)]), inc=sPe)
            cn2 = None
            for d in range(DT):
                cn2 = EM("vector", lambda e, d=d: e.scalar_tensor_tensor(
                    out=normed[:, d * FS:(d + 1) * FS],
                    in0=x32b[:, d * FS:(d + 1) * FS],
                    scalar=cst[:, OFF_WFN + d:OFF_WFN + d + 1], in1=pSc[:, 0:FS],
                    op0=ALU.mult, op1=ALU.mult), waits=[(sPe, mb)], inc=sV)
            evac_done(pSc, sV, cn2)
            c_h1_prev = 0
            for mh in range(ET):
                mmg = mm_chain("wgu", 2 * mh, DT,
                               lambda k: normed[:, k * FS:(k + 1) * FS], pA, 0,
                               extra_waits=[(sV, cn2)])
                mmu = mm_chain("wgu", 2 * mh + 1, DT,
                               lambda k: normed[:, k * FS:(k + 1) * FS], pB, 1,
                               extra_waits=[(sV, cn2)])
                ca1 = EM("scalar", lambda e: e.activation(
                    out=sgbuf, in_=pA[:], func=AF.Silu),
                    waits=[(sPe, mmg), (sV, c_h1_prev)], inc=sA)
                ca2 = EM("scalar", lambda e: e.activation(
                    out=upbuf, in_=pB[:], func=AF.Copy),
                    waits=[(sPe, mmu)], inc=sA)
                evac_done(pA, sA, ca1)
                evac_done(pB, sA, ca2)
                c_h1_prev = EM("vector", lambda e, mh=mh: e.tensor_tensor(
                    out=h1b[:, mh * FS:(mh + 1) * FS], in0=sgbuf,
                    in1=upbuf, op=ALU.mult),
                    waits=[(sA, ca2), (sPe, P.cnt[sPe.name])], inc=sV)
            c_h1 = NOW(sV)
            ldg = DMA("scalar", lambda e, su=su: e.dma_start(
                out=gbuf.rearrange("p (dt t) -> p dt t", t=FS),
                in_=gdram.rearrange("(dt p) t -> p dt t", p=128)[
                    :, :, su * FS:(su + 1) * FS]),
                waits=qsnap("gpsimd") + [(sV, c_h1)])
            for md in range(DT):
                pp = (pC, pDm)[md % 2]
                mm = mm_chain("wdn", md, ET,
                              lambda k: h1b[:, k * FS:(k + 1) * FS], pp, md % 2,
                              extra_waits=[(sV, c_h1)])
                cs = EM("vector", lambda e, pp=pp, md=md: e.tensor_tensor(
                    out=sfin[:, md * FS:(md + 1) * FS], in0=pp[:],
                    in1=x32b[:, md * FS:(md + 1) * FS], op=ALU.add),
                    waits=[(sPe, mm)], inc=sV)
                evac_done(pp, sV, cs)
                EM("vector", lambda e, md=md: e.tensor_tensor(
                    out=spt[:, 0:FS], in0=sfin[:, md * FS:(md + 1) * FS],
                    in1=gbuf[:, md * FS:(md + 1) * FS], op=ALU.mult),
                    waits=[ldg], inc=sV)
                cfo = EM("vector", lambda e, md=md: e.tensor_tensor(
                    out=x32a[:, md * FS:(md + 1) * FS],
                    in0=xsub[:, md * FS:(md + 1) * FS], in1=spt[:, 0:FS],
                    op=ALU.add), inc=sV)
            c_fin = NOW(sV)
            DMA("gpsimd", lambda e, su=su: e.dma_start(
                out=out.rearrange("(dt p) t -> p dt t", p=128)[
                    :, :, su * FS:(su + 1) * FS],
                in_=x32a[:].rearrange("p (dt t) -> p dt t", t=FS)),
                waits=[(sV, c_fin)])
        EM("gpsimd", lambda e: e.nop(),
           waits=qsnap("gpsimd") + qsnap("sync") + qsnap("scalar"))

        nc._dumps = dumps
        qs = {e: [] for e in Prog.ENG}
        for op in P.ops:
            qs[op.eng].append(op)

        def runner(ops):
            def run(eng):
                for op in ops:
                    for (s, c) in op.waits:
                        eng.wait_ge(s, c)
                    ins = op.fn(eng)
                    if op.inc is not None:
                        ins.then_inc(op.inc, op.amt)
            return run

        with nc.Block() as block:
            block.sync(runner(qs["sync"]))
            block.tensor(runner(qs["tensor"]))
            block.vector(runner(qs["vector"]))
            block.scalar(runner(qs["scalar"]))
            block.gpsimd(runner(qs["gpsimd"]))

    return nc


def kernel(**inputs):
    T = int(np.asarray(inputs["L0_attn_x"]).shape[1])
    nc = build_nc(T=T)
    in_maps = []
    for core in range(8):
        b, j = core // 2, core % 2
        in_maps.append(pack_core(inputs, b, j))
    res = run_bass_kernel_spmd(nc, in_maps, core_ids=list(range(8)))
    x = np.asarray(inputs["L0_attn_x"])
    B, T_, D_ = x.shape
    out = np.empty((B, T_, D_), np.float32)
    for core in range(8):
        b, j = core // 2, core % 2
        o = res.results[core]["out"]
        out[b, j * (T_ // 2):(j + 1) * (T_ // 2), :] = o.T
    return out


# version 06
# speedup vs baseline: 1.2116x; 1.0491x over previous
import contextlib
import numpy as np
import ml_dtypes
import concourse.bass as bass
import concourse.mybir as mybir
from concourse.bass_utils import run_bass_kernel_spmd

F32 = mybir.dt.float32
BF16 = mybir.dt.bfloat16
I32 = mybir.dt.int32
AF = mybir.ActivationFunctionType
ALU = mybir.AluOpType

D = 1024
E = 2048
EH = 1024
NST = 64
FH = 3072
DT = D // 128
ET = E // 128
EHT = EH // 128
FHT = FH // 128
EPS = 1e-5

_SIZES_F = [("ident", 128), ("wi", (ET + EHT) * DT * 128), ("wdt", EHT * ET * 128),
            ("wbc", 1 * ET * 128), ("wout", EHT * D),
            ("wgu", 2 * ET * DT * 128), ("wdn", DT * ET * 128)]
_SIZES_I = [("fgu", 2 * FHT * DT * 128), ("fdn", DT * FHT * 128)]
XF_COLS = sum(s for _, s in _SIZES_F)
XI_COLS = sum(s for _, s in _SIZES_I)
WOFF = {}
_o = 0
for _n, _s in _SIZES_F:
    WOFF[_n] = _o
    _o += _s
for _n, _s in _SIZES_I:
    WOFF[_n] = _o
    _o += _s
OFF_CW, OFF_CB, OFF_DTB, OFF_WIN, OFF_WFN, OFF_AL = 0, 64, 80, 88, 96, 104
OFF_K = 168
CONSTS = (1e-5, 1.0, -1.0, -0.5)
SM_COLS = OFF_K + len(CONSTS)


def _tileize(w, kt, mt):
    a = np.asarray(w).reshape(kt, 128, mt, 128).transpose(2, 1, 0, 3)
    return np.ascontiguousarray(
        a.reshape(mt, 128, kt * 128).transpose(1, 0, 2).reshape(128, mt * kt * 128))


def _rhs_tiles(w, kt, f):
    a = np.asarray(w).reshape(kt, 128, f)
    return np.ascontiguousarray(a.transpose(1, 0, 2).reshape(128, kt * f))


def _interleave_m(a, b):
    Dd, M = a.shape
    mt = M // 128
    o = np.empty((Dd, 2 * M), a.dtype)
    for m in range(mt):
        o[:, (2 * m) * 128:(2 * m + 1) * 128] = a[:, m * 128:(m + 1) * 128]
        o[:, (2 * m + 1) * 128:(2 * m + 2) * 128] = b[:, m * 128:(m + 1) * 128]
    return o


def pack_core(inp, b, j):
    f32 = np.float32
    x = np.asarray(inp["L0_attn_x"], f32)[b]
    T = x.shape[0]
    xT = np.ascontiguousarray(x.T)
    xo = np.ascontiguousarray(xT[:, j * (T // 2):(j + 1) * (T // 2)])

    perm = np.concatenate([np.arange(j * EH, (j + 1) * EH),
                           np.arange((1 - j) * EH, (2 - j) * EH)])
    Wi = np.asarray(inp["L0_attn_in_proj_W"], f32)
    cols = np.concatenate([perm, E + j * EH + np.arange(EH)])
    parts_f = [
        np.eye(128, dtype=f32),
        _tileize(Wi[:, cols], DT, ET + EHT),
        _tileize(np.asarray(inp["L0_attn_dt_W"], f32)[perm][:, j * EH:(j + 1) * EH],
                 ET, EHT),
        _tileize(np.concatenate([np.asarray(inp["L0_attn_B_W"], f32)[perm],
                                 np.asarray(inp["L0_attn_C_W"], f32)[perm]], axis=1),
                 ET, 1),
        _rhs_tiles(np.asarray(inp["L0_attn_out_proj_W"], f32)[j * EH:(j + 1) * EH],
                   EHT, D),
        _tileize(_interleave_m(np.asarray(inp["L0_attn_gate_W"], f32),
                               np.asarray(inp["L0_attn_up_W"], f32)), DT, 2 * ET),
        _tileize(np.asarray(inp["L0_attn_down_W"], f32), ET, DT),
    ]
    wf = np.concatenate(parts_f, axis=1)
    parts_i = [
        _tileize(_interleave_m(np.asarray(inp["L0_ffn_w_gate"], np.int32).T,
                               np.asarray(inp["L0_ffn_w_up"], np.int32).T),
                 DT, 2 * FHT),
        _tileize(np.asarray(inp["L0_ffn_w_down"], np.int32).T, FHT, DT),
    ]
    wi = np.concatenate(parts_i, axis=1)

    cw = np.asarray(inp["L0_attn_conv_W"], f32)[:, 0, :].T[perm]
    cwp = cw.reshape(ET, 128, 4).transpose(1, 0, 2).reshape(128, ET * 4)
    cb = np.asarray(inp["L0_attn_conv_b"], f32)[perm].reshape(ET, 128).T
    dtb = np.asarray(inp["L0_attn_dt_b"], f32)[j * EH:(j + 1) * EH].reshape(EHT, 128).T
    win = np.asarray(inp["L0_attn_input_norm_w"], f32).reshape(DT, 128).T
    wfn = np.asarray(inp["L0_attn_ffn_norm_w"], f32).reshape(DT, 128).T
    alog = np.broadcast_to(np.asarray(inp["L0_attn_A_log"], f32)[None, :],
                           (128, NST)).copy()
    kc = np.broadcast_to(np.asarray(CONSTS, f32)[None, :], (128, len(CONSTS))).copy()
    smalls = np.concatenate([cwp, cb, dtb, win, wfn, alog, kc], axis=1).astype(f32)
    assert smalls.shape[1] == SM_COLS
    return {"x": xT, "x_own": xo, "wf32": np.ascontiguousarray(wf),
            "wi32": np.ascontiguousarray(wi), "smalls": smalls}


class Op:
    __slots__ = ("eng", "waits", "inc", "amt", "fn")

    def __init__(self, eng, waits, inc, amt, fn):
        self.eng, self.waits, self.inc, self.amt, self.fn = eng, waits, inc, amt, fn


class Prog:
    ENG = ("sync", "tensor", "vector", "scalar", "gpsimd")

    def __init__(self):
        self.ops = []
        self.cnt = {}
        self.seen = {e: {} for e in self.ENG}

    def reg(self, s):
        self.cnt[s.name] = 0
        return s

    def emit(self, eng, fn, waits=(), inc=None, amt=1, selfw=True):
        waits = list(waits)
        if eng in ("vector", "scalar") and selfw and inc is not None:
            waits.append((inc, self.cnt[inc.name]))
        kept = []
        for (s, c) in waits:
            if c > 0 and self.seen[eng].get(s.name, 0) < c:
                kept.append((s, c))
                self.seen[eng][s.name] = c
        self.ops.append(Op(eng, kept, inc, amt, fn))
        if inc is not None:
            self.cnt[inc.name] += amt
            return self.cnt[inc.name]
        return None

    def now(self, s):
        return self.cnt[s.name]


def build_nc(T=8192, FC=1024, FS=512, NG=4, debug=False, dump=False, stop_after=None):
    NCH = T // FC
    NSUB = T // FS
    SPC = FC // FS
    TH = T // 2
    THS = TH // FS
    NGR = NST // NG

    nc = bass.Bass("TRN2", target_bir_lowering=False, debug=debug)

    x_in = nc.dram_tensor("x", [D, T], F32, kind="ExternalInput").ap()
    xown = nc.dram_tensor("x_own", [D, TH], F32, kind="ExternalInput").ap()
    wf32 = nc.dram_tensor("wf32", [128, XF_COLS], F32, kind="ExternalInput").ap()
    wi32 = nc.dram_tensor("wi32", [128, XI_COLS], I32, kind="ExternalInput").ap()
    smalls = nc.dram_tensor("smalls", [128, SM_COLS], F32, kind="ExternalInput").ap()
    out = nc.dram_tensor("out", [D, TH], F32, kind="ExternalOutput").ap()

    wbf_t = nc.dram_tensor("wbf", [128, XF_COLS + XI_COLS], BF16)
    partial_t = nc.dram_tensor("partial", [T, D], BF16)
    s1own_t = nc.dram_tensor("s1own", [TH, D], BF16)
    gdram_t = nc.dram_tensor("gdram", [D, TH], BF16)
    wbf = wbf_t.ap()
    partial = partial_t.ap()
    s1own = s1own_t.ap()
    gdram = gdram_t.ap()

    P = Prog()
    dumps = {}

    with contextlib.ExitStack() as st:
        def sb(name, shape, dt):
            return st.enter_context(nc.sbuf_tensor(name, shape, dt))

        def psb(name, shape):
            return st.enter_context(nc.psum_tensor(name, shape, F32))

        def sem(name):
            return P.reg(st.enter_context(nc.semaphore(name)))

        xsub = sb("xsub", [128, DT * FS], F32)
        normed = sb("normed", [128, DT * FS], BF16)
        WSLOT = 3072
        wst = sb("wst", [128, 2 * WSLOT], BF16)
        xzs = [sb(f"xz{i}", [128, ET * (FS + 3)], BF16) for i in range(2)]
        chunkbuf = sb("chunkbuf", [128, max(3 * EHT * FC, FHT * FS)], BF16)
        zsil = chunkbuf[:, 0:EHT * FC]
        dtc = chunkbuf[:, EHT * FC:2 * EHT * FC]
        xch = chunkbuf[:, 2 * EHT * FC:3 * EHT * FC]
        h1b = chunkbuf[:, 0:FHT * FS]
        xco = sb("xco", [128, EHT * FS], BF16)
        bcb = sb("bcb", [128, FC], BF16)
        bcgbuf = sb("bcgbuf", [128, max(4 * NG * FC, 4 * DT * FS)], BF16)
        bcg = [bcgbuf[:, 0:2 * NG * FC], bcgbuf[:, 2 * NG * FC:4 * NG * FC]]
        x32a = bcgbuf[:, 0:2 * DT * FS].bitcast(F32)
        x32b = bcgbuf[:, 2 * DT * FS:4 * DT * FS].bitcast(F32)
        dabuf = sb("dabuf", [128, max(2 * FC, DT * FS // 2)], F32)
        dAb = [dabuf[:, 0:FC], dabuf[:, FC:2 * FC]]
        sfin = dabuf[:, 0:DT * FS // 2].bitcast(BF16)
        scanb = sb("scanb", [128, max(4 * FC, DT * FS)], BF16)
        ub = [scanb[:, 0:FC], scanb[:, FC:2 * FC]]
        hbs = [scanb[:, 2 * FC:3 * FC], scanb[:, 3 * FC:4 * FC]]
        ycbs = ub
        gbuf = scanb[:, 0:DT * FS]
        cvtf = bcgbuf[:, 0:2 * DT * FS].bitcast(F32)
        cvto = scanb[:, 0:min(4096, 4 * FC)]
        tre = sb("tre", [128, max(2 * FC, 2 * FS)], BF16)
        sgbuf = tre[:, 0:FS]
        upbuf = tre[:, FS:2 * FS]
        ybuf = sb("ybuf", [128, EHT * FC], BF16)
        hst = sb("hst", [128, EHT * NST], F32)
        ptile = sb("ptile", [128, 4 * 128], BF16)
        spt = sb("spt", [128, 2 * FS], F32)
        sp2 = spt
        rows = sb("rows", [1, 2 * FS], F32)
        rowb = sb("rowb", [1, FS], BF16)
        onesc = sb("onesc", [128, 1], BF16)
        onesr = sb("onesr", [1, 128], BF16)
        cst = sb("cst", [128, SM_COLS], F32)
        identb = sb("identb", [128, 128], BF16)
        tailb = [sb(f"tail{i}", [128, ET], BF16) for i in range(2)]

        pA = psb("pA", [128, FS])
        pB = psb("pB", [128, FS])
        pC = psb("pC", [128, FS])
        pDm = psb("pD", [128, FS])
        pS1 = psb("pS1", [1, FS])
        pBC = psb("pBC", [128, FS])
        pM = psb("pM", [128, FS])
        pSc = psb("pSc", [128, FS])

        sIn = [sem(f"sIn{i}") for i in range(4)]
        sG = [sem(f"sG{i}") for i in range(4)]
        sScq = [sem(f"sScq{i}") for i in range(4)]
        sPe = sem("sPe")
        sV = sem("sV")
        sA = sem("sA")
        sCC = sem("sCC")
        sPl = sem("sPl")

        EM = P.emit
        NOW = P.now
        for ki, kv in enumerate(CONSTS):
            nc.const_aps.register(F32, kv, cst[:, OFF_K + ki:OFF_K + ki + 1])

        QSEM = {"sync": sIn, "gpsimd": sG, "scalar": sScq}
        QROT = {"sync": 0, "gpsimd": 0, "scalar": 0}

        def DMA(eng, fn, waits=()):
            q = QSEM[eng][QROT[eng] % 4]
            QROT[eng] += 1
            w = list(waits) + [(q, P.cnt[q.name])]
            EM(eng, fn, waits=w, inc=q, amt=16)
            return (q, P.cnt[q.name])

        def qsnap(eng):
            return [(s, P.cnt[s.name]) for s in QSEM[eng]]

        def DUMP(name, ap, waits):
            if not dump:
                return
            t = nc.dram_tensor("dbg_" + name, list(ap.shape), ap.dtype)
            dumps[name] = t
            DMA("gpsimd", lambda e, ap=ap, t=t: e.dma_start(out=t.ap(), in_=ap),
                waits=waits)
        pfree = {id(pA): (sV, 0), id(pB): (sV, 0), id(pC): (sV, 0), id(pDm): (sV, 0),
                 id(pS1): (sV, 0), id(pBC): (sV, 0), id(pM): (sV, 0), id(pSc): (sV, 0)}

        c_sm = DMA("sync", lambda e: e.dma_start(out=cst[:], in_=smalls))
        EM("vector", lambda e: e.memset(onesc[:], 1.0), inc=sV)
        EM("vector", lambda e: e.memset(onesr[:], 1.0), inc=sV)
        EM("vector", lambda e: e.memset(hst[:], 0.0), inc=sV)
        c_A1 = EM("scalar", lambda e: e.activation(
            out=cst[:, OFF_AL:OFF_AL + NST], in_=cst[:, OFF_AL:OFF_AL + NST],
            func=AF.Exp), waits=[c_sm], inc=sA)
        c_A = EM("vector", lambda e: e.tensor_scalar(
            out=cst[:, OFF_AL:OFF_AL + NST], in0=cst[:, OFF_AL:OFF_AL + NST],
            scalar1=-1.0, scalar2=None, op0=ALU.mult),
            waits=[(sA, c_A1)], inc=sV)

        CH = min(4096, 4 * FC)
        prev_cv = 0
        for (tot, srcap, isint) in ((XF_COLS, wf32, False), (XI_COLS, wi32, True)):
            nchk = (tot + CH - 1) // CH
            base = 0 if not isint else XF_COLS
            for i in range(nchk):
                c0, w = i * CH, min((i + 1) * CH, tot) - i * CH
                st_toks = qsnap("gpsimd")
                if isint:
                    ld = DMA("sync", lambda e, c0=c0, w=w, srcap=srcap: e.dma_start(
                        out=cvtf[:, 0:w].bitcast(I32),
                        in_=srcap[:, c0:c0 + w]),
                        waits=[(sV, prev_cv)])
                    cv = EM("vector", lambda e, w=w: e.tensor_scalar(
                        out=cvto[:, 0:w], in0=cvtf[:, 0:w].bitcast(I32),
                        scalar1=1.0 / 127.0, scalar2=None, op0=ALU.mult),
                        waits=[ld] + st_toks, inc=sV)
                else:
                    ld = DMA("sync", lambda e, c0=c0, w=w, srcap=srcap: e.dma_start(
                        out=cvtf[:, 0:w], in_=srcap[:, c0:c0 + w]),
                        waits=[(sV, prev_cv)])
                    cv = EM("vector", lambda e, w=w: e.tensor_copy(
                        out=cvto[:, 0:w], in_=cvtf[:, 0:w]),
                        waits=[ld] + st_toks, inc=sV)
                prev_cv = cv
                DMA("gpsimd", lambda e, c0=c0, w=w, base=base: e.dma_start(
                    out=wbf[:, base + c0:base + c0 + w], in_=cvto[:, 0:w]),
                    waits=[(sV, cv)])
        c_wready = qsnap("gpsimd")
        c_id = DMA("sync", lambda e: e.dma_start(
            out=identb[:], in_=wbf[:, WOFF["ident"]:WOFF["ident"] + 128]),
            waits=c_wready)

        x_r = x_in.rearrange("(dt p) t -> p dt t", p=128)
        xo_r = xown.rearrange("(dt p) t -> p dt t", p=128)

        def mmwait(pp, extra):
            w = list(extra)
            w.append(pfree[id(pp)])
            return w

        def evac_done(pp, sem_, cnt):
            pfree[id(pp)] = (sem_, cnt)

        def emit_ln(src_r, t0, f, wcol, want_f32=None, nb_dst=None):
            nb = nb_dst if nb_dst is not None else normed
            ld = DMA("sync", lambda e: e.dma_start(
                out=xsub[:].rearrange("p (dt t) -> p dt t", t=f),
                in_=src_r[:, :, t0:t0 + f]),
                waits=[(sV, P.cnt[sV.name]), (sA, P.cnt[sA.name])])
            pe_snap = P.cnt[sPe.name]
            cp1 = EM("vector", lambda e: e.tensor_copy(
                out=nb[:, 0:DT * f], in_=xsub[:, 0:DT * f]),
                waits=[ld, (sPe, pe_snap)], inc=sV)
            cps = [cp1] * DT
            sq1 = EM("scalar", lambda e: e.activation(
                out=xco[:, 0:DT * f], in_=xsub[:, 0:DT * f],
                func=AF.Square), waits=[ld, (sPe, pe_snap)], inc=sA)
            sqs = [sq1] * DT
            m1 = m2 = None
            for d in range(DT):
                m1 = EM("tensor", lambda e, d=d: e.matmul(
                    pS1[:, 0:f], lhsT=onesc[:], rhs=nb[:, d * f:(d + 1) * f],
                    start=(d == 0), stop=(d == DT - 1)),
                    waits=mmwait(pS1, [(sV, cps[d])]) if d == 0 else [(sV, cps[d])],
                    inc=(sPe if d == DT - 1 else None))
            cm0 = EM("vector", lambda e: e.tensor_scalar(
                out=rows[:, 0:f], in0=pS1[0:1, 0:f], scalar1=1.0 / D, scalar2=None,
                op0=ALU.mult), waits=[(sPe, m1)], inc=sV)
            evac_done(pS1, sV, cm0)
            for d in range(DT):
                m2 = EM("tensor", lambda e, d=d: e.matmul(
                    pS1[:, 0:f], lhsT=onesc[:], rhs=xco[:, d * f:(d + 1) * f],
                    start=(d == 0), stop=(d == DT - 1)),
                    waits=mmwait(pS1, [(sA, sqs[d])]) if d == 0 else [(sA, sqs[d])],
                    inc=(sPe if d == DT - 1 else None))
            cvm = EM("vector", lambda e: e.tensor_copy(
                out=rowb[:, 0:f], in_=rows[:, 0:f]), inc=sV)
            mb = EM("tensor", lambda e: e.matmul(
                pM[:, 0:f], lhsT=onesr[:], rhs=rowb[:, 0:f], start=True, stop=True),
                waits=mmwait(pM, [(sV, cvm)]), inc=sPe)
            EM("vector", lambda e: e.tensor_tensor(
                out=rows[:, f:2 * f], in0=rows[:, 0:f], in1=rows[:, 0:f],
                op=ALU.mult), inc=sV)
            cv1 = EM("vector", lambda e: e.scalar_tensor_tensor(
                out=rows[:, f:2 * f], in0=pS1[0:1, 0:f], scalar=1.0 / D,
                in1=rows[:, f:2 * f], op0=ALU.mult, op1=ALU.subtract),
                waits=[(sPe, m2)], inc=sV)
            evac_done(pS1, sV, cv1)
            EM("scalar", lambda e: e.activation(
                out=rows[:, f:2 * f], in_=rows[:, f:2 * f], func=AF.Ln, bias=EPS),
                waits=[(sV, cv1)], inc=sA)
            ca = EM("scalar", lambda e: e.activation(
                out=rows[:, 0:f], in_=rows[:, f:2 * f], func=AF.Exp,
                scale=-0.5), waits=[(sV, cvm)], inc=sA)
            if wcol is not None:
                EM("scalar", lambda e: e.activation(
                    out=rows[:, f:2 * f], in_=rows[:, f:2 * f], func=AF.Exp,
                    scale=-1.0), inc=sA)
                ca2 = EM("vector", lambda e: e.tensor_scalar(
                    out=rows[:, f:2 * f], in0=rows[:, f:2 * f], scalar1=-EPS,
                    scalar2=1.0, op0=ALU.mult, op1=ALU.add),
                    waits=[(sA, NOW(sA))], inc=sV)
                EM("scalar", lambda e: e.activation(
                    out=rows[:, f:2 * f], in_=rows[:, f:2 * f], func=AF.Ln,
                    bias=EPS), waits=[(sV, ca2)], inc=sA)
                EM("scalar", lambda e: e.activation(
                    out=rows[:, f:2 * f], in_=rows[:, f:2 * f], func=AF.Exp,
                    scale=-0.5), inc=sA)
                cv3 = EM("vector", lambda e: e.tensor_tensor(
                    out=rows[:, f:2 * f], in0=rows[:, 0:f],
                    in1=rows[:, f:2 * f], op=ALU.mult),
                    waits=[(sA, NOW(sA))], inc=sV)
            else:
                cv3 = EM("vector", lambda e: e.tensor_copy(
                    out=rows[:, f:2 * f], in_=rows[:, 0:f]),
                    waits=[(sA, ca)], inc=sV)
            cv4 = EM("vector", lambda e: e.tensor_copy(
                out=rowb[:, 0:f], in_=rows[:, f:2 * f]),
                waits=[(sPe, mb)], inc=sV)
            sc = EM("tensor", lambda e: e.matmul(
                pSc[:, 0:f], lhsT=onesr[:], rhs=rowb[:, 0:f], start=True,
                stop=True), waits=mmwait(pSc, [(sV, cv4)]), inc=sPe)
            last = None
            sa_snap = P.cnt[sA.name]
            for d in range(DT):
                EM("vector", lambda e, d=d: e.tensor_tensor(
                    out=spt[:, 0:f], in0=xsub[:, d * f:(d + 1) * f], in1=pM[:, 0:f],
                    op=ALU.subtract), waits=[(sPe, sc), (sA, sa_snap)], inc=sV)
                if want_f32 is not None:
                    EM("vector", lambda e, d=d: e.tensor_tensor(
                        out=want_f32[:, d * f:(d + 1) * f], in0=spt[:, 0:f],
                        in1=pSc[:, 0:f], op=ALU.mult), inc=sV)
                if wcol is not None:
                    last = EM("vector", lambda e, d=d: e.scalar_tensor_tensor(
                        out=nb[:, d * f:(d + 1) * f], in0=spt[:, 0:f],
                        scalar=cst[:, wcol + d:wcol + d + 1], in1=pSc[:, 0:f],
                        op0=ALU.mult, op1=ALU.mult), inc=sV)
                else:
                    last = EM("vector", lambda e, d=d: e.tensor_tensor(
                        out=nb[:, d * f:(d + 1) * f], in0=spt[:, 0:f],
                        in1=pSc[:, 0:f], op=ALU.mult), inc=sV)
            evac_done(pM, sV, last)
            evac_done(pSc, sV, last)
            return last

        wload_slot_free = [(sPe, 0), (sPe, 0)]

        def w_load(region, m, kt, slot):
            off = WOFF[region] + m * kt * 128
            return DMA("sync", lambda e: e.dma_start(
                out=wst[:, slot * WSLOT:slot * WSLOT + kt * 128],
                in_=wbf[:, off:off + kt * 128]),
                waits=[wload_slot_free[slot]] + c_wready)

        def wtile(slot, k):
            return wst[:, slot * WSLOT + k * 128:slot * WSLOT + (k + 1) * 128]

        def mm_chain(region, m, kt, rhs_fn, pp, slot, extra_waits=()):
            lw = w_load(region, m, kt, slot)
            last = None
            for k in range(kt):
                w = [lw] + list(extra_waits)
                if k == 0:
                    w = mmwait(pp, w)
                last = EM("tensor",
                          lambda e, k=k, pp=pp, slot=slot, rhs_fn=rhs_fn: e.matmul(
                              pp[:], lhsT=wtile(slot, k), rhs=rhs_fn(k),
                              start=(k == 0), stop=(k == kt - 1)),
                          waits=w, inc=(sPe if k == kt - 1 else None))
            wload_slot_free[slot] = (sPe, last)
            return last

        c_conv_done = [0] * NSUB
        c_z_read = 0
        c_scan_done_v = 0
        c_scan_done_a = 0
        c_ln_q = {}

        def emit_A_xz(s):
            sc = s % SPC
            xz = xzs[s % 2]
            gate = [(sV, c_conv_done[s - 2])] if s >= 2 else []
            c_n = emit_ln(x_r, s * FS, FS, OFF_WIN)
            c_ln_q[s] = c_n
            for m in range(ET):
                pp = (pA, pB)[m % 2]
                slot = m % 2
                mm = mm_chain("wi", m, DT, lambda k: normed[:, k * FS:(k + 1) * FS],
                              pp, slot, extra_waits=[(sV, c_n)])
                ca = EM("scalar", lambda e, m=m, pp=pp, xz=xz, sc=sc: e.activation(
                    out=xz[:, m * (FS + 3) + 1:m * (FS + 3) + 1 + FS],
                    in_=pp[:], func=AF.Copy),
                    waits=[(sPe, mm)] + gate, inc=sA)
                evac_done(pp, sA, ca)
            return NOW(sA)

        def emit_A_z(s):
            sc = s % SPC
            for mz in range(EHT):
                m = ET + mz
                pp = (pA, pB)[m % 2]
                slot = m % 2
                mm = mm_chain("wi", m, DT, lambda k: normed[:, k * FS:(k + 1) * FS],
                              pp, slot, extra_waits=[(sV, c_ln_q[s]), (sV, c_z_read)])
                ca = EM("scalar", lambda e, mz=mz, pp=pp, sc=sc: e.activation(
                    out=zsil[:, mz * FC + sc * FS:mz * FC + (sc + 1) * FS],
                    in_=pp[:], func=AF.Silu),
                    waits=[(sPe, mm)], inc=sA)
                evac_done(pp, sA, ca)
            return NOW(sA)

        def emit_patch(s, c_next_xz):
            xz = xzs[s % 2]
            tb = tailb[s % 2]
            tbp = tailb[(s - 1) % 2]
            last = None
            for m in range(ET):
                b0 = m * (FS + 3)
                if s == NSUB - 1:
                    last = EM("vector", lambda e, b0=b0: e.memset(
                        xz[:, b0 + 1 + FS:b0 + 3 + FS], 0.0),
                        waits=[(sA, c_next_xz)], inc=sV)
                else:
                    xzn = xzs[(s + 1) % 2]
                    last = EM("vector", lambda e, b0=b0, xzn=xzn: e.tensor_copy(
                        out=xz[:, b0 + 1 + FS:b0 + 3 + FS],
                        in_=xzn[:, b0 + 1:b0 + 3]),
                        waits=[(sA, c_next_xz)], inc=sV)
                last = EM("vector", lambda e, b0=b0, m=m, tb=tb: e.tensor_copy(
                    out=tb[:, m:m + 1], in_=xz[:, b0 + FS:b0 + FS + 1]), inc=sV)
                if s == 0:
                    last = EM("vector", lambda e, b0=b0: e.memset(
                        xz[:, b0:b0 + 1], 0.0), inc=sV)
                else:
                    last = EM("vector", lambda e, b0=b0, m=m, tbp=tbp: e.tensor_copy(
                        out=xz[:, b0:b0 + 1], in_=tbp[:, m:m + 1]), inc=sV)
            return last

        def emit_conv(s, c_patch):
            sc = s % SPC
            xz = xzs[s % 2]
            sil_counts = []
            for et in range(ET):
                b0 = et * (FS + 3)
                sl = et % 2
                EM("vector", lambda e, b0=b0, et=et, sl=sl: e.tensor_scalar(
                    out=spt[:, sl * FS:(sl + 1) * FS], in0=xz[:, b0:b0 + FS],
                    scalar1=cst[:, OFF_CW + et * 4:OFF_CW + et * 4 + 1],
                    scalar2=None, op0=ALU.mult),
                    waits=[(sV, c_patch), (sA, sil_counts[-2] if et >= 2 else 0),
                           (sA, c_scan_done_a)],
                    inc=sV)
                for w in range(1, 4):
                    EM("vector", lambda e, b0=b0, et=et, w=w, sl=sl:
                       e.scalar_tensor_tensor(
                           out=spt[:, sl * FS:(sl + 1) * FS],
                           in0=xz[:, b0 + w:b0 + w + FS],
                           scalar=cst[:, OFF_CW + et * 4 + w:OFF_CW + et * 4 + w + 1],
                           in1=spt[:, sl * FS:(sl + 1) * FS],
                           op0=ALU.mult, op1=ALU.add), inc=sV)
                cvv = NOW(sV)
                if et < EHT:
                    dst = xch[:, et * FC + sc * FS:et * FC + (sc + 1) * FS]
                else:
                    dst = xco[:, (et - EHT) * FS:(et - EHT + 1) * FS]
                sil_counts.append(EM("scalar", lambda e, dst=dst, et=et, sl=sl:
                                     e.activation(out=dst,
                                                  in_=spt[:, sl * FS:(sl + 1) * FS],
                                                  func=AF.Silu,
                                                  bias=cst[:, OFF_CB + et:OFF_CB + et + 1]),
                                     waits=[(sV, cvv), (sPe, P.cnt[sPe.name])],
                                     inc=sA))
            c_conv_done[s] = NOW(sV)
            return sil_counts[-1]

        def xc_rhs(sc):
            def f(k):
                if k < EHT:
                    return xch[:, k * FC + sc * FS:k * FC + (sc + 1) * FS]
                return xco[:, (k - EHT) * FS:(k - EHT + 1) * FS]
            return f

        def emit_dtbc(s, c_conv):
            sc = s % SPC
            for m in range(EHT):
                pp = (pC, pDm)[m % 2]
                slot = m % 2
                mm = mm_chain("wdt", m, ET, xc_rhs(sc), pp, slot,
                              extra_waits=[(sA, c_conv)])
                EM("scalar", lambda e, pp=pp, m=m: e.activation(
                    out=sp2[:, 0:FS], in_=pp[:], func=AF.Abs,
                    bias=cst[:, OFF_DTB + m:OFF_DTB + m + 1]),
                    waits=[(sPe, mm), (sV, NOW(sV))], inc=sA)
                EM("scalar", lambda e: e.activation(
                    out=sp2[:, 0:FS], in_=sp2[:, 0:FS], func=AF.Exp, scale=-1.0),
                    inc=sA)
                EM("scalar", lambda e: e.activation(
                    out=sp2[:, 0:FS], in_=sp2[:, 0:FS], func=AF.Ln, bias=1.0),
                    inc=sA)
                ca = EM("scalar", lambda e, pp=pp, m=m: e.activation(
                    out=sp2[:, FS:2 * FS], in_=pp[:], func=AF.Relu,
                    bias=cst[:, OFF_DTB + m:OFF_DTB + m + 1]), inc=sA)
                evac_done(pp, sA, ca)
                EM("vector", lambda e: e.tensor_tensor(
                    out=sp2[:, 0:FS], in0=sp2[:, 0:FS], in1=sp2[:, FS:2 * FS],
                    op=ALU.add), waits=[(sA, ca)], inc=sV)
                EM("vector", lambda e, m=m, sc=sc: e.tensor_scalar(
                    out=dtc[:, m * FC + sc * FS:m * FC + (sc + 1) * FS],
                    in0=sp2[:, 0:FS], scalar1=0.1, scalar2=0.001,
                    op0=ALU.min, op1=ALU.max),
                    waits=[(sA, c_scan_done_a)], inc=sV)
            mm = mm_chain("wbc", 0, ET, xc_rhs(sc), pC, 0,
                          extra_waits=[(sA, c_conv)])
            ca = EM("scalar", lambda e, sc=sc: e.activation(
                out=bcb[:, sc * FS:(sc + 1) * FS], in_=pC[:], func=AF.Copy),
                waits=[(sPe, mm), (sPe, c_bcb_free[0])], inc=sA)
            evac_done(pC, sA, ca)
            return NOW(sV), ca

        def emit_scan(c, c_dt_v, c_bc_a):
            nonlocal c_scan_done_v, c_scan_done_a
            for et in range(EHT):
                EM("vector", lambda e, et=et: e.tensor_tensor(
                    out=xch[:, et * FC:(et + 1) * FC],
                    in0=dtc[:, et * FC:(et + 1) * FC],
                    in1=xch[:, et * FC:(et + 1) * FC], op=ALU.mult),
                    waits=[(sA, c_bc_a), (sV, c_dt_v)] + qsnap("gpsimd"),
                    inc=sV)
            c_w = NOW(sV)
            grp_free = [0, 0]
            scan_tok = {}
            u_tok = {}
            yc_tok = {}
            add_tok = {}
            pool_i = {}
            idx = 0
            for g in range(NGR):
                n0 = g * NG
                bc_ready = []
                gsl = (g % 2) * 8 * FC
                for ni in range(NG):
                    n = n0 + ni
                    for (ci, row) in ((0, n), (1, NST + n)):
                        dst_base = gsl + (ci * 4 + ni) * FC
                        for half in range(FC // FS):
                            mmb = EM("tensor", lambda e, row=row, half=half:
                                     e.matmul(
                                         pBC[:],
                                         lhsT=identb[:, row:row + 1].rearrange(
                                             "p (o f) -> p o f", o=1
                                         ).broadcast_to([128, 1, 128]),
                                         rhs=bcb[:, half * FS:(half + 1) * FS],
                                         start=True, stop=True),
                                     waits=mmwait(pBC, [(sA, c_bc_a),
                                                        (sV, grp_free[g % 2]),
                                                        c_id]),
                                     inc=sPe)
                            cev = EM("scalar", lambda e, dst_base=dst_base,
                                     half=half: e.activation(
                                         out=bcgbuf[:, dst_base + half * FS:
                                                    dst_base + (half + 1) * FS],
                                         in_=pBC[:], func=AF.Copy),
                                     waits=[(sPe, mmb)], inc=sA)
                            evac_done(pBC, sA, cev)
                    bc_ready.append(NOW(sA))
                for ep in range(EHT // 2):
                    for ni in range(NG):
                        n = n0 + ni
                        iA, iB = idx, idx + 1
                        idx += 2
                        ets = (2 * ep, 2 * ep + 1)
                        for st in (0, 1):
                            EM("scalar", lambda e, st=st, et=ets[st], n=n:
                               e.activation(
                                   out=dAb[st], in_=dtc[:, et * FC:(et + 1) * FC],
                                   func=AF.Exp,
                                   scale=cst[:, OFF_AL + n:OFF_AL + n + 1]),
                               waits=[(sV, c_dt_v), (sV, c_A),
                                      (sV, scan_tok.get((iA, iB)[st] - 2, 0))],
                               inc=sA)
                        c_exp = NOW(sA)
                        for st in (0, 1):
                            i_ = (iA, iB)[st]
                            u_tok[i_] = EM("vector", lambda e, st=st,
                                           et=ets[st], ni=ni, gsl=gsl: e.tensor_tensor(
                                               out=ub[st],
                                               in0=xch[:, et * FC:(et + 1) * FC],
                                               in1=bcgbuf[:, gsl + ni * FC:
                                                          gsl + (ni + 1) * FC],
                                               op=ALU.mult),
                                           waits=[(sA, bc_ready[ni]),
                                                  (sV, c_w),
                                                  (sV, add_tok.get(i_ - 2, 0)),
                                                  (sV, scan_tok.get(i_ - 2, 0))],
                                           inc=sV, selfw=False)
                        for st in (0, 1):
                            i_ = (iA, iB)[st]
                            scan_tok[i_] = EM("vector", lambda e, st=st,
                                              et=ets[st], n=n: e.tensor_tensor_scan(
                                                  out=hbs[st], data0=dAb[st],
                                                  data1=ub[st],
                                                  initial=hst[:, et * NST + n:
                                                              et * NST + n + 1],
                                                  op0=ALU.mult, op1=ALU.add),
                                              waits=[(sA, c_exp),
                                                     (sV, u_tok[i_]),
                                                     (sV, yc_tok.get(i_ - 2, 0)),
                                                     (sPl, pool_i.get(i_ - 2, 0)),
                                                     (sPl, pool_tok.get(
                                                         (ets[st], n, (c + 1) % 2), 0))],
                                              inc=sV, selfw=False)
                        for st in (0, 1):
                            i_ = (iA, iB)[st]
                            pool_i[i_] = pool_tok[(ets[st], n, c % 2)] = EM(
                                "gpsimd", lambda e, st=st, et=ets[st], n=n:
                                e.tensor_copy(
                                    out=hst[:, et * NST + n:et * NST + n + 1],
                                    in_=hbs[st][:, FC - 1:FC]),
                                waits=[(sV, scan_tok[i_])], inc=sPl)
                        for st in (0, 1):
                            i_ = (iA, iB)[st]
                            dst = tre[:, st * FC:(st + 1) * FC] if ni == 0 else ycbs[st]
                            yc_tok[i_] = EM("vector", lambda e, st=st, ni=ni,
                                            dst=dst, gsl=gsl: e.tensor_tensor(
                                                out=dst, in0=hbs[st],
                                                in1=bcgbuf[:, gsl + (4 + ni) * FC:
                                                           gsl + (4 + ni + 1) * FC],
                                                op=ALU.mult),
                                            waits=[(sV, scan_tok[i_]),
                                                   (sA, bc_ready[ni]),
                                                   (sV, add_tok.get(i_ - 2, 0))],
                                            inc=sV, selfw=False)
                        if ni > 0:
                            for st in (0, 1):
                                i_ = (iA, iB)[st]
                                add_tok[i_] = EM("vector", lambda e, st=st:
                                                 e.tensor_tensor(
                                                     out=tre[:, st * FC:(st + 1) * FC],
                                                     in0=ycbs[st],
                                                     in1=tre[:, st * FC:(st + 1) * FC],
                                                     op=ALU.add),
                                                 waits=[(sV, yc_tok[i_])],
                                                 inc=sV, selfw=False)
                        if ni == NG - 1:
                            for st in (0, 1):
                                i_ = (iA, iB)[st]
                                et = ets[st]
                                ydst = ybuf[:, et * FC:(et + 1) * FC]
                                wt = [(sV, add_tok[i_])]
                                if g == 0:
                                    wt.append((sPe, c_ybuf_free[0]))
                                    add_tok[i_] = EM("vector", lambda e, yd=ydst, st=st:
                                       e.tensor_copy(out=yd, in_=tre[:, st * FC:(st + 1) * FC]),
                                       waits=wt, inc=sV, selfw=False)
                                else:
                                    add_tok[i_] = EM("vector", lambda e, yd=ydst, st=st:
                                       e.tensor_tensor(
                                           out=yd, in0=yd,
                                           in1=tre[:, st * FC:(st + 1) * FC],
                                           op=ALU.add),
                                       waits=wt, inc=sV, selfw=False)
                grp_free[g % 2] = NOW(sV)
            c_bcb_free[0] = P.cnt[sPe.name]
            c_scan_done_v = NOW(sV)
            c_scan_done_a = NOW(sA)
            return c_scan_done_v

        def emit_yzop(c, c_scan_v):
            nonlocal c_z_read
            for et in range(EHT):
                EM("vector", lambda e, et=et: e.tensor_tensor(
                    out=xch[:, et * FC:(et + 1) * FC],
                    in0=ybuf[:, et * FC:(et + 1) * FC],
                    in1=zsil[:, et * FC:(et + 1) * FC], op=ALU.mult),
                    waits=[(sV, c_scan_v), (sA, NOW(sA))] + qsnap("gpsimd"),
                    inc=sV)
            c_yz = NOW(sV)
            c_z_read = c_yz
            wo_r = wbf[:, WOFF["wout"]:WOFF["wout"] + EHT * D].rearrange(
                "p (k d) -> p k d", d=D)
            npc = D // 128
            for pc_ in range(npc):
                slot = pc_ % 2
                lw = DMA("sync", lambda e, pc_=pc_, slot=slot: e.dma_start(
                    out=wst[:, slot * WSLOT:slot * WSLOT + EHT * 128].rearrange(
                        "p (k f) -> p k f", f=128),
                    in_=wo_r[:, :, pc_ * 128:(pc_ + 1) * 128]),
                    waits=[wload_slot_free[slot]] + c_wready)
                lastmm = None
                for tt in range(FC // 128):
                    pp = (pA, pB)[tt % 2]
                    last = None
                    for k in range(EHT):
                        w = [(sV, c_yz), lw]
                        if k == 0:
                            w = mmwait(pp, w)
                        last = EM("tensor", lambda e, pp=pp, k=k, tt=tt, slot=slot:
                                  e.matmul(
                                      pp[:, 0:128],
                                      lhsT=xch[:, k * FC + tt * 128:k * FC + (tt + 1) * 128],
                                      rhs=wst[:, slot * WSLOT + k * 128:slot * WSLOT + (k + 1) * 128],
                                      start=(k == 0), stop=(k == EHT - 1)),
                                  waits=w, inc=(sPe if k == EHT - 1 else None))
                    lastmm = last
                    psl = (pc_ * (FC // 128) + tt) % 4
                    cv = EM("vector", lambda e, pp=pp, psl=psl: e.tensor_copy(
                        out=ptile[:, psl * 128:(psl + 1) * 128], in_=pp[:, 0:128]),
                        waits=[(sPe, last), c_pt_free[psl]], inc=sV)
                    evac_done(pp, sV, cv)
                    cst_ = DMA("gpsimd", lambda e, tt=tt, c=c, pc_=pc_, psl=psl:
                               e.dma_start(
                        out=partial[c * FC + tt * 128:c * FC + (tt + 1) * 128,
                                    pc_ * 128:(pc_ + 1) * 128],
                        in_=ptile[:, psl * 128:(psl + 1) * 128]),
                        waits=[(sV, cv)])
                    c_pt_free[psl] = cst_
                wload_slot_free[slot] = (sPe, lastmm)
            c_ybuf_free[0] = P.cnt[sPe.name]

        c_pt_free = [(sV, 0)] * 4
        pool_tok = {}
        c_bcb_free = [0]
        c_ybuf_free = [0]

        pend = None
        for s in range(NSUB if stop_after != "prologue" else 0):
            c_axz = emit_A_xz(s)
            if s >= 1:
                sp = s - 1
                c_p = emit_patch(sp, c_axz)
                c_conv = emit_conv(sp, c_p)
                c_dt_v, c_bc_a = emit_dtbc(sp, c_conv)
                if (sp + 1) % SPC == 0:
                    cch = sp // SPC
                    if cch == 0:
                        DUMP("xch0", xch[:], [(sV, c_dt_v), (sA, c_bc_a)])
                        DUMP("dtc0", dtc[:], [(sV, c_dt_v)])
                        DUMP("bcb0", bcb[:], [(sA, c_bc_a)])
                    c_sv = emit_scan(cch, c_dt_v, c_bc_a)
                    if cch == 0:
                        DUMP("ybuf0", ybuf[:], [(sV, c_sv)])
                        DUMP("hst0", hst[:], [(sV, c_sv)])
                        DUMP("zsil0", zsil[:], [(sA, P.cnt[sA.name])])
                    emit_yzop(cch, c_sv)
            emit_A_z(s)
        if stop_after != "prologue":
            sp = NSUB - 1
            c_p = emit_patch(sp, NOW(sA))
            c_conv = emit_conv(sp, c_p)
            c_dt_v, c_bc_a = emit_dtbc(sp, c_conv)
            c_sv = emit_scan(NCH - 1, c_dt_v, c_bc_a)
            emit_yzop(NCH - 1, c_sv)
        c_pi_st = qsnap("gpsimd")

        c_g_free = [(sV, 0), (sV, 0)]
        for su in range(THS if stop_after in (None, "g", "cc", "pii") else 0):
            c_n = emit_ln(xo_r, su * FS, FS, None)
            c_h1_prev = 0
            for mh in range(FHT):
                mmg = mm_chain("fgu", 2 * mh, DT,
                               lambda k: normed[:, k * FS:(k + 1) * FS], pA, 0,
                               extra_waits=[(sV, c_n)])
                mmu = mm_chain("fgu", 2 * mh + 1, DT,
                               lambda k: normed[:, k * FS:(k + 1) * FS], pB, 1,
                               extra_waits=[(sV, c_n)])
                ca1 = EM("scalar", lambda e: e.activation(
                    out=sgbuf, in_=pA[:], func=AF.Silu),
                    waits=[(sPe, mmg), (sV, c_h1_prev)], inc=sA)
                ca2 = EM("scalar", lambda e: e.activation(
                    out=upbuf, in_=pB[:], func=AF.Copy),
                    waits=[(sPe, mmu)], inc=sA)
                evac_done(pA, sA, ca1)
                evac_done(pB, sA, ca2)
                c_h1_prev = EM("vector", lambda e, mh=mh: e.tensor_tensor(
                    out=h1b[:, mh * FS:(mh + 1) * FS], in0=sgbuf,
                    in1=upbuf, op=ALU.mult),
                    waits=[(sA, ca2), (sPe, P.cnt[sPe.name])], inc=sV)
            c_h1 = NOW(sV)
            for md in range(DT):
                pp = (pC, pDm)[md % 2]
                sl = md % 2
                mm = mm_chain("fdn", md, FHT,
                              lambda k: h1b[:, k * FS:(k + 1) * FS], pp, md % 2,
                              extra_waits=[(sV, c_h1)])
                ca = EM("scalar", lambda e, pp=pp, sl=sl: e.activation(
                    out=gbuf[:, sl * FS:(sl + 1) * FS], in_=pp[:], func=AF.Silu),
                    waits=[(sPe, mm), c_g_free[sl]], inc=sA)
                evac_done(pp, sA, ca)
                cg = DMA("gpsimd", lambda e, md=md, su=su, sl=sl: e.dma_start(
                    out=gdram.rearrange("(dt p) t -> p dt t", p=128)[
                        :, md, su * FS:(su + 1) * FS],
                    in_=gbuf[:, sl * FS:(sl + 1) * FS]),
                    waits=[(sA, ca)])
                c_g_free[sl] = cg

        c_cc = 0
        if stop_after in (None, "cc", "pii"):
            c_cc = EM("gpsimd", lambda e: e.collective_compute(
                "ReduceScatter", ALU.add,
                replica_groups=[[0, 1], [2, 3], [4, 5], [6, 7]],
                ins=[partial.opt()], outs=[s1own.opt()]),
                waits=c_pi_st, inc=sCC, amt=1)

        s1_r = s1own
        if stop_after not in (None, "pii"):
            for su in range(THS):
                DMA("gpsimd", lambda e, su=su: e.dma_start(
                    out=out.rearrange("(dt p) t -> p dt t", p=128)[
                        :, :, su * FS:(su + 1) * FS],
                    in_=xsub[:].rearrange("p (dt t) -> p dt t", t=FS)),
                    waits=[(sV, P.cnt[sV.name])])
        for su in range(THS if stop_after in (None, "pii") else 0):
            c_n = emit_ln(xo_r, su * FS, FS, None, want_f32=x32a)
            lds = []
            sv_snap2 = P.cnt[sV.name]
            for d in range(DT):
                lds.append(DMA("scalar", lambda e, d=d, su=su: e.dma_start(
                    out=sfin[:, d * FS:(d + 1) * FS],
                    in_=s1_r[su * FS:(su + 1) * FS, d * 128:(d + 1) * 128],
                    transpose=True),
                    waits=[(sCC, c_cc), (sV, sv_snap2)]))
            cmid = None
            for d in range(DT):
                cmid = EM("vector", lambda e, d=d: e.tensor_tensor(
                    out=x32b[:, d * FS:(d + 1) * FS], in0=x32a[:, d * FS:(d + 1) * FS],
                    in1=sfin[:, d * FS:(d + 1) * FS], op=ALU.add),
                    waits=[lds[d], (sV, c_n)], inc=sV)
            sqs = []
            for d in range(DT):
                sqs.append(EM("scalar", lambda e, d=d: e.activation(
                    out=xco[:, d * FS:(d + 1) * FS], in_=x32b[:, d * FS:(d + 1) * FS],
                    func=AF.Square), waits=[(sV, cmid)], inc=sA))
            m2 = None
            for d in range(DT):
                m2 = EM("tensor", lambda e, d=d: e.matmul(
                    pS1[:, 0:FS], lhsT=onesc[:], rhs=xco[:, d * FS:(d + 1) * FS],
                    start=(d == 0), stop=(d == DT - 1)),
                    waits=mmwait(pS1, [(sA, sqs[d])]) if d == 0 else [(sA, sqs[d])],
                    inc=(sPe if d == DT - 1 else None))
            cv1 = EM("vector", lambda e: e.tensor_scalar(
                out=rows[:, 0:FS], in0=pS1[0:1, 0:FS], scalar1=1.0 / D, scalar2=None,
                op0=ALU.mult), waits=[(sPe, m2)], inc=sV)
            evac_done(pS1, sV, cv1)
            EM("scalar", lambda e: e.activation(
                out=rows[:, FS:2 * FS], in_=rows[:, 0:FS], func=AF.Ln, bias=EPS),
                waits=[(sV, cv1)], inc=sA)
            ca = EM("scalar", lambda e: e.activation(
                out=rows[:, FS:2 * FS], in_=rows[:, FS:2 * FS], func=AF.Exp,
                scale=-0.5), inc=sA)
            cv2 = EM("vector", lambda e: e.tensor_copy(
                out=rowb[:, 0:FS], in_=rows[:, FS:2 * FS]),
                waits=[(sA, ca)], inc=sV)
            mb = EM("tensor", lambda e: e.matmul(
                pSc[:, 0:FS], lhsT=onesr[:], rhs=rowb[:, 0:FS], start=True,
                stop=True), waits=mmwait(pSc, [(sV, cv2)]), inc=sPe)
            cn2 = None
            for d in range(DT):
                cn2 = EM("vector", lambda e, d=d: e.scalar_tensor_tensor(
                    out=normed[:, d * FS:(d + 1) * FS],
                    in0=x32b[:, d * FS:(d + 1) * FS],
                    scalar=cst[:, OFF_WFN + d:OFF_WFN + d + 1], in1=pSc[:, 0:FS],
                    op0=ALU.mult, op1=ALU.mult), waits=[(sPe, mb)], inc=sV)
            evac_done(pSc, sV, cn2)
            c_h1_prev = 0
            for mh in range(ET):
                mmg = mm_chain("wgu", 2 * mh, DT,
                               lambda k: normed[:, k * FS:(k + 1) * FS], pA, 0,
                               extra_waits=[(sV, cn2)])
                mmu = mm_chain("wgu", 2 * mh + 1, DT,
                               lambda k: normed[:, k * FS:(k + 1) * FS], pB, 1,
                               extra_waits=[(sV, cn2)])
                ca1 = EM("scalar", lambda e: e.activation(
                    out=sgbuf, in_=pA[:], func=AF.Silu),
                    waits=[(sPe, mmg), (sV, c_h1_prev)], inc=sA)
                ca2 = EM("scalar", lambda e: e.activation(
                    out=upbuf, in_=pB[:], func=AF.Copy),
                    waits=[(sPe, mmu)], inc=sA)
                evac_done(pA, sA, ca1)
                evac_done(pB, sA, ca2)
                c_h1_prev = EM("vector", lambda e, mh=mh: e.tensor_tensor(
                    out=h1b[:, mh * FS:(mh + 1) * FS], in0=sgbuf,
                    in1=upbuf, op=ALU.mult),
                    waits=[(sA, ca2), (sPe, P.cnt[sPe.name])], inc=sV)
            c_h1 = NOW(sV)
            ldg = DMA("scalar", lambda e, su=su: e.dma_start(
                out=gbuf.rearrange("p (dt t) -> p dt t", t=FS),
                in_=gdram.rearrange("(dt p) t -> p dt t", p=128)[
                    :, :, su * FS:(su + 1) * FS]),
                waits=qsnap("gpsimd") + [(sV, c_h1)])
            for md in range(DT):
                pp = (pC, pDm)[md % 2]
                mm = mm_chain("wdn", md, ET,
                              lambda k: h1b[:, k * FS:(k + 1) * FS], pp, md % 2,
                              extra_waits=[(sV, c_h1)])
                cs = EM("vector", lambda e, pp=pp, md=md: e.tensor_tensor(
                    out=sfin[:, md * FS:(md + 1) * FS], in0=pp[:],
                    in1=x32b[:, md * FS:(md + 1) * FS], op=ALU.add),
                    waits=[(sPe, mm)], inc=sV)
                evac_done(pp, sV, cs)
                EM("vector", lambda e, md=md: e.tensor_tensor(
                    out=spt[:, 0:FS], in0=sfin[:, md * FS:(md + 1) * FS],
                    in1=gbuf[:, md * FS:(md + 1) * FS], op=ALU.mult),
                    waits=[ldg], inc=sV)
                cfo = EM("vector", lambda e, md=md: e.tensor_tensor(
                    out=x32a[:, md * FS:(md + 1) * FS],
                    in0=xsub[:, md * FS:(md + 1) * FS], in1=spt[:, 0:FS],
                    op=ALU.add), inc=sV)
            c_fin = NOW(sV)
            DMA("gpsimd", lambda e, su=su: e.dma_start(
                out=out.rearrange("(dt p) t -> p dt t", p=128)[
                    :, :, su * FS:(su + 1) * FS],
                in_=x32a[:].rearrange("p (dt t) -> p dt t", t=FS)),
                waits=[(sV, c_fin)])
        EM("gpsimd", lambda e: e.nop(),
           waits=qsnap("gpsimd") + qsnap("sync") + qsnap("scalar"))

        nc._dumps = dumps
        qs = {e: [] for e in Prog.ENG}
        for op in P.ops:
            qs[op.eng].append(op)

        def runner(ops):
            def run(eng):
                for op in ops:
                    for (s, c) in op.waits:
                        eng.wait_ge(s, c)
                    ins = op.fn(eng)
                    if op.inc is not None:
                        ins.then_inc(op.inc, op.amt)
            return run

        with nc.Block() as block:
            block.sync(runner(qs["sync"]))
            block.tensor(runner(qs["tensor"]))
            block.vector(runner(qs["vector"]))
            block.scalar(runner(qs["scalar"]))
            block.gpsimd(runner(qs["gpsimd"]))

    return nc


def kernel(**inputs):
    T = int(np.asarray(inputs["L0_attn_x"]).shape[1])
    nc = build_nc(T=T)
    in_maps = []
    for core in range(8):
        b, j = core // 2, core % 2
        in_maps.append(pack_core(inputs, b, j))
    res = run_bass_kernel_spmd(nc, in_maps, core_ids=list(range(8)))
    x = np.asarray(inputs["L0_attn_x"])
    B, T_, D_ = x.shape
    out = np.empty((B, T_, D_), np.float32)
    for core in range(8):
        b, j = core // 2, core % 2
        o = res.results[core]["out"]
        out[b, j * (T_ // 2):(j + 1) * (T_ // 2), :] = o.T
    return out


# version 07
# speedup vs baseline: 1.2177x; 1.0051x over previous
import contextlib
import numpy as np
import ml_dtypes
import concourse.bass as bass
import concourse.mybir as mybir
from concourse.bass_utils import run_bass_kernel_spmd

F32 = mybir.dt.float32
BF16 = mybir.dt.bfloat16
I32 = mybir.dt.int32
AF = mybir.ActivationFunctionType
ALU = mybir.AluOpType

D = 1024
E = 2048
EH = 1024
NST = 64
FH = 3072
DT = D // 128
ET = E // 128
EHT = EH // 128
FHT = FH // 128
EPS = 1e-5

_SIZES_F = [("ident", 128), ("wi", (ET + EHT) * DT * 128), ("wdt", EHT * ET * 128),
            ("wbc", 1 * ET * 128), ("wout", EHT * D),
            ("wgu", 2 * ET * DT * 128), ("wdn", DT * ET * 128)]
_SIZES_I = [("fgu", 2 * FHT * DT * 128), ("fdn", DT * FHT * 128)]
XF_COLS = sum(s for _, s in _SIZES_F)
XI_COLS = sum(s for _, s in _SIZES_I)
WOFF = {}
_o = 0
for _n, _s in _SIZES_F:
    WOFF[_n] = _o
    _o += _s
for _n, _s in _SIZES_I:
    WOFF[_n] = _o
    _o += _s
OFF_CW, OFF_CB, OFF_DTB, OFF_WIN, OFF_WFN, OFF_AL = 0, 64, 80, 88, 96, 104
OFF_K = 168
CONSTS = (1e-5, 1.0, -1.0, -0.5)
SM_COLS = OFF_K + len(CONSTS)


def _tileize(w, kt, mt):
    a = np.asarray(w).reshape(kt, 128, mt, 128).transpose(2, 1, 0, 3)
    return np.ascontiguousarray(
        a.reshape(mt, 128, kt * 128).transpose(1, 0, 2).reshape(128, mt * kt * 128))


def _rhs_tiles(w, kt, f):
    a = np.asarray(w).reshape(kt, 128, f)
    return np.ascontiguousarray(a.transpose(1, 0, 2).reshape(128, kt * f))


def _interleave_m(a, b):
    Dd, M = a.shape
    mt = M // 128
    o = np.empty((Dd, 2 * M), a.dtype)
    for m in range(mt):
        o[:, (2 * m) * 128:(2 * m + 1) * 128] = a[:, m * 128:(m + 1) * 128]
        o[:, (2 * m + 1) * 128:(2 * m + 2) * 128] = b[:, m * 128:(m + 1) * 128]
    return o


def pack_core(inp, b, j):
    f32 = np.float32
    x = np.asarray(inp["L0_attn_x"], f32)[b]
    T = x.shape[0]
    xT = np.ascontiguousarray(x.T)
    xo = np.ascontiguousarray(xT[:, j * (T // 2):(j + 1) * (T // 2)])

    perm = np.concatenate([np.arange(j * EH, (j + 1) * EH),
                           np.arange((1 - j) * EH, (2 - j) * EH)])
    Wi = np.asarray(inp["L0_attn_in_proj_W"], f32)
    cols = np.concatenate([perm, E + j * EH + np.arange(EH)])
    parts_f = [
        np.eye(128, dtype=f32),
        _tileize(Wi[:, cols], DT, ET + EHT),
        _tileize(np.asarray(inp["L0_attn_dt_W"], f32)[perm][:, j * EH:(j + 1) * EH],
                 ET, EHT),
        _tileize(np.concatenate([np.asarray(inp["L0_attn_B_W"], f32)[perm],
                                 np.asarray(inp["L0_attn_C_W"], f32)[perm]], axis=1),
                 ET, 1),
        _rhs_tiles(np.asarray(inp["L0_attn_out_proj_W"], f32)[j * EH:(j + 1) * EH],
                   EHT, D),
        _tileize(_interleave_m(np.asarray(inp["L0_attn_gate_W"], f32),
                               np.asarray(inp["L0_attn_up_W"], f32)), DT, 2 * ET),
        _tileize(np.asarray(inp["L0_attn_down_W"], f32), ET, DT),
    ]
    wf = np.concatenate(parts_f, axis=1)
    parts_i = [
        _tileize(_interleave_m(np.asarray(inp["L0_ffn_w_gate"], np.int32).T,
                               np.asarray(inp["L0_ffn_w_up"], np.int32).T),
                 DT, 2 * FHT),
        _tileize(np.asarray(inp["L0_ffn_w_down"], np.int32).T, FHT, DT),
    ]
    wi = np.concatenate(parts_i, axis=1)

    cw = np.asarray(inp["L0_attn_conv_W"], f32)[:, 0, :].T[perm]
    cwp = cw.reshape(ET, 128, 4).transpose(1, 0, 2).reshape(128, ET * 4)
    cb = np.asarray(inp["L0_attn_conv_b"], f32)[perm].reshape(ET, 128).T
    dtb = np.asarray(inp["L0_attn_dt_b"], f32)[j * EH:(j + 1) * EH].reshape(EHT, 128).T
    win = np.asarray(inp["L0_attn_input_norm_w"], f32).reshape(DT, 128).T
    wfn = np.asarray(inp["L0_attn_ffn_norm_w"], f32).reshape(DT, 128).T
    alog = np.broadcast_to(np.asarray(inp["L0_attn_A_log"], f32)[None, :],
                           (128, NST)).copy()
    kc = np.broadcast_to(np.asarray(CONSTS, f32)[None, :], (128, len(CONSTS))).copy()
    smalls = np.concatenate([cwp, cb, dtb, win, wfn, alog, kc], axis=1).astype(f32)
    assert smalls.shape[1] == SM_COLS
    return {"x": xT, "x_own": xo, "wf32": np.ascontiguousarray(wf),
            "wi32": np.ascontiguousarray(wi), "smalls": smalls}


class Op:
    __slots__ = ("eng", "waits", "inc", "amt", "fn")

    def __init__(self, eng, waits, inc, amt, fn):
        self.eng, self.waits, self.inc, self.amt, self.fn = eng, waits, inc, amt, fn


class Prog:
    ENG = ("sync", "tensor", "vector", "scalar", "gpsimd")

    def __init__(self):
        self.ops = []
        self.cnt = {}
        self.seen = {e: {} for e in self.ENG}

    def reg(self, s):
        self.cnt[s.name] = 0
        return s

    def emit(self, eng, fn, waits=(), inc=None, amt=1, selfw=True):
        waits = list(waits)
        if eng in ("vector", "scalar") and selfw and inc is not None:
            waits.append((inc, self.cnt[inc.name]))
        kept = []
        for (s, c) in waits:
            if c > 0 and self.seen[eng].get(s.name, 0) < c:
                kept.append((s, c))
                self.seen[eng][s.name] = c
        self.ops.append(Op(eng, kept, inc, amt, fn))
        if inc is not None:
            self.cnt[inc.name] += amt
            return self.cnt[inc.name]
        return None

    def now(self, s):
        return self.cnt[s.name]


def build_nc(T=8192, FC=1024, FS=512, NG=4, debug=False, dump=False, stop_after=None):
    NCH = T // FC
    NSUB = T // FS
    SPC = FC // FS
    TH = T // 2
    THS = TH // FS
    NGR = NST // NG

    nc = bass.Bass("TRN2", target_bir_lowering=False, debug=debug)

    x_in = nc.dram_tensor("x", [D, T], F32, kind="ExternalInput").ap()
    xown = nc.dram_tensor("x_own", [D, TH], F32, kind="ExternalInput").ap()
    wf32 = nc.dram_tensor("wf32", [128, XF_COLS], F32, kind="ExternalInput").ap()
    wi32 = nc.dram_tensor("wi32", [128, XI_COLS], I32, kind="ExternalInput").ap()
    smalls = nc.dram_tensor("smalls", [128, SM_COLS], F32, kind="ExternalInput").ap()
    out = nc.dram_tensor("out", [D, TH], F32, kind="ExternalOutput").ap()

    wbf_t = nc.dram_tensor("wbf", [128, XF_COLS + XI_COLS], BF16)
    partial_t = nc.dram_tensor("partial", [T, D], BF16)
    s1own_t = nc.dram_tensor("s1own", [TH, D], BF16)
    gdram_t = nc.dram_tensor("gdram", [D, TH], BF16)
    wbf = wbf_t.ap()
    partial = partial_t.ap()
    s1own = s1own_t.ap()
    gdram = gdram_t.ap()

    P = Prog()
    dumps = {}

    with contextlib.ExitStack() as st:
        def sb(name, shape, dt):
            return st.enter_context(nc.sbuf_tensor(name, shape, dt))

        def psb(name, shape):
            return st.enter_context(nc.psum_tensor(name, shape, F32))

        def sem(name):
            return P.reg(st.enter_context(nc.semaphore(name)))

        xsub = sb("xsub", [128, DT * FS], F32)
        normed = sb("normed", [128, DT * FS], BF16)
        WSLOT = 3072
        wst = sb("wst", [128, 2 * WSLOT], BF16)
        xzs = [sb(f"xz{i}", [128, ET * (FS + 3)], BF16) for i in range(2)]
        chunkbuf = sb("chunkbuf", [128, max(3 * EHT * FC, FHT * FS)], BF16)
        zsil = chunkbuf[:, 0:EHT * FC]
        dtc = chunkbuf[:, EHT * FC:2 * EHT * FC]
        xch = chunkbuf[:, 2 * EHT * FC:3 * EHT * FC]
        h1b = chunkbuf[:, 0:FHT * FS]
        xco = sb("xco", [128, EHT * FS], BF16)
        bcb = sb("bcb", [128, FC], BF16)
        bcgbuf = sb("bcgbuf", [128, max(4 * NG * FC, 4 * DT * FS)], BF16)
        bcg = [bcgbuf[:, 0:2 * NG * FC], bcgbuf[:, 2 * NG * FC:4 * NG * FC]]
        x32a = bcgbuf[:, 0:2 * DT * FS].bitcast(F32)
        x32b = bcgbuf[:, 2 * DT * FS:4 * DT * FS].bitcast(F32)
        dabuf = sb("dabuf", [128, max(2 * FC, DT * FS // 2)], F32)
        dAb = [dabuf[:, 0:FC], dabuf[:, FC:2 * FC]]
        sfin = dabuf[:, 0:DT * FS // 2].bitcast(BF16)
        scanb = sb("scanb", [128, max(4 * FC, DT * FS)], BF16)
        ub = [scanb[:, 0:FC], scanb[:, FC:2 * FC]]
        hbs = [scanb[:, 2 * FC:3 * FC], scanb[:, 3 * FC:4 * FC]]
        ycbs = ub
        gbuf = scanb[:, 0:DT * FS]
        BCGC = max(4 * NG * FC, 4 * DT * FS)
        cvtf = bcgbuf[:, 0:BCGC].bitcast(F32)
        cvto = chunkbuf[:, 0:min(8192, 3 * EHT * FC)]
        tre = sb("tre", [128, max(2 * FC, 2 * FS)], BF16)
        sgbuf = tre[:, 0:FS]
        upbuf = tre[:, FS:2 * FS]
        ybuf = sb("ybuf", [128, EHT * FC], BF16)
        hst = sb("hst", [128, EHT * NST], F32)
        ptile = sb("ptile", [128, 4 * 128], BF16)
        spt = sb("spt", [128, 2 * FS], F32)
        sp2 = spt
        rows = sb("rows", [1, 2 * FS], F32)
        rowb = sb("rowb", [1, FS], BF16)
        onesc = sb("onesc", [128, 1], BF16)
        onesr = sb("onesr", [1, 128], BF16)
        cst = sb("cst", [128, SM_COLS], F32)
        identb = sb("identb", [128, 128], BF16)
        tailb = [sb(f"tail{i}", [128, ET], BF16) for i in range(2)]

        pA = psb("pA", [128, FS])
        pB = psb("pB", [128, FS])
        pC = psb("pC", [128, FS])
        pDm = psb("pD", [128, FS])
        pS1 = psb("pS1", [1, FS])
        pBC = psb("pBC", [128, FS])
        pM = psb("pM", [128, FS])
        pSc = psb("pSc", [128, FS])

        sIn = [sem(f"sIn{i}") for i in range(4)]
        sG = [sem(f"sG{i}") for i in range(4)]
        sScq = [sem(f"sScq{i}") for i in range(4)]
        sPe = sem("sPe")
        sV = sem("sV")
        sA = sem("sA")
        sCC = sem("sCC")
        sPl = sem("sPl")

        EM = P.emit
        NOW = P.now
        for ki, kv in enumerate(CONSTS):
            nc.const_aps.register(F32, kv, cst[:, OFF_K + ki:OFF_K + ki + 1])

        QSEM = {"sync": sIn, "gpsimd": sG, "scalar": sScq}
        QROT = {"sync": 0, "gpsimd": 0, "scalar": 0}

        def DMA(eng, fn, waits=()):
            q = QSEM[eng][QROT[eng] % 4]
            QROT[eng] += 1
            w = list(waits) + [(q, P.cnt[q.name])]
            EM(eng, fn, waits=w, inc=q, amt=16)
            return (q, P.cnt[q.name])

        def qsnap(eng):
            return [(s, P.cnt[s.name]) for s in QSEM[eng]]

        def DUMP(name, ap, waits):
            if not dump:
                return
            t = nc.dram_tensor("dbg_" + name, list(ap.shape), ap.dtype)
            dumps[name] = t
            DMA("gpsimd", lambda e, ap=ap, t=t: e.dma_start(out=t.ap(), in_=ap),
                waits=waits)
        pfree = {id(pA): (sV, 0), id(pB): (sV, 0), id(pC): (sV, 0), id(pDm): (sV, 0),
                 id(pS1): (sV, 0), id(pBC): (sV, 0), id(pM): (sV, 0), id(pSc): (sV, 0)}

        c_sm = DMA("sync", lambda e: e.dma_start(out=cst[:], in_=smalls))
        EM("vector", lambda e: e.memset(onesc[:], 1.0), inc=sV)
        EM("vector", lambda e: e.memset(onesr[:], 1.0), inc=sV)
        EM("vector", lambda e: e.memset(hst[:], 0.0), inc=sV)
        c_A1 = EM("scalar", lambda e: e.activation(
            out=cst[:, OFF_AL:OFF_AL + NST], in_=cst[:, OFF_AL:OFF_AL + NST],
            func=AF.Exp), waits=[c_sm], inc=sA)
        c_A = EM("vector", lambda e: e.tensor_scalar(
            out=cst[:, OFF_AL:OFF_AL + NST], in0=cst[:, OFF_AL:OFF_AL + NST],
            scalar1=-1.0, scalar2=None, op0=ALU.mult),
            waits=[(sA, c_A1)], inc=sV)

        CH = min(BCGC // 2, 3 * EHT * FC)
        prev_cv = 0
        for (tot, srcap, isint) in ((XF_COLS, wf32, False), (XI_COLS, wi32, True)):
            nchk = (tot + CH - 1) // CH
            base = 0 if not isint else XF_COLS
            for i in range(nchk):
                c0, w = i * CH, min((i + 1) * CH, tot) - i * CH
                st_toks = qsnap("gpsimd")
                if isint:
                    ld = DMA("sync", lambda e, c0=c0, w=w, srcap=srcap: e.dma_start(
                        out=cvtf[:, 0:w].bitcast(I32),
                        in_=srcap[:, c0:c0 + w]),
                        waits=[(sV, prev_cv)])
                    cv = EM("vector", lambda e, w=w: e.tensor_scalar(
                        out=cvto[:, 0:w], in0=cvtf[:, 0:w].bitcast(I32),
                        scalar1=1.0 / 127.0, scalar2=None, op0=ALU.mult),
                        waits=[ld] + st_toks, inc=sV)
                else:
                    ld = DMA("sync", lambda e, c0=c0, w=w, srcap=srcap: e.dma_start(
                        out=cvtf[:, 0:w], in_=srcap[:, c0:c0 + w]),
                        waits=[(sV, prev_cv)])
                    cv = EM("vector", lambda e, w=w: e.tensor_copy(
                        out=cvto[:, 0:w], in_=cvtf[:, 0:w]),
                        waits=[ld] + st_toks, inc=sV)
                prev_cv = cv
                DMA("gpsimd", lambda e, c0=c0, w=w, base=base: e.dma_start(
                    out=wbf[:, base + c0:base + c0 + w], in_=cvto[:, 0:w]),
                    waits=[(sV, cv)])
        c_wready = qsnap("gpsimd")
        c_id = DMA("sync", lambda e: e.dma_start(
            out=identb[:], in_=wbf[:, WOFF["ident"]:WOFF["ident"] + 128]),
            waits=c_wready)

        x_r = x_in.rearrange("(dt p) t -> p dt t", p=128)
        xo_r = xown.rearrange("(dt p) t -> p dt t", p=128)

        def mmwait(pp, extra):
            w = list(extra)
            w.append(pfree[id(pp)])
            return w

        def evac_done(pp, sem_, cnt):
            pfree[id(pp)] = (sem_, cnt)

        def emit_ln(src_r, t0, f, wcol, want_f32=None, nb_dst=None):
            nb = nb_dst if nb_dst is not None else normed
            ld = DMA("sync", lambda e: e.dma_start(
                out=xsub[:].rearrange("p (dt t) -> p dt t", t=f),
                in_=src_r[:, :, t0:t0 + f]),
                waits=[(sV, P.cnt[sV.name]), (sA, P.cnt[sA.name])])
            pe_snap = P.cnt[sPe.name]
            cp1 = EM("vector", lambda e: e.tensor_copy(
                out=nb[:, 0:DT * f], in_=xsub[:, 0:DT * f]),
                waits=[ld, (sPe, pe_snap)], inc=sV)
            cps = [cp1] * DT
            sq1 = EM("scalar", lambda e: e.activation(
                out=xco[:, 0:DT * f], in_=xsub[:, 0:DT * f],
                func=AF.Square), waits=[ld, (sPe, pe_snap)], inc=sA)
            sqs = [sq1] * DT
            m1 = m2 = None
            for d in range(DT):
                m1 = EM("tensor", lambda e, d=d: e.matmul(
                    pS1[:, 0:f], lhsT=onesc[:], rhs=nb[:, d * f:(d + 1) * f],
                    start=(d == 0), stop=(d == DT - 1)),
                    waits=mmwait(pS1, [(sV, cps[d])]) if d == 0 else [(sV, cps[d])],
                    inc=(sPe if d == DT - 1 else None))
            cm0 = EM("vector", lambda e: e.tensor_scalar(
                out=rows[:, 0:f], in0=pS1[0:1, 0:f], scalar1=1.0 / D, scalar2=None,
                op0=ALU.mult), waits=[(sPe, m1)], inc=sV)
            evac_done(pS1, sV, cm0)
            for d in range(DT):
                m2 = EM("tensor", lambda e, d=d: e.matmul(
                    pS1[:, 0:f], lhsT=onesc[:], rhs=xco[:, d * f:(d + 1) * f],
                    start=(d == 0), stop=(d == DT - 1)),
                    waits=mmwait(pS1, [(sA, sqs[d])]) if d == 0 else [(sA, sqs[d])],
                    inc=(sPe if d == DT - 1 else None))
            cvm = EM("vector", lambda e: e.tensor_copy(
                out=rowb[:, 0:f], in_=rows[:, 0:f]), inc=sV)
            mb = EM("tensor", lambda e: e.matmul(
                pM[:, 0:f], lhsT=onesr[:], rhs=rowb[:, 0:f], start=True, stop=True),
                waits=mmwait(pM, [(sV, cvm)]), inc=sPe)
            EM("vector", lambda e: e.tensor_tensor(
                out=rows[:, f:2 * f], in0=rows[:, 0:f], in1=rows[:, 0:f],
                op=ALU.mult), inc=sV)
            cv1 = EM("vector", lambda e: e.scalar_tensor_tensor(
                out=rows[:, f:2 * f], in0=pS1[0:1, 0:f], scalar=1.0 / D,
                in1=rows[:, f:2 * f], op0=ALU.mult, op1=ALU.subtract),
                waits=[(sPe, m2)], inc=sV)
            evac_done(pS1, sV, cv1)
            EM("scalar", lambda e: e.activation(
                out=rows[:, f:2 * f], in_=rows[:, f:2 * f], func=AF.Ln, bias=EPS),
                waits=[(sV, cv1)], inc=sA)
            ca = EM("scalar", lambda e: e.activation(
                out=rows[:, 0:f], in_=rows[:, f:2 * f], func=AF.Exp,
                scale=-0.5), waits=[(sV, cvm)], inc=sA)
            if wcol is not None:
                EM("scalar", lambda e: e.activation(
                    out=rows[:, f:2 * f], in_=rows[:, f:2 * f], func=AF.Exp,
                    scale=-1.0), inc=sA)
                ca2 = EM("vector", lambda e: e.tensor_scalar(
                    out=rows[:, f:2 * f], in0=rows[:, f:2 * f], scalar1=-EPS,
                    scalar2=1.0, op0=ALU.mult, op1=ALU.add),
                    waits=[(sA, NOW(sA))], inc=sV)
                EM("scalar", lambda e: e.activation(
                    out=rows[:, f:2 * f], in_=rows[:, f:2 * f], func=AF.Ln,
                    bias=EPS), waits=[(sV, ca2)], inc=sA)
                EM("scalar", lambda e: e.activation(
                    out=rows[:, f:2 * f], in_=rows[:, f:2 * f], func=AF.Exp,
                    scale=-0.5), inc=sA)
                cv3 = EM("vector", lambda e: e.tensor_tensor(
                    out=rows[:, f:2 * f], in0=rows[:, 0:f],
                    in1=rows[:, f:2 * f], op=ALU.mult),
                    waits=[(sA, NOW(sA))], inc=sV)
            else:
                cv3 = EM("vector", lambda e: e.tensor_copy(
                    out=rows[:, f:2 * f], in_=rows[:, 0:f]),
                    waits=[(sA, ca)], inc=sV)
            cv4 = EM("vector", lambda e: e.tensor_copy(
                out=rowb[:, 0:f], in_=rows[:, f:2 * f]),
                waits=[(sPe, mb)], inc=sV)
            sc = EM("tensor", lambda e: e.matmul(
                pSc[:, 0:f], lhsT=onesr[:], rhs=rowb[:, 0:f], start=True,
                stop=True), waits=mmwait(pSc, [(sV, cv4)]), inc=sPe)
            last = None
            sa_snap = P.cnt[sA.name]
            for d in range(DT):
                EM("vector", lambda e, d=d: e.tensor_tensor(
                    out=spt[:, 0:f], in0=xsub[:, d * f:(d + 1) * f], in1=pM[:, 0:f],
                    op=ALU.subtract), waits=[(sPe, sc), (sA, sa_snap)], inc=sV)
                if want_f32 is not None:
                    EM("vector", lambda e, d=d: e.tensor_tensor(
                        out=want_f32[:, d * f:(d + 1) * f], in0=spt[:, 0:f],
                        in1=pSc[:, 0:f], op=ALU.mult), inc=sV)
                if wcol is not None:
                    last = EM("vector", lambda e, d=d: e.scalar_tensor_tensor(
                        out=nb[:, d * f:(d + 1) * f], in0=spt[:, 0:f],
                        scalar=cst[:, wcol + d:wcol + d + 1], in1=pSc[:, 0:f],
                        op0=ALU.mult, op1=ALU.mult), inc=sV)
                else:
                    last = EM("vector", lambda e, d=d: e.tensor_tensor(
                        out=nb[:, d * f:(d + 1) * f], in0=spt[:, 0:f],
                        in1=pSc[:, 0:f], op=ALU.mult), inc=sV)
            evac_done(pM, sV, last)
            evac_done(pSc, sV, last)
            return last

        wload_slot_free = [(sPe, 0), (sPe, 0)]

        def w_load(region, m, kt, slot):
            off = WOFF[region] + m * kt * 128
            return DMA("sync", lambda e: e.dma_start(
                out=wst[:, slot * WSLOT:slot * WSLOT + kt * 128],
                in_=wbf[:, off:off + kt * 128]),
                waits=[wload_slot_free[slot]] + c_wready)

        def wtile(slot, k):
            return wst[:, slot * WSLOT + k * 128:slot * WSLOT + (k + 1) * 128]

        def mm_chain(region, m, kt, rhs_fn, pp, slot, extra_waits=()):
            lw = w_load(region, m, kt, slot)
            last = None
            for k in range(kt):
                w = [lw] + list(extra_waits)
                if k == 0:
                    w = mmwait(pp, w)
                last = EM("tensor",
                          lambda e, k=k, pp=pp, slot=slot, rhs_fn=rhs_fn: e.matmul(
                              pp[:], lhsT=wtile(slot, k), rhs=rhs_fn(k),
                              start=(k == 0), stop=(k == kt - 1)),
                          waits=w, inc=(sPe if k == kt - 1 else None))
            wload_slot_free[slot] = (sPe, last)
            return last

        c_conv_done = [0] * NSUB
        c_z_read = 0
        c_scan_done_v = 0
        c_scan_done_a = 0
        c_ln_q = {}

        def emit_A_xz(s):
            sc = s % SPC
            xz = xzs[s % 2]
            gate = [(sV, c_conv_done[s - 2])] if s >= 2 else []
            c_n = emit_ln(x_r, s * FS, FS, OFF_WIN)
            c_ln_q[s] = c_n
            for m in range(ET):
                pp = (pA, pB)[m % 2]
                slot = m % 2
                mm = mm_chain("wi", m, DT, lambda k: normed[:, k * FS:(k + 1) * FS],
                              pp, slot, extra_waits=[(sV, c_n)])
                ca = EM("scalar", lambda e, m=m, pp=pp, xz=xz, sc=sc: e.activation(
                    out=xz[:, m * (FS + 3) + 1:m * (FS + 3) + 1 + FS],
                    in_=pp[:], func=AF.Copy),
                    waits=[(sPe, mm)] + gate, inc=sA)
                evac_done(pp, sA, ca)
            return NOW(sA)

        def emit_A_z(s):
            sc = s % SPC
            for mz in range(EHT):
                m = ET + mz
                pp = (pA, pB)[m % 2]
                slot = m % 2
                mm = mm_chain("wi", m, DT, lambda k: normed[:, k * FS:(k + 1) * FS],
                              pp, slot, extra_waits=[(sV, c_ln_q[s]), (sV, c_z_read)])
                ca = EM("scalar", lambda e, mz=mz, pp=pp, sc=sc: e.activation(
                    out=zsil[:, mz * FC + sc * FS:mz * FC + (sc + 1) * FS],
                    in_=pp[:], func=AF.Silu),
                    waits=[(sPe, mm)], inc=sA)
                evac_done(pp, sA, ca)
            return NOW(sA)

        def emit_patch(s, c_next_xz):
            xz = xzs[s % 2]
            tb = tailb[s % 2]
            tbp = tailb[(s - 1) % 2]
            last = None
            for m in range(ET):
                b0 = m * (FS + 3)
                if s == NSUB - 1:
                    last = EM("vector", lambda e, b0=b0: e.memset(
                        xz[:, b0 + 1 + FS:b0 + 3 + FS], 0.0),
                        waits=[(sA, c_next_xz)], inc=sV)
                else:
                    xzn = xzs[(s + 1) % 2]
                    last = EM("vector", lambda e, b0=b0, xzn=xzn: e.tensor_copy(
                        out=xz[:, b0 + 1 + FS:b0 + 3 + FS],
                        in_=xzn[:, b0 + 1:b0 + 3]),
                        waits=[(sA, c_next_xz)], inc=sV)
                last = EM("vector", lambda e, b0=b0, m=m, tb=tb: e.tensor_copy(
                    out=tb[:, m:m + 1], in_=xz[:, b0 + FS:b0 + FS + 1]), inc=sV)
                if s == 0:
                    last = EM("vector", lambda e, b0=b0: e.memset(
                        xz[:, b0:b0 + 1], 0.0), inc=sV)
                else:
                    last = EM("vector", lambda e, b0=b0, m=m, tbp=tbp: e.tensor_copy(
                        out=xz[:, b0:b0 + 1], in_=tbp[:, m:m + 1]), inc=sV)
            return last

        def emit_conv(s, c_patch):
            sc = s % SPC
            xz = xzs[s % 2]
            sil_counts = []
            for et in range(ET):
                b0 = et * (FS + 3)
                sl = et % 2
                EM("vector", lambda e, b0=b0, et=et, sl=sl: e.tensor_scalar(
                    out=spt[:, sl * FS:(sl + 1) * FS], in0=xz[:, b0:b0 + FS],
                    scalar1=cst[:, OFF_CW + et * 4:OFF_CW + et * 4 + 1],
                    scalar2=None, op0=ALU.mult),
                    waits=[(sV, c_patch), (sA, sil_counts[-2] if et >= 2 else 0),
                           (sA, c_scan_done_a)],
                    inc=sV)
                for w in range(1, 4):
                    EM("vector", lambda e, b0=b0, et=et, w=w, sl=sl:
                       e.scalar_tensor_tensor(
                           out=spt[:, sl * FS:(sl + 1) * FS],
                           in0=xz[:, b0 + w:b0 + w + FS],
                           scalar=cst[:, OFF_CW + et * 4 + w:OFF_CW + et * 4 + w + 1],
                           in1=spt[:, sl * FS:(sl + 1) * FS],
                           op0=ALU.mult, op1=ALU.add), inc=sV)
                cvv = NOW(sV)
                if et < EHT:
                    dst = xch[:, et * FC + sc * FS:et * FC + (sc + 1) * FS]
                else:
                    dst = xco[:, (et - EHT) * FS:(et - EHT + 1) * FS]
                sil_counts.append(EM("scalar", lambda e, dst=dst, et=et, sl=sl:
                                     e.activation(out=dst,
                                                  in_=spt[:, sl * FS:(sl + 1) * FS],
                                                  func=AF.Silu,
                                                  bias=cst[:, OFF_CB + et:OFF_CB + et + 1]),
                                     waits=[(sV, cvv), (sPe, P.cnt[sPe.name])],
                                     inc=sA))
            c_conv_done[s] = NOW(sV)
            return sil_counts[-1]

        def xc_rhs(sc):
            def f(k):
                if k < EHT:
                    return xch[:, k * FC + sc * FS:k * FC + (sc + 1) * FS]
                return xco[:, (k - EHT) * FS:(k - EHT + 1) * FS]
            return f

        def emit_dtbc(s, c_conv):
            sc = s % SPC
            for m in range(EHT):
                pp = (pC, pDm)[m % 2]
                slot = m % 2
                mm = mm_chain("wdt", m, ET, xc_rhs(sc), pp, slot,
                              extra_waits=[(sA, c_conv)])
                EM("scalar", lambda e, pp=pp, m=m: e.activation(
                    out=sp2[:, 0:FS], in_=pp[:], func=AF.Abs,
                    bias=cst[:, OFF_DTB + m:OFF_DTB + m + 1]),
                    waits=[(sPe, mm), (sV, NOW(sV))], inc=sA)
                EM("scalar", lambda e: e.activation(
                    out=sp2[:, 0:FS], in_=sp2[:, 0:FS], func=AF.Exp, scale=-1.0),
                    inc=sA)
                EM("scalar", lambda e: e.activation(
                    out=sp2[:, 0:FS], in_=sp2[:, 0:FS], func=AF.Ln, bias=1.0),
                    inc=sA)
                ca = EM("scalar", lambda e, pp=pp, m=m: e.activation(
                    out=sp2[:, FS:2 * FS], in_=pp[:], func=AF.Relu,
                    bias=cst[:, OFF_DTB + m:OFF_DTB + m + 1]), inc=sA)
                evac_done(pp, sA, ca)
                EM("vector", lambda e: e.tensor_tensor(
                    out=sp2[:, 0:FS], in0=sp2[:, 0:FS], in1=sp2[:, FS:2 * FS],
                    op=ALU.add), waits=[(sA, ca)], inc=sV)
                EM("vector", lambda e, m=m, sc=sc: e.tensor_scalar(
                    out=dtc[:, m * FC + sc * FS:m * FC + (sc + 1) * FS],
                    in0=sp2[:, 0:FS], scalar1=0.1, scalar2=0.001,
                    op0=ALU.min, op1=ALU.max),
                    waits=[(sA, c_scan_done_a)], inc=sV)
            mm = mm_chain("wbc", 0, ET, xc_rhs(sc), pC, 0,
                          extra_waits=[(sA, c_conv)])
            ca = EM("scalar", lambda e, sc=sc: e.activation(
                out=bcb[:, sc * FS:(sc + 1) * FS], in_=pC[:], func=AF.Copy),
                waits=[(sPe, mm), (sPe, c_bcb_free[0])], inc=sA)
            evac_done(pC, sA, ca)
            return NOW(sV), ca

        def emit_scan(c, c_dt_v, c_bc_a):
            nonlocal c_scan_done_v, c_scan_done_a
            for et in range(EHT):
                EM("vector", lambda e, et=et: e.tensor_tensor(
                    out=xch[:, et * FC:(et + 1) * FC],
                    in0=dtc[:, et * FC:(et + 1) * FC],
                    in1=xch[:, et * FC:(et + 1) * FC], op=ALU.mult),
                    waits=[(sA, c_bc_a), (sV, c_dt_v)] + qsnap("gpsimd"),
                    inc=sV)
            c_w = NOW(sV)
            grp_free = [0, 0]

            def emit_bc(g):
                n0 = g * NG
                gsl = (g % 2) * 8 * FC
                toks = []
                for ni in range(NG):
                    n = n0 + ni
                    for (ci, row) in ((0, n), (1, NST + n)):
                        dst_base = gsl + (ci * 4 + ni) * FC
                        for half in range(FC // FS):
                            mmb = EM("tensor", lambda e, row=row, half=half:
                                     e.matmul(
                                         pBC[:],
                                         lhsT=identb[:, row:row + 1].rearrange(
                                             "p (o f) -> p o f", o=1
                                         ).broadcast_to([128, 1, 128]),
                                         rhs=bcb[:, half * FS:(half + 1) * FS],
                                         start=True, stop=True),
                                     waits=mmwait(pBC, [(sA, c_bc_a),
                                                        (sV, grp_free[g % 2]),
                                                        c_id]),
                                     inc=sPe)
                            cev = EM("scalar", lambda e, dst_base=dst_base,
                                     half=half: e.activation(
                                         out=bcgbuf[:, dst_base + half * FS:
                                                    dst_base + (half + 1) * FS],
                                         in_=pBC[:], func=AF.Copy),
                                     waits=[(sPe, mmb)], inc=sA)
                            evac_done(pBC, sA, cev)
                    toks.append(NOW(sA))
                return toks

            scan_tok = {}
            u_tok = {}
            yc_tok = {}
            add_tok = {}
            pool_i = {}
            idx = 0
            for g in range(NGR):
                n0 = g * NG
                if g == 0:
                    bc_ready_cur = emit_bc(0)
                bc_ready = bc_ready_cur
                gsl = (g % 2) * 8 * FC
                for ep in range(EHT // 2):
                    if ep == 1 and g + 1 < NGR:
                        bc_ready_cur = emit_bc(g + 1)
                    for ni in range(NG):
                        n = n0 + ni
                        iA, iB = idx, idx + 1
                        idx += 2
                        ets = (2 * ep, 2 * ep + 1)
                        for st in (0, 1):
                            EM("scalar", lambda e, st=st, et=ets[st], n=n:
                               e.activation(
                                   out=dAb[st], in_=dtc[:, et * FC:(et + 1) * FC],
                                   func=AF.Exp,
                                   scale=cst[:, OFF_AL + n:OFF_AL + n + 1]),
                               waits=[(sV, c_dt_v), (sV, c_A),
                                      (sV, scan_tok.get((iA, iB)[st] - 2, 0))],
                               inc=sA)
                        c_exp = NOW(sA)
                        for st in (0, 1):
                            i_ = (iA, iB)[st]
                            u_tok[i_] = EM("vector", lambda e, st=st,
                                           et=ets[st], ni=ni, gsl=gsl: e.tensor_tensor(
                                               out=ub[st],
                                               in0=xch[:, et * FC:(et + 1) * FC],
                                               in1=bcgbuf[:, gsl + ni * FC:
                                                          gsl + (ni + 1) * FC],
                                               op=ALU.mult),
                                           waits=[(sA, bc_ready[ni]),
                                                  (sV, c_w),
                                                  (sV, add_tok.get(i_ - 2, 0)),
                                                  (sV, scan_tok.get(i_ - 2, 0))],
                                           inc=sV, selfw=False)
                        for st in (0, 1):
                            i_ = (iA, iB)[st]
                            scan_tok[i_] = EM("vector", lambda e, st=st,
                                              et=ets[st], n=n: e.tensor_tensor_scan(
                                                  out=hbs[st], data0=dAb[st],
                                                  data1=ub[st],
                                                  initial=hst[:, et * NST + n:
                                                              et * NST + n + 1],
                                                  op0=ALU.mult, op1=ALU.add),
                                              waits=[(sA, c_exp),
                                                     (sV, u_tok[i_]),
                                                     (sV, yc_tok.get(i_ - 2, 0)),
                                                     (sPl, pool_i.get(i_ - 2, 0)),
                                                     (sPl, pool_tok.get(
                                                         (ets[st], n, (c + 1) % 2), 0))],
                                              inc=sV, selfw=False)
                        for st in (0, 1):
                            i_ = (iA, iB)[st]
                            pool_i[i_] = pool_tok[(ets[st], n, c % 2)] = EM(
                                "gpsimd", lambda e, st=st, et=ets[st], n=n:
                                e.tensor_copy(
                                    out=hst[:, et * NST + n:et * NST + n + 1],
                                    in_=hbs[st][:, FC - 1:FC]),
                                waits=[(sV, scan_tok[i_])], inc=sPl)
                        for st in (0, 1):
                            i_ = (iA, iB)[st]
                            dst = tre[:, st * FC:(st + 1) * FC] if ni == 0 else ycbs[st]
                            yc_tok[i_] = EM("vector", lambda e, st=st, ni=ni,
                                            dst=dst, gsl=gsl: e.tensor_tensor(
                                                out=dst, in0=hbs[st],
                                                in1=bcgbuf[:, gsl + (4 + ni) * FC:
                                                           gsl + (4 + ni + 1) * FC],
                                                op=ALU.mult),
                                            waits=[(sV, scan_tok[i_]),
                                                   (sA, bc_ready[ni]),
                                                   (sV, add_tok.get(i_ - 2, 0))],
                                            inc=sV, selfw=False)
                        if ni > 0:
                            for st in (0, 1):
                                i_ = (iA, iB)[st]
                                add_tok[i_] = EM("vector", lambda e, st=st:
                                                 e.tensor_tensor(
                                                     out=tre[:, st * FC:(st + 1) * FC],
                                                     in0=ycbs[st],
                                                     in1=tre[:, st * FC:(st + 1) * FC],
                                                     op=ALU.add),
                                                 waits=[(sV, yc_tok[i_])],
                                                 inc=sV, selfw=False)
                        if ni == NG - 1:
                            for st in (0, 1):
                                i_ = (iA, iB)[st]
                                et = ets[st]
                                ydst = ybuf[:, et * FC:(et + 1) * FC]
                                wt = [(sV, add_tok[i_])]
                                if g == 0:
                                    wt.append((sPe, c_ybuf_free[0]))
                                    add_tok[i_] = EM("vector", lambda e, yd=ydst, st=st:
                                       e.tensor_copy(out=yd, in_=tre[:, st * FC:(st + 1) * FC]),
                                       waits=wt, inc=sV, selfw=False)
                                else:
                                    add_tok[i_] = EM("vector", lambda e, yd=ydst, st=st:
                                       e.tensor_tensor(
                                           out=yd, in0=yd,
                                           in1=tre[:, st * FC:(st + 1) * FC],
                                           op=ALU.add),
                                       waits=wt, inc=sV, selfw=False)
                grp_free[g % 2] = NOW(sV)
            c_bcb_free[0] = P.cnt[sPe.name]
            c_scan_done_v = NOW(sV)
            c_scan_done_a = NOW(sA)
            return c_scan_done_v

        def emit_yzop(c, c_scan_v):
            nonlocal c_z_read
            for et in range(EHT):
                EM("vector", lambda e, et=et: e.tensor_tensor(
                    out=xch[:, et * FC:(et + 1) * FC],
                    in0=ybuf[:, et * FC:(et + 1) * FC],
                    in1=zsil[:, et * FC:(et + 1) * FC], op=ALU.mult),
                    waits=[(sV, c_scan_v), (sA, NOW(sA))] + qsnap("gpsimd"),
                    inc=sV)
            c_yz = NOW(sV)
            c_z_read = c_yz
            wo_r = wbf[:, WOFF["wout"]:WOFF["wout"] + EHT * D].rearrange(
                "p (k d) -> p k d", d=D)
            npc = D // 128
            for pc_ in range(npc):
                slot = pc_ % 2
                lw = DMA("sync", lambda e, pc_=pc_, slot=slot: e.dma_start(
                    out=wst[:, slot * WSLOT:slot * WSLOT + EHT * 128].rearrange(
                        "p (k f) -> p k f", f=128),
                    in_=wo_r[:, :, pc_ * 128:(pc_ + 1) * 128]),
                    waits=[wload_slot_free[slot]] + c_wready)
                lastmm = None
                for tt in range(FC // 128):
                    pp = (pA, pB)[tt % 2]
                    last = None
                    for k in range(EHT):
                        w = [(sV, c_yz), lw]
                        if k == 0:
                            w = mmwait(pp, w)
                        last = EM("tensor", lambda e, pp=pp, k=k, tt=tt, slot=slot:
                                  e.matmul(
                                      pp[:, 0:128],
                                      lhsT=xch[:, k * FC + tt * 128:k * FC + (tt + 1) * 128],
                                      rhs=wst[:, slot * WSLOT + k * 128:slot * WSLOT + (k + 1) * 128],
                                      start=(k == 0), stop=(k == EHT - 1)),
                                  waits=w, inc=(sPe if k == EHT - 1 else None))
                    lastmm = last
                    psl = (pc_ * (FC // 128) + tt) % 4
                    cv = EM("vector", lambda e, pp=pp, psl=psl: e.tensor_copy(
                        out=ptile[:, psl * 128:(psl + 1) * 128], in_=pp[:, 0:128]),
                        waits=[(sPe, last), c_pt_free[psl]], inc=sV)
                    evac_done(pp, sV, cv)
                    cst_ = DMA("gpsimd", lambda e, tt=tt, c=c, pc_=pc_, psl=psl:
                               e.dma_start(
                        out=partial[c * FC + tt * 128:c * FC + (tt + 1) * 128,
                                    pc_ * 128:(pc_ + 1) * 128],
                        in_=ptile[:, psl * 128:(psl + 1) * 128]),
                        waits=[(sV, cv)])
                    c_pt_free[psl] = cst_
                wload_slot_free[slot] = (sPe, lastmm)
            c_ybuf_free[0] = P.cnt[sPe.name]

        c_pt_free = [(sV, 0)] * 4
        pool_tok = {}
        c_bcb_free = [0]
        c_ybuf_free = [0]

        pend = None
        for s in range(NSUB if stop_after != "prologue" else 0):
            c_axz = emit_A_xz(s)
            if s >= 1:
                sp = s - 1
                c_p = emit_patch(sp, c_axz)
                c_conv = emit_conv(sp, c_p)
                c_dt_v, c_bc_a = emit_dtbc(sp, c_conv)
                if (sp + 1) % SPC == 0:
                    cch = sp // SPC
                    if cch == 0:
                        DUMP("xch0", xch[:], [(sV, c_dt_v), (sA, c_bc_a)])
                        DUMP("dtc0", dtc[:], [(sV, c_dt_v)])
                        DUMP("bcb0", bcb[:], [(sA, c_bc_a)])
                    c_sv = emit_scan(cch, c_dt_v, c_bc_a)
                    if cch == 0:
                        DUMP("ybuf0", ybuf[:], [(sV, c_sv)])
                        DUMP("hst0", hst[:], [(sV, c_sv)])
                        DUMP("zsil0", zsil[:], [(sA, P.cnt[sA.name])])
                    emit_yzop(cch, c_sv)
            emit_A_z(s)
        if stop_after != "prologue":
            sp = NSUB - 1
            c_p = emit_patch(sp, NOW(sA))
            c_conv = emit_conv(sp, c_p)
            c_dt_v, c_bc_a = emit_dtbc(sp, c_conv)
            c_sv = emit_scan(NCH - 1, c_dt_v, c_bc_a)
            emit_yzop(NCH - 1, c_sv)
        c_pi_st = qsnap("gpsimd")

        c_g_free = [(sV, 0), (sV, 0)]
        for su in range(THS if stop_after in (None, "g", "cc", "pii") else 0):
            c_n = emit_ln(xo_r, su * FS, FS, None)
            c_h1_prev = 0
            for mh in range(FHT):
                mmg = mm_chain("fgu", 2 * mh, DT,
                               lambda k: normed[:, k * FS:(k + 1) * FS], pA, 0,
                               extra_waits=[(sV, c_n)])
                mmu = mm_chain("fgu", 2 * mh + 1, DT,
                               lambda k: normed[:, k * FS:(k + 1) * FS], pB, 1,
                               extra_waits=[(sV, c_n)])
                ca1 = EM("scalar", lambda e: e.activation(
                    out=sgbuf, in_=pA[:], func=AF.Silu),
                    waits=[(sPe, mmg), (sV, c_h1_prev)], inc=sA)
                ca2 = EM("scalar", lambda e: e.activation(
                    out=upbuf, in_=pB[:], func=AF.Copy),
                    waits=[(sPe, mmu)], inc=sA)
                evac_done(pA, sA, ca1)
                evac_done(pB, sA, ca2)
                c_h1_prev = EM("vector", lambda e, mh=mh: e.tensor_tensor(
                    out=h1b[:, mh * FS:(mh + 1) * FS], in0=sgbuf,
                    in1=upbuf, op=ALU.mult),
                    waits=[(sA, ca2), (sPe, P.cnt[sPe.name])], inc=sV)
            c_h1 = NOW(sV)
            for md in range(DT):
                pp = (pC, pDm)[md % 2]
                sl = md % 2
                mm = mm_chain("fdn", md, FHT,
                              lambda k: h1b[:, k * FS:(k + 1) * FS], pp, md % 2,
                              extra_waits=[(sV, c_h1)])
                ca = EM("scalar", lambda e, pp=pp, sl=sl: e.activation(
                    out=gbuf[:, sl * FS:(sl + 1) * FS], in_=pp[:], func=AF.Silu),
                    waits=[(sPe, mm), c_g_free[sl]], inc=sA)
                evac_done(pp, sA, ca)
                cg = DMA("gpsimd", lambda e, md=md, su=su, sl=sl: e.dma_start(
                    out=gdram.rearrange("(dt p) t -> p dt t", p=128)[
                        :, md, su * FS:(su + 1) * FS],
                    in_=gbuf[:, sl * FS:(sl + 1) * FS]),
                    waits=[(sA, ca)])
                c_g_free[sl] = cg

        c_cc = 0
        if stop_after in (None, "cc", "pii"):
            c_cc = EM("gpsimd", lambda e: e.collective_compute(
                "ReduceScatter", ALU.add,
                replica_groups=[[0, 1], [2, 3], [4, 5], [6, 7]],
                ins=[partial.opt()], outs=[s1own.opt()]),
                waits=c_pi_st, inc=sCC, amt=1)

        s1_r = s1own
        if stop_after not in (None, "pii"):
            for su in range(THS):
                DMA("gpsimd", lambda e, su=su: e.dma_start(
                    out=out.rearrange("(dt p) t -> p dt t", p=128)[
                        :, :, su * FS:(su + 1) * FS],
                    in_=xsub[:].rearrange("p (dt t) -> p dt t", t=FS)),
                    waits=[(sV, P.cnt[sV.name])])
        for su in range(THS if stop_after in (None, "pii") else 0):
            c_n = emit_ln(xo_r, su * FS, FS, None, want_f32=x32a)
            lds = []
            sv_snap2 = P.cnt[sV.name]
            for d in range(DT):
                lds.append(DMA("scalar", lambda e, d=d, su=su: e.dma_start(
                    out=sfin[:, d * FS:(d + 1) * FS],
                    in_=s1_r[su * FS:(su + 1) * FS, d * 128:(d + 1) * 128],
                    transpose=True),
                    waits=[(sCC, c_cc), (sV, sv_snap2)]))
            cmid = None
            for d in range(DT):
                cmid = EM("vector", lambda e, d=d: e.tensor_tensor(
                    out=x32b[:, d * FS:(d + 1) * FS], in0=x32a[:, d * FS:(d + 1) * FS],
                    in1=sfin[:, d * FS:(d + 1) * FS], op=ALU.add),
                    waits=[lds[d], (sV, c_n)], inc=sV)
            sqs = []
            for d in range(DT):
                sqs.append(EM("scalar", lambda e, d=d: e.activation(
                    out=xco[:, d * FS:(d + 1) * FS], in_=x32b[:, d * FS:(d + 1) * FS],
                    func=AF.Square), waits=[(sV, cmid)], inc=sA))
            m2 = None
            for d in range(DT):
                m2 = EM("tensor", lambda e, d=d: e.matmul(
                    pS1[:, 0:FS], lhsT=onesc[:], rhs=xco[:, d * FS:(d + 1) * FS],
                    start=(d == 0), stop=(d == DT - 1)),
                    waits=mmwait(pS1, [(sA, sqs[d])]) if d == 0 else [(sA, sqs[d])],
                    inc=(sPe if d == DT - 1 else None))
            cv1 = EM("vector", lambda e: e.tensor_scalar(
                out=rows[:, 0:FS], in0=pS1[0:1, 0:FS], scalar1=1.0 / D, scalar2=None,
                op0=ALU.mult), waits=[(sPe, m2)], inc=sV)
            evac_done(pS1, sV, cv1)
            EM("scalar", lambda e: e.activation(
                out=rows[:, FS:2 * FS], in_=rows[:, 0:FS], func=AF.Ln, bias=EPS),
                waits=[(sV, cv1)], inc=sA)
            ca = EM("scalar", lambda e: e.activation(
                out=rows[:, FS:2 * FS], in_=rows[:, FS:2 * FS], func=AF.Exp,
                scale=-0.5), inc=sA)
            cv2 = EM("vector", lambda e: e.tensor_copy(
                out=rowb[:, 0:FS], in_=rows[:, FS:2 * FS]),
                waits=[(sA, ca)], inc=sV)
            mb = EM("tensor", lambda e: e.matmul(
                pSc[:, 0:FS], lhsT=onesr[:], rhs=rowb[:, 0:FS], start=True,
                stop=True), waits=mmwait(pSc, [(sV, cv2)]), inc=sPe)
            cn2 = None
            for d in range(DT):
                cn2 = EM("vector", lambda e, d=d: e.scalar_tensor_tensor(
                    out=normed[:, d * FS:(d + 1) * FS],
                    in0=x32b[:, d * FS:(d + 1) * FS],
                    scalar=cst[:, OFF_WFN + d:OFF_WFN + d + 1], in1=pSc[:, 0:FS],
                    op0=ALU.mult, op1=ALU.mult), waits=[(sPe, mb)], inc=sV)
            evac_done(pSc, sV, cn2)
            c_h1_prev = 0
            for mh in range(ET):
                mmg = mm_chain("wgu", 2 * mh, DT,
                               lambda k: normed[:, k * FS:(k + 1) * FS], pA, 0,
                               extra_waits=[(sV, cn2)])
                mmu = mm_chain("wgu", 2 * mh + 1, DT,
                               lambda k: normed[:, k * FS:(k + 1) * FS], pB, 1,
                               extra_waits=[(sV, cn2)])
                ca1 = EM("scalar", lambda e: e.activation(
                    out=sgbuf, in_=pA[:], func=AF.Silu),
                    waits=[(sPe, mmg), (sV, c_h1_prev)], inc=sA)
                ca2 = EM("scalar", lambda e: e.activation(
                    out=upbuf, in_=pB[:], func=AF.Copy),
                    waits=[(sPe, mmu)], inc=sA)
                evac_done(pA, sA, ca1)
                evac_done(pB, sA, ca2)
                c_h1_prev = EM("vector", lambda e, mh=mh: e.tensor_tensor(
                    out=h1b[:, mh * FS:(mh + 1) * FS], in0=sgbuf,
                    in1=upbuf, op=ALU.mult),
                    waits=[(sA, ca2), (sPe, P.cnt[sPe.name])], inc=sV)
            c_h1 = NOW(sV)
            ldg = DMA("scalar", lambda e, su=su: e.dma_start(
                out=gbuf.rearrange("p (dt t) -> p dt t", t=FS),
                in_=gdram.rearrange("(dt p) t -> p dt t", p=128)[
                    :, :, su * FS:(su + 1) * FS]),
                waits=qsnap("gpsimd") + [(sV, c_h1)])
            for md in range(DT):
                pp = (pC, pDm)[md % 2]
                mm = mm_chain("wdn", md, ET,
                              lambda k: h1b[:, k * FS:(k + 1) * FS], pp, md % 2,
                              extra_waits=[(sV, c_h1)])
                cs = EM("vector", lambda e, pp=pp, md=md: e.tensor_tensor(
                    out=sfin[:, md * FS:(md + 1) * FS], in0=pp[:],
                    in1=x32b[:, md * FS:(md + 1) * FS], op=ALU.add),
                    waits=[(sPe, mm)], inc=sV)
                evac_done(pp, sV, cs)
                EM("vector", lambda e, md=md: e.tensor_tensor(
                    out=spt[:, 0:FS], in0=sfin[:, md * FS:(md + 1) * FS],
                    in1=gbuf[:, md * FS:(md + 1) * FS], op=ALU.mult),
                    waits=[ldg], inc=sV)
                cfo = EM("vector", lambda e, md=md: e.tensor_tensor(
                    out=x32a[:, md * FS:(md + 1) * FS],
                    in0=xsub[:, md * FS:(md + 1) * FS], in1=spt[:, 0:FS],
                    op=ALU.add), inc=sV)
            c_fin = NOW(sV)
            DMA("gpsimd", lambda e, su=su: e.dma_start(
                out=out.rearrange("(dt p) t -> p dt t", p=128)[
                    :, :, su * FS:(su + 1) * FS],
                in_=x32a[:].rearrange("p (dt t) -> p dt t", t=FS)),
                waits=[(sV, c_fin)])
        EM("gpsimd", lambda e: e.nop(),
           waits=qsnap("gpsimd") + qsnap("sync") + qsnap("scalar"))

        nc._dumps = dumps
        qs = {e: [] for e in Prog.ENG}
        for op in P.ops:
            qs[op.eng].append(op)

        def runner(ops):
            def run(eng):
                for op in ops:
                    for (s, c) in op.waits:
                        eng.wait_ge(s, c)
                    ins = op.fn(eng)
                    if op.inc is not None:
                        ins.then_inc(op.inc, op.amt)
            return run

        with nc.Block() as block:
            block.sync(runner(qs["sync"]))
            block.tensor(runner(qs["tensor"]))
            block.vector(runner(qs["vector"]))
            block.scalar(runner(qs["scalar"]))
            block.gpsimd(runner(qs["gpsimd"]))

    return nc


def kernel(**inputs):
    T = int(np.asarray(inputs["L0_attn_x"]).shape[1])
    nc = build_nc(T=T)
    in_maps = []
    for core in range(8):
        b, j = core // 2, core % 2
        in_maps.append(pack_core(inputs, b, j))
    res = run_bass_kernel_spmd(nc, in_maps, core_ids=list(range(8)))
    x = np.asarray(inputs["L0_attn_x"])
    B, T_, D_ = x.shape
    out = np.empty((B, T_, D_), np.float32)
    for core in range(8):
        b, j = core // 2, core % 2
        o = res.results[core]["out"]
        out[b, j * (T_ // 2):(j + 1) * (T_ // 2), :] = o.T
    return out


# version 09
# speedup vs baseline: 1.2249x; 1.0059x over previous
import contextlib
import numpy as np
import ml_dtypes
import concourse.bass as bass
import concourse.mybir as mybir
from concourse.bass_utils import run_bass_kernel_spmd

F32 = mybir.dt.float32
BF16 = mybir.dt.bfloat16
I32 = mybir.dt.int32
AF = mybir.ActivationFunctionType
ALU = mybir.AluOpType

D = 1024
E = 2048
EH = 1024
NST = 64
FH = 3072
DT = D // 128
ET = E // 128
EHT = EH // 128
FHT = FH // 128
EPS = 1e-5

_SIZES_F = [("ident", 128), ("wi", (ET + EHT) * DT * 128), ("wdt", EHT * ET * 128),
            ("wbc", 1 * ET * 128), ("wout", EHT * D),
            ("wgu", 2 * ET * DT * 128), ("wdn", DT * ET * 128)]
_SIZES_I = [("fgu", 2 * FHT * DT * 128), ("fdn", DT * FHT * 128)]
XF_COLS = sum(s for _, s in _SIZES_F)
XI_COLS = sum(s for _, s in _SIZES_I)
WOFF = {}
_o = 0
for _n, _s in _SIZES_F:
    WOFF[_n] = _o
    _o += _s
for _n, _s in _SIZES_I:
    WOFF[_n] = _o
    _o += _s
OFF_CW, OFF_CB, OFF_DTB, OFF_WIN, OFF_WFN, OFF_AL = 0, 64, 80, 88, 96, 104
OFF_K = 168
CONSTS = (1e-5, 1.0, -1.0, -0.5)
SM_COLS = OFF_K + len(CONSTS)


def _tileize(w, kt, mt):
    a = np.asarray(w).reshape(kt, 128, mt, 128).transpose(2, 1, 0, 3)
    return np.ascontiguousarray(
        a.reshape(mt, 128, kt * 128).transpose(1, 0, 2).reshape(128, mt * kt * 128))


def _rhs_tiles(w, kt, f):
    a = np.asarray(w).reshape(kt, 128, f)
    return np.ascontiguousarray(a.transpose(1, 0, 2).reshape(128, kt * f))


def _interleave_m(a, b):
    Dd, M = a.shape
    mt = M // 128
    o = np.empty((Dd, 2 * M), a.dtype)
    for m in range(mt):
        o[:, (2 * m) * 128:(2 * m + 1) * 128] = a[:, m * 128:(m + 1) * 128]
        o[:, (2 * m + 1) * 128:(2 * m + 2) * 128] = b[:, m * 128:(m + 1) * 128]
    return o


def pack_core(inp, b, j):
    f32 = np.float32
    x = np.asarray(inp["L0_attn_x"], f32)[b]
    T = x.shape[0]
    xT = np.ascontiguousarray(x.T)
    xo = np.ascontiguousarray(xT[:, j * (T // 2):(j + 1) * (T // 2)])

    perm = np.concatenate([np.arange(j * EH, (j + 1) * EH),
                           np.arange((1 - j) * EH, (2 - j) * EH)])
    Wi = np.asarray(inp["L0_attn_in_proj_W"], f32)
    cols = np.concatenate([perm, E + j * EH + np.arange(EH)])
    parts_f = [
        np.eye(128, dtype=f32),
        _tileize(Wi[:, cols], DT, ET + EHT),
        _tileize(np.asarray(inp["L0_attn_dt_W"], f32)[perm][:, j * EH:(j + 1) * EH],
                 ET, EHT),
        _tileize(np.concatenate([np.asarray(inp["L0_attn_B_W"], f32)[perm],
                                 np.asarray(inp["L0_attn_C_W"], f32)[perm]], axis=1),
                 ET, 1),
        _rhs_tiles(np.asarray(inp["L0_attn_out_proj_W"], f32)[j * EH:(j + 1) * EH],
                   EHT, D),
        _tileize(_interleave_m(np.asarray(inp["L0_attn_gate_W"], f32),
                               np.asarray(inp["L0_attn_up_W"], f32)), DT, 2 * ET),
        _tileize(np.asarray(inp["L0_attn_down_W"], f32), ET, DT),
    ]
    wf = np.concatenate(parts_f, axis=1)
    parts_i = [
        _tileize(_interleave_m(np.asarray(inp["L0_ffn_w_gate"], np.int32).T,
                               np.asarray(inp["L0_ffn_w_up"], np.int32).T),
                 DT, 2 * FHT),
        _tileize(np.asarray(inp["L0_ffn_w_down"], np.int32).T, FHT, DT),
    ]
    wi = np.concatenate(parts_i, axis=1)

    cw = np.asarray(inp["L0_attn_conv_W"], f32)[:, 0, :].T[perm]
    cwp = cw.reshape(ET, 128, 4).transpose(1, 0, 2).reshape(128, ET * 4)
    cb = np.asarray(inp["L0_attn_conv_b"], f32)[perm].reshape(ET, 128).T
    dtb = np.asarray(inp["L0_attn_dt_b"], f32)[j * EH:(j + 1) * EH].reshape(EHT, 128).T
    win = np.asarray(inp["L0_attn_input_norm_w"], f32).reshape(DT, 128).T
    wfn = np.asarray(inp["L0_attn_ffn_norm_w"], f32).reshape(DT, 128).T
    alog = np.broadcast_to(np.asarray(inp["L0_attn_A_log"], f32)[None, :],
                           (128, NST)).copy()
    kc = np.broadcast_to(np.asarray(CONSTS, f32)[None, :], (128, len(CONSTS))).copy()
    smalls = np.concatenate([cwp, cb, dtb, win, wfn, alog, kc], axis=1).astype(f32)
    assert smalls.shape[1] == SM_COLS
    return {"x": xT, "x_own": xo, "wf32": np.ascontiguousarray(wf),
            "wi32": np.ascontiguousarray(wi), "smalls": smalls}


class Op:
    __slots__ = ("eng", "waits", "inc", "amt", "fn")

    def __init__(self, eng, waits, inc, amt, fn):
        self.eng, self.waits, self.inc, self.amt, self.fn = eng, waits, inc, amt, fn


class Prog:
    ENG = ("sync", "tensor", "vector", "scalar", "gpsimd")

    def __init__(self):
        self.ops = []
        self.cnt = {}
        self.seen = {e: {} for e in self.ENG}

    def reg(self, s):
        self.cnt[s.name] = 0
        return s

    def emit(self, eng, fn, waits=(), inc=None, amt=1, selfw=True):
        waits = list(waits)
        if eng in ("vector", "scalar") and selfw and inc is not None:
            waits.append((inc, self.cnt[inc.name]))
        kept = []
        for (s, c) in waits:
            if c > 0 and self.seen[eng].get(s.name, 0) < c:
                kept.append((s, c))
                self.seen[eng][s.name] = c
        self.ops.append(Op(eng, kept, inc, amt, fn))
        if inc is not None:
            self.cnt[inc.name] += amt
            return self.cnt[inc.name]
        return None

    def now(self, s):
        return self.cnt[s.name]


def build_nc(T=8192, FC=1024, FS=512, NG=4, debug=False, dump=False, stop_after=None):
    NCH = T // FC
    NSUB = T // FS
    SPC = FC // FS
    TH = T // 2
    THS = TH // FS
    NGR = NST // NG

    nc = bass.Bass("TRN2", target_bir_lowering=False, debug=debug)

    x_in = nc.dram_tensor("x", [D, T], F32, kind="ExternalInput").ap()
    xown = nc.dram_tensor("x_own", [D, TH], F32, kind="ExternalInput").ap()
    wf32 = nc.dram_tensor("wf32", [128, XF_COLS], F32, kind="ExternalInput").ap()
    wi32 = nc.dram_tensor("wi32", [128, XI_COLS], I32, kind="ExternalInput").ap()
    smalls = nc.dram_tensor("smalls", [128, SM_COLS], F32, kind="ExternalInput").ap()
    out = nc.dram_tensor("out", [D, TH], F32, kind="ExternalOutput").ap()

    wbf_t = nc.dram_tensor("wbf", [128, XF_COLS + XI_COLS], BF16)
    partial_t = nc.dram_tensor("partial", [T, D], BF16)
    s1own_t = nc.dram_tensor("s1own", [TH, D], BF16)
    gdram_t = nc.dram_tensor("gdram", [D, TH], BF16)
    wbf = wbf_t.ap()
    partial = partial_t.ap()
    s1own = s1own_t.ap()
    gdram = gdram_t.ap()

    P = Prog()
    dumps = {}

    with contextlib.ExitStack() as st:
        def sb(name, shape, dt):
            return st.enter_context(nc.sbuf_tensor(name, shape, dt))

        def psb(name, shape):
            return st.enter_context(nc.psum_tensor(name, shape, F32))

        def sem(name):
            return P.reg(st.enter_context(nc.semaphore(name)))

        xsub = sb("xsub", [128, DT * FS], F32)
        normed = sb("normed", [128, DT * FS], BF16)
        WSLOT = 3072
        wst = sb("wst", [128, 2 * WSLOT], BF16)
        xzs = [sb(f"xz{i}", [128, ET * (FS + 3)], BF16) for i in range(2)]
        chunkbuf = sb("chunkbuf", [128, max(3 * EHT * FC, FHT * FS)], BF16)
        zsil = chunkbuf[:, 0:EHT * FC]
        dtc = chunkbuf[:, EHT * FC:2 * EHT * FC]
        xch = chunkbuf[:, 2 * EHT * FC:3 * EHT * FC]
        h1b = chunkbuf[:, 0:FHT * FS]
        xco = sb("xco", [128, EHT * FS], BF16)
        bcb = sb("bcb", [128, FC], BF16)
        bcgbuf = sb("bcgbuf", [128, max(4 * NG * FC, 4 * DT * FS)], BF16)
        bcg = [bcgbuf[:, 0:2 * NG * FC], bcgbuf[:, 2 * NG * FC:4 * NG * FC]]
        x32a = bcgbuf[:, 0:2 * DT * FS].bitcast(F32)
        x32b = bcgbuf[:, 2 * DT * FS:4 * DT * FS].bitcast(F32)
        dabuf = sb("dabuf", [128, max(2 * FC, DT * FS // 2)], F32)
        dAb = [dabuf[:, 0:FC], dabuf[:, FC:2 * FC]]
        sfin = dabuf[:, 0:DT * FS // 2].bitcast(BF16)
        scanb = sb("scanb", [128, max(6 * FC, DT * FS)], BF16)
        ub = [scanb[:, 0:FC], scanb[:, FC:2 * FC]]
        hbs = [[scanb[:, 2 * FC:3 * FC], scanb[:, 3 * FC:4 * FC]],
               [scanb[:, 4 * FC:5 * FC], scanb[:, 5 * FC:6 * FC]]]
        ycbs = ub
        gbuf = scanb[:, 0:DT * FS]
        BCGC = max(4 * NG * FC, 4 * DT * FS)
        cvtf = bcgbuf[:, 0:BCGC].bitcast(F32)
        cvto = chunkbuf[:, 0:min(8192, 3 * EHT * FC)]
        tre = sb("tre", [128, max(2 * FC, 2 * FS)], BF16)
        sgbuf = tre[:, 0:FS]
        upbuf = tre[:, FS:2 * FS]
        ybuf = sb("ybuf", [128, EHT * FC], BF16)
        hst = sb("hst", [128, EHT * NST], F32)
        ptile = sb("ptile", [128, 4 * 128], BF16)
        spt = sb("spt", [128, 2 * FS], F32)
        sp2 = spt
        rows = spt[0:1, 0:2 * FS]
        rowb = sb("rowb", [1, FS], BF16)
        onesc = sb("onesc", [128, 1], BF16)
        onesr = sb("onesr", [1, 128], BF16)
        cst = sb("cst", [128, SM_COLS], F32)
        identb = sb("identb", [128, 128], BF16)
        tailb = [sb(f"tail{i}", [128, ET], BF16) for i in range(2)]

        pA = psb("pA", [128, FS])
        pB = psb("pB", [128, FS])
        pC = psb("pC", [128, FS])
        pDm = psb("pD", [128, FS])
        pS1 = psb("pS1", [1, FS])
        pBC = psb("pBC", [128, FS])
        pM = psb("pM", [128, FS])
        pSc = psb("pSc", [128, FS])

        sIn = [sem(f"sIn{i}") for i in range(4)]
        sG = [sem(f"sG{i}") for i in range(4)]
        sScq = [sem(f"sScq{i}") for i in range(4)]
        sPe = sem("sPe")
        sV = sem("sV")
        sA = sem("sA")
        sCC = sem("sCC")
        sPl = sem("sPl")

        EM = P.emit
        NOW = P.now
        for ki, kv in enumerate(CONSTS):
            nc.const_aps.register(F32, kv, cst[:, OFF_K + ki:OFF_K + ki + 1])

        QSEM = {"sync": sIn, "gpsimd": sG, "scalar": sScq}
        QROT = {"sync": 0, "gpsimd": 0, "scalar": 0}

        def DMA(eng, fn, waits=()):
            q = QSEM[eng][QROT[eng] % 4]
            QROT[eng] += 1
            w = list(waits) + [(q, P.cnt[q.name])]
            EM(eng, fn, waits=w, inc=q, amt=16)
            return (q, P.cnt[q.name])

        def qsnap(eng):
            return [(s, P.cnt[s.name]) for s in QSEM[eng]]

        def DUMP(name, ap, waits):
            if not dump:
                return
            t = nc.dram_tensor("dbg_" + name, list(ap.shape), ap.dtype)
            dumps[name] = t
            DMA("gpsimd", lambda e, ap=ap, t=t: e.dma_start(out=t.ap(), in_=ap),
                waits=waits)
        pfree = {id(pA): (sV, 0), id(pB): (sV, 0), id(pC): (sV, 0), id(pDm): (sV, 0),
                 id(pS1): (sV, 0), id(pBC): (sV, 0), id(pM): (sV, 0), id(pSc): (sV, 0)}

        c_sm = DMA("sync", lambda e: e.dma_start(out=cst[:], in_=smalls))
        EM("vector", lambda e: e.memset(onesc[:], 1.0), inc=sV)
        EM("vector", lambda e: e.memset(onesr[:], 1.0), inc=sV)
        EM("vector", lambda e: e.memset(hst[:], 0.0), inc=sV)
        c_A1 = EM("scalar", lambda e: e.activation(
            out=cst[:, OFF_AL:OFF_AL + NST], in_=cst[:, OFF_AL:OFF_AL + NST],
            func=AF.Exp), waits=[c_sm], inc=sA)
        c_A = EM("vector", lambda e: e.tensor_scalar(
            out=cst[:, OFF_AL:OFF_AL + NST], in0=cst[:, OFF_AL:OFF_AL + NST],
            scalar1=-1.0, scalar2=None, op0=ALU.mult),
            waits=[(sA, c_A1)], inc=sV)

        CH = min(BCGC // 2, 3 * EHT * FC)
        prev_cv = 0
        for (tot, srcap, isint) in ((XF_COLS, wf32, False), (XI_COLS, wi32, True)):
            nchk = (tot + CH - 1) // CH
            base = 0 if not isint else XF_COLS
            for i in range(nchk):
                c0, w = i * CH, min((i + 1) * CH, tot) - i * CH
                st_toks = qsnap("gpsimd")
                if isint:
                    ld = DMA("sync", lambda e, c0=c0, w=w, srcap=srcap: e.dma_start(
                        out=cvtf[:, 0:w].bitcast(I32),
                        in_=srcap[:, c0:c0 + w]),
                        waits=[(sV, prev_cv)])
                    cv = EM("vector", lambda e, w=w: e.tensor_scalar(
                        out=cvto[:, 0:w], in0=cvtf[:, 0:w].bitcast(I32),
                        scalar1=1.0 / 127.0, scalar2=None, op0=ALU.mult),
                        waits=[ld] + st_toks, inc=sV)
                else:
                    ld = DMA("sync", lambda e, c0=c0, w=w, srcap=srcap: e.dma_start(
                        out=cvtf[:, 0:w], in_=srcap[:, c0:c0 + w]),
                        waits=[(sV, prev_cv)])
                    cv = EM("vector", lambda e, w=w: e.tensor_copy(
                        out=cvto[:, 0:w], in_=cvtf[:, 0:w]),
                        waits=[ld] + st_toks, inc=sV)
                prev_cv = cv
                DMA("gpsimd", lambda e, c0=c0, w=w, base=base: e.dma_start(
                    out=wbf[:, base + c0:base + c0 + w], in_=cvto[:, 0:w]),
                    waits=[(sV, cv)])
        c_wready = qsnap("gpsimd")
        c_id = DMA("sync", lambda e: e.dma_start(
            out=identb[:], in_=wbf[:, WOFF["ident"]:WOFF["ident"] + 128]),
            waits=c_wready)

        x_r = x_in.rearrange("(dt p) t -> p dt t", p=128)
        xo_r = xown.rearrange("(dt p) t -> p dt t", p=128)

        def mmwait(pp, extra):
            w = list(extra)
            w.append(pfree[id(pp)])
            return w

        def evac_done(pp, sem_, cnt):
            pfree[id(pp)] = (sem_, cnt)

        def emit_ln(src_r, t0, f, wcol, want_f32=None, nb_dst=None):
            nb = nb_dst if nb_dst is not None else normed
            ld = DMA("sync", lambda e: e.dma_start(
                out=xsub[:].rearrange("p (dt t) -> p dt t", t=f),
                in_=src_r[:, :, t0:t0 + f]),
                waits=[(sV, P.cnt[sV.name]), (sA, P.cnt[sA.name])])
            pe_snap = P.cnt[sPe.name]
            cp1 = EM("vector", lambda e: e.tensor_copy(
                out=nb[:, 0:DT * f], in_=xsub[:, 0:DT * f]),
                waits=[ld, (sPe, pe_snap)], inc=sV)
            cps = [cp1] * DT
            sq1 = EM("scalar", lambda e: e.activation(
                out=xco[:, 0:DT * f], in_=xsub[:, 0:DT * f],
                func=AF.Square), waits=[ld, (sPe, pe_snap)], inc=sA)
            sqs = [sq1] * DT
            m1 = m2 = None
            for d in range(DT):
                m1 = EM("tensor", lambda e, d=d: e.matmul(
                    pS1[:, 0:f], lhsT=onesc[:], rhs=nb[:, d * f:(d + 1) * f],
                    start=(d == 0), stop=(d == DT - 1)),
                    waits=mmwait(pS1, [(sV, cps[d])]) if d == 0 else [(sV, cps[d])],
                    inc=(sPe if d == DT - 1 else None))
            cm0 = EM("vector", lambda e: e.tensor_scalar(
                out=rows[:, 0:f], in0=pS1[0:1, 0:f], scalar1=1.0 / D, scalar2=None,
                op0=ALU.mult), waits=[(sPe, m1)], inc=sV)
            evac_done(pS1, sV, cm0)
            for d in range(DT):
                m2 = EM("tensor", lambda e, d=d: e.matmul(
                    pS1[:, 0:f], lhsT=onesc[:], rhs=xco[:, d * f:(d + 1) * f],
                    start=(d == 0), stop=(d == DT - 1)),
                    waits=mmwait(pS1, [(sA, sqs[d])]) if d == 0 else [(sA, sqs[d])],
                    inc=(sPe if d == DT - 1 else None))
            cvm = EM("vector", lambda e: e.tensor_copy(
                out=rowb[:, 0:f], in_=rows[:, 0:f]), inc=sV)
            mb = EM("tensor", lambda e: e.matmul(
                pM[:, 0:f], lhsT=onesr[:], rhs=rowb[:, 0:f], start=True, stop=True),
                waits=mmwait(pM, [(sV, cvm)]), inc=sPe)
            EM("vector", lambda e: e.tensor_tensor(
                out=rows[:, f:2 * f], in0=rows[:, 0:f], in1=rows[:, 0:f],
                op=ALU.mult), inc=sV)
            cv1 = EM("vector", lambda e: e.scalar_tensor_tensor(
                out=rows[:, f:2 * f], in0=pS1[0:1, 0:f], scalar=1.0 / D,
                in1=rows[:, f:2 * f], op0=ALU.mult, op1=ALU.subtract),
                waits=[(sPe, m2)], inc=sV)
            evac_done(pS1, sV, cv1)
            EM("scalar", lambda e: e.activation(
                out=rows[:, f:2 * f], in_=rows[:, f:2 * f], func=AF.Ln, bias=EPS),
                waits=[(sV, cv1)], inc=sA)
            ca = EM("scalar", lambda e: e.activation(
                out=rows[:, 0:f], in_=rows[:, f:2 * f], func=AF.Exp,
                scale=-0.5), waits=[(sV, cvm)], inc=sA)
            if wcol is not None:
                EM("scalar", lambda e: e.activation(
                    out=rows[:, f:2 * f], in_=rows[:, f:2 * f], func=AF.Exp,
                    scale=-1.0), inc=sA)
                ca2 = EM("vector", lambda e: e.tensor_scalar(
                    out=rows[:, f:2 * f], in0=rows[:, f:2 * f], scalar1=-EPS,
                    scalar2=1.0, op0=ALU.mult, op1=ALU.add),
                    waits=[(sA, NOW(sA))], inc=sV)
                EM("scalar", lambda e: e.activation(
                    out=rows[:, f:2 * f], in_=rows[:, f:2 * f], func=AF.Ln,
                    bias=EPS), waits=[(sV, ca2)], inc=sA)
                EM("scalar", lambda e: e.activation(
                    out=rows[:, f:2 * f], in_=rows[:, f:2 * f], func=AF.Exp,
                    scale=-0.5), inc=sA)
                cv3 = EM("vector", lambda e: e.tensor_tensor(
                    out=rows[:, f:2 * f], in0=rows[:, 0:f],
                    in1=rows[:, f:2 * f], op=ALU.mult),
                    waits=[(sA, NOW(sA))], inc=sV)
            else:
                cv3 = EM("vector", lambda e: e.tensor_copy(
                    out=rows[:, f:2 * f], in_=rows[:, 0:f]),
                    waits=[(sA, ca)], inc=sV)
            cv4 = EM("vector", lambda e: e.tensor_copy(
                out=rowb[:, 0:f], in_=rows[:, f:2 * f]),
                waits=[(sPe, mb)], inc=sV)
            sc = EM("tensor", lambda e: e.matmul(
                pSc[:, 0:f], lhsT=onesr[:], rhs=rowb[:, 0:f], start=True,
                stop=True), waits=mmwait(pSc, [(sV, cv4)]), inc=sPe)
            last = None
            sa_snap = P.cnt[sA.name]
            for d in range(DT):
                EM("vector", lambda e, d=d: e.tensor_tensor(
                    out=spt[:, 0:f], in0=xsub[:, d * f:(d + 1) * f], in1=pM[:, 0:f],
                    op=ALU.subtract), waits=[(sPe, sc), (sA, sa_snap)], inc=sV)
                if want_f32 is not None:
                    EM("vector", lambda e, d=d: e.tensor_tensor(
                        out=want_f32[:, d * f:(d + 1) * f], in0=spt[:, 0:f],
                        in1=pSc[:, 0:f], op=ALU.mult), inc=sV)
                if wcol is not None:
                    last = EM("vector", lambda e, d=d: e.scalar_tensor_tensor(
                        out=nb[:, d * f:(d + 1) * f], in0=spt[:, 0:f],
                        scalar=cst[:, wcol + d:wcol + d + 1], in1=pSc[:, 0:f],
                        op0=ALU.mult, op1=ALU.mult), inc=sV)
                else:
                    last = EM("vector", lambda e, d=d: e.tensor_tensor(
                        out=nb[:, d * f:(d + 1) * f], in0=spt[:, 0:f],
                        in1=pSc[:, 0:f], op=ALU.mult), inc=sV)
            evac_done(pM, sV, last)
            evac_done(pSc, sV, last)
            return last

        wload_slot_free = [(sPe, 0), (sPe, 0)]

        def w_load(region, m, kt, slot):
            off = WOFF[region] + m * kt * 128
            return DMA("sync", lambda e: e.dma_start(
                out=wst[:, slot * WSLOT:slot * WSLOT + kt * 128],
                in_=wbf[:, off:off + kt * 128]),
                waits=[wload_slot_free[slot]] + c_wready)

        def wtile(slot, k):
            return wst[:, slot * WSLOT + k * 128:slot * WSLOT + (k + 1) * 128]

        def mm_chain(region, m, kt, rhs_fn, pp, slot, extra_waits=()):
            lw = w_load(region, m, kt, slot)
            last = None
            for k in range(kt):
                w = [lw] + list(extra_waits)
                if k == 0:
                    w = mmwait(pp, w)
                last = EM("tensor",
                          lambda e, k=k, pp=pp, slot=slot, rhs_fn=rhs_fn: e.matmul(
                              pp[:], lhsT=wtile(slot, k), rhs=rhs_fn(k),
                              start=(k == 0), stop=(k == kt - 1)),
                          waits=w, inc=(sPe if k == kt - 1 else None))
            wload_slot_free[slot] = (sPe, last)
            return last

        c_conv_done = [0] * NSUB
        c_z_read = 0
        c_scan_done_v = 0
        c_scan_done_a = 0
        c_ln_q = {}

        def emit_A_xz(s):
            sc = s % SPC
            xz = xzs[s % 2]
            gate = [(sV, c_conv_done[s - 2])] if s >= 2 else []
            c_n = emit_ln(x_r, s * FS, FS, OFF_WIN)
            c_ln_q[s] = c_n
            for m in range(ET):
                pp = (pA, pB)[m % 2]
                slot = m % 2
                mm = mm_chain("wi", m, DT, lambda k: normed[:, k * FS:(k + 1) * FS],
                              pp, slot, extra_waits=[(sV, c_n)])
                ca = EM("scalar", lambda e, m=m, pp=pp, xz=xz, sc=sc: e.activation(
                    out=xz[:, m * (FS + 3) + 1:m * (FS + 3) + 1 + FS],
                    in_=pp[:], func=AF.Copy),
                    waits=[(sPe, mm)] + gate, inc=sA)
                evac_done(pp, sA, ca)
            return NOW(sA)

        def emit_A_z(s):
            sc = s % SPC
            for mz in range(EHT):
                m = ET + mz
                pp = (pA, pB)[m % 2]
                slot = m % 2
                mm = mm_chain("wi", m, DT, lambda k: normed[:, k * FS:(k + 1) * FS],
                              pp, slot, extra_waits=[(sV, c_ln_q[s]), (sV, c_z_read)])
                ca = EM("scalar", lambda e, mz=mz, pp=pp, sc=sc: e.activation(
                    out=zsil[:, mz * FC + sc * FS:mz * FC + (sc + 1) * FS],
                    in_=pp[:], func=AF.Silu),
                    waits=[(sPe, mm)], inc=sA)
                evac_done(pp, sA, ca)
            return NOW(sA)

        def emit_patch(s, c_next_xz):
            xz = xzs[s % 2]
            tb = tailb[s % 2]
            tbp = tailb[(s - 1) % 2]
            last = None
            for m in range(ET):
                b0 = m * (FS + 3)
                if s == NSUB - 1:
                    last = EM("vector", lambda e, b0=b0: e.memset(
                        xz[:, b0 + 1 + FS:b0 + 3 + FS], 0.0),
                        waits=[(sA, c_next_xz)], inc=sV)
                else:
                    xzn = xzs[(s + 1) % 2]
                    last = EM("vector", lambda e, b0=b0, xzn=xzn: e.tensor_copy(
                        out=xz[:, b0 + 1 + FS:b0 + 3 + FS],
                        in_=xzn[:, b0 + 1:b0 + 3]),
                        waits=[(sA, c_next_xz)], inc=sV)
                last = EM("vector", lambda e, b0=b0, m=m, tb=tb: e.tensor_copy(
                    out=tb[:, m:m + 1], in_=xz[:, b0 + FS:b0 + FS + 1]), inc=sV)
                if s == 0:
                    last = EM("vector", lambda e, b0=b0: e.memset(
                        xz[:, b0:b0 + 1], 0.0), inc=sV)
                else:
                    last = EM("vector", lambda e, b0=b0, m=m, tbp=tbp: e.tensor_copy(
                        out=xz[:, b0:b0 + 1], in_=tbp[:, m:m + 1]), inc=sV)
            return last

        def emit_conv(s, c_patch):
            sc = s % SPC
            xz = xzs[s % 2]
            sil_counts = []
            for et in range(ET):
                b0 = et * (FS + 3)
                sl = et % 2
                EM("vector", lambda e, b0=b0, et=et, sl=sl: e.tensor_scalar(
                    out=spt[:, sl * FS:(sl + 1) * FS], in0=xz[:, b0:b0 + FS],
                    scalar1=cst[:, OFF_CW + et * 4:OFF_CW + et * 4 + 1],
                    scalar2=None, op0=ALU.mult),
                    waits=[(sV, c_patch), (sA, sil_counts[-2] if et >= 2 else 0),
                           (sA, c_scan_done_a)],
                    inc=sV)
                for w in range(1, 4):
                    EM("vector", lambda e, b0=b0, et=et, w=w, sl=sl:
                       e.scalar_tensor_tensor(
                           out=spt[:, sl * FS:(sl + 1) * FS],
                           in0=xz[:, b0 + w:b0 + w + FS],
                           scalar=cst[:, OFF_CW + et * 4 + w:OFF_CW + et * 4 + w + 1],
                           in1=spt[:, sl * FS:(sl + 1) * FS],
                           op0=ALU.mult, op1=ALU.add), inc=sV)
                cvv = NOW(sV)
                if et < EHT:
                    dst = xch[:, et * FC + sc * FS:et * FC + (sc + 1) * FS]
                else:
                    dst = xco[:, (et - EHT) * FS:(et - EHT + 1) * FS]
                sil_counts.append(EM("scalar", lambda e, dst=dst, et=et, sl=sl:
                                     e.activation(out=dst,
                                                  in_=spt[:, sl * FS:(sl + 1) * FS],
                                                  func=AF.Silu,
                                                  bias=cst[:, OFF_CB + et:OFF_CB + et + 1]),
                                     waits=[(sV, cvv), (sPe, P.cnt[sPe.name])],
                                     inc=sA))
            c_conv_done[s] = NOW(sV)
            return sil_counts[-1]

        def xc_rhs(sc):
            def f(k):
                if k < EHT:
                    return xch[:, k * FC + sc * FS:k * FC + (sc + 1) * FS]
                return xco[:, (k - EHT) * FS:(k - EHT + 1) * FS]
            return f

        def emit_dtbc(s, c_conv):
            sc = s % SPC
            for m in range(EHT):
                pp = (pC, pDm)[m % 2]
                slot = m % 2
                mm = mm_chain("wdt", m, ET, xc_rhs(sc), pp, slot,
                              extra_waits=[(sA, c_conv)])
                EM("scalar", lambda e, pp=pp, m=m: e.activation(
                    out=sp2[:, 0:FS], in_=pp[:], func=AF.Abs,
                    bias=cst[:, OFF_DTB + m:OFF_DTB + m + 1]),
                    waits=[(sPe, mm), (sV, NOW(sV))], inc=sA)
                EM("scalar", lambda e: e.activation(
                    out=sp2[:, 0:FS], in_=sp2[:, 0:FS], func=AF.Exp, scale=-1.0),
                    inc=sA)
                EM("scalar", lambda e: e.activation(
                    out=sp2[:, 0:FS], in_=sp2[:, 0:FS], func=AF.Ln, bias=1.0),
                    inc=sA)
                ca = EM("scalar", lambda e, pp=pp, m=m: e.activation(
                    out=sp2[:, FS:2 * FS], in_=pp[:], func=AF.Relu,
                    bias=cst[:, OFF_DTB + m:OFF_DTB + m + 1]), inc=sA)
                evac_done(pp, sA, ca)
                EM("vector", lambda e: e.tensor_tensor(
                    out=sp2[:, 0:FS], in0=sp2[:, 0:FS], in1=sp2[:, FS:2 * FS],
                    op=ALU.add), waits=[(sA, ca)], inc=sV)
                EM("vector", lambda e, m=m, sc=sc: e.tensor_scalar(
                    out=dtc[:, m * FC + sc * FS:m * FC + (sc + 1) * FS],
                    in0=sp2[:, 0:FS], scalar1=0.1, scalar2=0.001,
                    op0=ALU.min, op1=ALU.max),
                    waits=[(sA, c_scan_done_a)], inc=sV)
            mm = mm_chain("wbc", 0, ET, xc_rhs(sc), pC, 0,
                          extra_waits=[(sA, c_conv)])
            ca = EM("scalar", lambda e, sc=sc: e.activation(
                out=bcb[:, sc * FS:(sc + 1) * FS], in_=pC[:], func=AF.Copy),
                waits=[(sPe, mm), (sPe, c_bcb_free[0])], inc=sA)
            evac_done(pC, sA, ca)
            return NOW(sV), ca

        def emit_scan(c, c_dt_v, c_bc_a):
            nonlocal c_scan_done_v, c_scan_done_a
            for et in range(EHT):
                EM("vector", lambda e, et=et: e.tensor_tensor(
                    out=xch[:, et * FC:(et + 1) * FC],
                    in0=dtc[:, et * FC:(et + 1) * FC],
                    in1=xch[:, et * FC:(et + 1) * FC], op=ALU.mult),
                    waits=[(sA, c_bc_a), (sV, c_dt_v)] + qsnap("gpsimd"),
                    inc=sV)
            c_w = NOW(sV)
            grp_free = [0, 0]

            def emit_bc(g):
                n0 = g * NG
                gsl = (g % 2) * 8 * FC
                toks = []
                for ni in range(NG):
                    n = n0 + ni
                    for (ci, row) in ((0, n), (1, NST + n)):
                        dst_base = gsl + (ci * 4 + ni) * FC
                        for half in range(FC // FS):
                            mmb = EM("tensor", lambda e, row=row, half=half:
                                     e.matmul(
                                         pBC[:],
                                         lhsT=identb[:, row:row + 1].rearrange(
                                             "p (o f) -> p o f", o=1
                                         ).broadcast_to([128, 1, 128]),
                                         rhs=bcb[:, half * FS:(half + 1) * FS],
                                         start=True, stop=True),
                                     waits=mmwait(pBC, [(sA, c_bc_a),
                                                        (sV, grp_free[g % 2]),
                                                        c_id]),
                                     inc=sPe)
                            cev = EM("scalar", lambda e, dst_base=dst_base,
                                     half=half: e.activation(
                                         out=bcgbuf[:, dst_base + half * FS:
                                                    dst_base + (half + 1) * FS],
                                         in_=pBC[:], func=AF.Copy),
                                     waits=[(sPe, mmb)], inc=sA)
                            evac_done(pBC, sA, cev)
                    toks.append(NOW(sA))
                return toks

            scan_tok = {}
            u_tok = {}
            yc_tok = {}
            add_tok = {}
            pool_i = {}
            idx = 0
            for g in range(NGR):
                n0 = g * NG
                if g == 0:
                    bc_ready_cur = emit_bc(0)
                bc_ready = bc_ready_cur
                gsl = (g % 2) * 8 * FC
                for ep in range(EHT // 2):
                    if ep == 1 and g + 1 < NGR:
                        bc_ready_cur = emit_bc(g + 1)
                    for ni in range(NG):
                        n = n0 + ni
                        iA, iB = idx, idx + 1
                        idx += 2
                        ets = (2 * ep, 2 * ep + 1)
                        for st in (0, 1):
                            EM("scalar", lambda e, st=st, et=ets[st], n=n:
                               e.activation(
                                   out=dAb[st], in_=dtc[:, et * FC:(et + 1) * FC],
                                   func=AF.Exp,
                                   scale=cst[:, OFF_AL + n:OFF_AL + n + 1]),
                               waits=[(sV, c_dt_v), (sV, c_A),
                                      (sV, scan_tok.get((iA, iB)[st] - 2, 0))],
                               inc=sA)
                        c_exp = NOW(sA)
                        for st in (0, 1):
                            i_ = (iA, iB)[st]
                            u_tok[i_] = EM("vector", lambda e, st=st,
                                           et=ets[st], ni=ni, gsl=gsl: e.tensor_tensor(
                                               out=ub[st],
                                               in0=xch[:, et * FC:(et + 1) * FC],
                                               in1=bcgbuf[:, gsl + ni * FC:
                                                          gsl + (ni + 1) * FC],
                                               op=ALU.mult),
                                           waits=[(sA, bc_ready[ni]),
                                                  (sV, c_w),
                                                  (sV, add_tok.get(i_ - 2, 0)),
                                                  (sV, scan_tok.get(i_ - 2, 0))],
                                           inc=sV, selfw=False)
                        for st in (0, 1):
                            i_ = (iA, iB)[st]
                            hsl = (i_ // 2) % 2
                            scan_tok[i_] = EM("vector", lambda e, st=st, hsl=hsl,
                                              et=ets[st], n=n: e.tensor_tensor_scan(
                                                  out=hbs[st][hsl], data0=dAb[st],
                                                  data1=ub[st],
                                                  initial=hst[:, et * NST + n:
                                                              et * NST + n + 1],
                                                  op0=ALU.mult, op1=ALU.add),
                                              waits=[(sA, c_exp),
                                                     (sV, u_tok[i_]),
                                                     (sV, yc_tok.get(i_ - 4, 0)),
                                                     (sPl, pool_i.get(i_ - 4, 0))],
                                              inc=sV, selfw=False)
                        for st in (0, 1):
                            i_ = (iA, iB)[st]
                            hsl = (i_ // 2) % 2
                            pool_i[i_] = EM(
                                "gpsimd", lambda e, st=st, hsl=hsl, et=ets[st], n=n:
                                e.tensor_copy(
                                    out=hst[:, et * NST + n:et * NST + n + 1],
                                    in_=hbs[st][hsl][:, FC - 1:FC]),
                                waits=[(sV, scan_tok[i_])], inc=sPl)
                        for st in (0, 1):
                            i_ = (iA, iB)[st]
                            dst = tre[:, st * FC:(st + 1) * FC] if ni == 0 else ycbs[st]
                            hsl = (i_ // 2) % 2
                            yc_tok[i_] = EM("vector", lambda e, st=st, ni=ni, hsl=hsl,
                                            dst=dst, gsl=gsl: e.tensor_tensor(
                                                out=dst, in0=hbs[st][hsl],
                                                in1=bcgbuf[:, gsl + (4 + ni) * FC:
                                                           gsl + (4 + ni + 1) * FC],
                                                op=ALU.mult),
                                            waits=[(sV, scan_tok[i_]),
                                                   (sA, bc_ready[ni]),
                                                   (sV, add_tok.get(i_ - 2, 0))],
                                            inc=sV, selfw=False)
                        if ni > 0:
                            for st in (0, 1):
                                i_ = (iA, iB)[st]
                                add_tok[i_] = EM("vector", lambda e, st=st:
                                                 e.tensor_tensor(
                                                     out=tre[:, st * FC:(st + 1) * FC],
                                                     in0=ycbs[st],
                                                     in1=tre[:, st * FC:(st + 1) * FC],
                                                     op=ALU.add),
                                                 waits=[(sV, yc_tok[i_])],
                                                 inc=sV, selfw=False)
                        if ni == NG - 1:
                            for st in (0, 1):
                                i_ = (iA, iB)[st]
                                et = ets[st]
                                ydst = ybuf[:, et * FC:(et + 1) * FC]
                                wt = [(sV, add_tok[i_])]
                                if g == 0:
                                    wt.append((sPe, c_ybuf_free[0]))
                                    add_tok[i_] = EM("vector", lambda e, yd=ydst, st=st:
                                       e.tensor_copy(out=yd, in_=tre[:, st * FC:(st + 1) * FC]),
                                       waits=wt, inc=sV, selfw=False)
                                else:
                                    add_tok[i_] = EM("vector", lambda e, yd=ydst, st=st:
                                       e.tensor_tensor(
                                           out=yd, in0=yd,
                                           in1=tre[:, st * FC:(st + 1) * FC],
                                           op=ALU.add),
                                       waits=wt, inc=sV, selfw=False)
                grp_free[g % 2] = NOW(sV)
            c_bcb_free[0] = P.cnt[sPe.name]
            c_scan_done_v = NOW(sV)
            c_scan_done_a = NOW(sA)
            return c_scan_done_v

        def emit_yzop(c, c_scan_v):
            nonlocal c_z_read
            for et in range(EHT):
                EM("vector", lambda e, et=et: e.tensor_tensor(
                    out=xch[:, et * FC:(et + 1) * FC],
                    in0=ybuf[:, et * FC:(et + 1) * FC],
                    in1=zsil[:, et * FC:(et + 1) * FC], op=ALU.mult),
                    waits=[(sV, c_scan_v), (sA, NOW(sA))] + qsnap("gpsimd"),
                    inc=sV)
            c_yz = NOW(sV)
            c_z_read = c_yz
            wo_r = wbf[:, WOFF["wout"]:WOFF["wout"] + EHT * D].rearrange(
                "p (k d) -> p k d", d=D)
            npc = D // 128
            for pc_ in range(npc):
                slot = pc_ % 2
                lw = DMA("sync", lambda e, pc_=pc_, slot=slot: e.dma_start(
                    out=wst[:, slot * WSLOT:slot * WSLOT + EHT * 128].rearrange(
                        "p (k f) -> p k f", f=128),
                    in_=wo_r[:, :, pc_ * 128:(pc_ + 1) * 128]),
                    waits=[wload_slot_free[slot]] + c_wready)
                lastmm = None
                for tt in range(FC // 128):
                    pp = (pA, pB)[tt % 2]
                    last = None
                    for k in range(EHT):
                        w = [(sV, c_yz), lw]
                        if k == 0:
                            w = mmwait(pp, w)
                        last = EM("tensor", lambda e, pp=pp, k=k, tt=tt, slot=slot:
                                  e.matmul(
                                      pp[:, 0:128],
                                      lhsT=xch[:, k * FC + tt * 128:k * FC + (tt + 1) * 128],
                                      rhs=wst[:, slot * WSLOT + k * 128:slot * WSLOT + (k + 1) * 128],
                                      start=(k == 0), stop=(k == EHT - 1)),
                                  waits=w, inc=(sPe if k == EHT - 1 else None))
                    lastmm = last
                    psl = (pc_ * (FC // 128) + tt) % 4
                    cv = EM("vector", lambda e, pp=pp, psl=psl: e.tensor_copy(
                        out=ptile[:, psl * 128:(psl + 1) * 128], in_=pp[:, 0:128]),
                        waits=[(sPe, last), c_pt_free[psl]], inc=sV)
                    evac_done(pp, sV, cv)
                    cst_ = DMA("gpsimd", lambda e, tt=tt, c=c, pc_=pc_, psl=psl:
                               e.dma_start(
                        out=partial[c * FC + tt * 128:c * FC + (tt + 1) * 128,
                                    pc_ * 128:(pc_ + 1) * 128],
                        in_=ptile[:, psl * 128:(psl + 1) * 128]),
                        waits=[(sV, cv)])
                    c_pt_free[psl] = cst_
                wload_slot_free[slot] = (sPe, lastmm)
            c_ybuf_free[0] = P.cnt[sPe.name]

        c_pt_free = [(sV, 0)] * 4
        pool_tok = {}
        c_bcb_free = [0]
        c_ybuf_free = [0]

        pend = None
        for s in range(NSUB if stop_after != "prologue" else 0):
            c_axz = emit_A_xz(s)
            if s >= 1:
                sp = s - 1
                c_p = emit_patch(sp, c_axz)
                c_conv = emit_conv(sp, c_p)
                c_dt_v, c_bc_a = emit_dtbc(sp, c_conv)
                if (sp + 1) % SPC == 0:
                    cch = sp // SPC
                    if cch == 0:
                        DUMP("xch0", xch[:], [(sV, c_dt_v), (sA, c_bc_a)])
                        DUMP("dtc0", dtc[:], [(sV, c_dt_v)])
                        DUMP("bcb0", bcb[:], [(sA, c_bc_a)])
                    c_sv = emit_scan(cch, c_dt_v, c_bc_a)
                    if cch == 0:
                        DUMP("ybuf0", ybuf[:], [(sV, c_sv)])
                        DUMP("hst0", hst[:], [(sV, c_sv)])
                        DUMP("zsil0", zsil[:], [(sA, P.cnt[sA.name])])
                    emit_yzop(cch, c_sv)
            emit_A_z(s)
        if stop_after != "prologue":
            sp = NSUB - 1
            c_p = emit_patch(sp, NOW(sA))
            c_conv = emit_conv(sp, c_p)
            c_dt_v, c_bc_a = emit_dtbc(sp, c_conv)
            c_sv = emit_scan(NCH - 1, c_dt_v, c_bc_a)
            emit_yzop(NCH - 1, c_sv)
        c_pi_st = qsnap("gpsimd")

        c_g_free = [(sV, 0), (sV, 0)]
        for su in range(THS if stop_after in (None, "g", "cc", "pii") else 0):
            c_n = emit_ln(xo_r, su * FS, FS, None)
            c_h1_prev = 0
            for mh in range(FHT):
                mmg = mm_chain("fgu", 2 * mh, DT,
                               lambda k: normed[:, k * FS:(k + 1) * FS], pA, 0,
                               extra_waits=[(sV, c_n)])
                mmu = mm_chain("fgu", 2 * mh + 1, DT,
                               lambda k: normed[:, k * FS:(k + 1) * FS], pB, 1,
                               extra_waits=[(sV, c_n)])
                ca1 = EM("scalar", lambda e: e.activation(
                    out=sgbuf, in_=pA[:], func=AF.Silu),
                    waits=[(sPe, mmg), (sV, c_h1_prev)], inc=sA)
                ca2 = EM("scalar", lambda e: e.activation(
                    out=upbuf, in_=pB[:], func=AF.Copy),
                    waits=[(sPe, mmu)], inc=sA)
                evac_done(pA, sA, ca1)
                evac_done(pB, sA, ca2)
                c_h1_prev = EM("vector", lambda e, mh=mh: e.tensor_tensor(
                    out=h1b[:, mh * FS:(mh + 1) * FS], in0=sgbuf,
                    in1=upbuf, op=ALU.mult),
                    waits=[(sA, ca2), (sPe, P.cnt[sPe.name])], inc=sV)
            c_h1 = NOW(sV)
            for md in range(DT):
                pp = (pC, pDm)[md % 2]
                sl = md % 2
                mm = mm_chain("fdn", md, FHT,
                              lambda k: h1b[:, k * FS:(k + 1) * FS], pp, md % 2,
                              extra_waits=[(sV, c_h1)])
                ca = EM("scalar", lambda e, pp=pp, sl=sl: e.activation(
                    out=gbuf[:, sl * FS:(sl + 1) * FS], in_=pp[:], func=AF.Silu),
                    waits=[(sPe, mm), c_g_free[sl]], inc=sA)
                evac_done(pp, sA, ca)
                cg = DMA("gpsimd", lambda e, md=md, su=su, sl=sl: e.dma_start(
                    out=gdram.rearrange("(dt p) t -> p dt t", p=128)[
                        :, md, su * FS:(su + 1) * FS],
                    in_=gbuf[:, sl * FS:(sl + 1) * FS]),
                    waits=[(sA, ca)])
                c_g_free[sl] = cg

        c_cc = 0
        if stop_after in (None, "cc", "pii"):
            c_cc = EM("gpsimd", lambda e: e.collective_compute(
                "ReduceScatter", ALU.add,
                replica_groups=[[0, 1], [2, 3], [4, 5], [6, 7]],
                ins=[partial.opt()], outs=[s1own.opt()]),
                waits=c_pi_st, inc=sCC, amt=1)

        s1_r = s1own
        if stop_after not in (None, "pii"):
            for su in range(THS):
                DMA("gpsimd", lambda e, su=su: e.dma_start(
                    out=out.rearrange("(dt p) t -> p dt t", p=128)[
                        :, :, su * FS:(su + 1) * FS],
                    in_=xsub[:].rearrange("p (dt t) -> p dt t", t=FS)),
                    waits=[(sV, P.cnt[sV.name])])
        for su in range(THS if stop_after in (None, "pii") else 0):
            c_n = emit_ln(xo_r, su * FS, FS, None, want_f32=x32a)
            lds = []
            sv_snap2 = P.cnt[sV.name]
            for d in range(DT):
                lds.append(DMA("scalar", lambda e, d=d, su=su: e.dma_start(
                    out=sfin[:, d * FS:(d + 1) * FS],
                    in_=s1_r[su * FS:(su + 1) * FS, d * 128:(d + 1) * 128],
                    transpose=True),
                    waits=[(sCC, c_cc), (sV, sv_snap2)]))
            cmid = None
            for d in range(DT):
                cmid = EM("vector", lambda e, d=d: e.tensor_tensor(
                    out=x32b[:, d * FS:(d + 1) * FS], in0=x32a[:, d * FS:(d + 1) * FS],
                    in1=sfin[:, d * FS:(d + 1) * FS], op=ALU.add),
                    waits=[lds[d], (sV, c_n)], inc=sV)
            sqs = []
            for d in range(DT):
                sqs.append(EM("scalar", lambda e, d=d: e.activation(
                    out=xco[:, d * FS:(d + 1) * FS], in_=x32b[:, d * FS:(d + 1) * FS],
                    func=AF.Square), waits=[(sV, cmid)], inc=sA))
            m2 = None
            for d in range(DT):
                m2 = EM("tensor", lambda e, d=d: e.matmul(
                    pS1[:, 0:FS], lhsT=onesc[:], rhs=xco[:, d * FS:(d + 1) * FS],
                    start=(d == 0), stop=(d == DT - 1)),
                    waits=mmwait(pS1, [(sA, sqs[d])]) if d == 0 else [(sA, sqs[d])],
                    inc=(sPe if d == DT - 1 else None))
            cv1 = EM("vector", lambda e: e.tensor_scalar(
                out=rows[:, 0:FS], in0=pS1[0:1, 0:FS], scalar1=1.0 / D, scalar2=None,
                op0=ALU.mult), waits=[(sPe, m2)], inc=sV)
            evac_done(pS1, sV, cv1)
            EM("scalar", lambda e: e.activation(
                out=rows[:, FS:2 * FS], in_=rows[:, 0:FS], func=AF.Ln, bias=EPS),
                waits=[(sV, cv1)], inc=sA)
            ca = EM("scalar", lambda e: e.activation(
                out=rows[:, FS:2 * FS], in_=rows[:, FS:2 * FS], func=AF.Exp,
                scale=-0.5), inc=sA)
            cv2 = EM("vector", lambda e: e.tensor_copy(
                out=rowb[:, 0:FS], in_=rows[:, FS:2 * FS]),
                waits=[(sA, ca)], inc=sV)
            mb = EM("tensor", lambda e: e.matmul(
                pSc[:, 0:FS], lhsT=onesr[:], rhs=rowb[:, 0:FS], start=True,
                stop=True), waits=mmwait(pSc, [(sV, cv2)]), inc=sPe)
            cn2 = None
            for d in range(DT):
                cn2 = EM("vector", lambda e, d=d: e.scalar_tensor_tensor(
                    out=normed[:, d * FS:(d + 1) * FS],
                    in0=x32b[:, d * FS:(d + 1) * FS],
                    scalar=cst[:, OFF_WFN + d:OFF_WFN + d + 1], in1=pSc[:, 0:FS],
                    op0=ALU.mult, op1=ALU.mult), waits=[(sPe, mb)], inc=sV)
            evac_done(pSc, sV, cn2)
            c_h1_prev = 0
            for mh in range(ET):
                mmg = mm_chain("wgu", 2 * mh, DT,
                               lambda k: normed[:, k * FS:(k + 1) * FS], pA, 0,
                               extra_waits=[(sV, cn2)])
                mmu = mm_chain("wgu", 2 * mh + 1, DT,
                               lambda k: normed[:, k * FS:(k + 1) * FS], pB, 1,
                               extra_waits=[(sV, cn2)])
                ca1 = EM("scalar", lambda e: e.activation(
                    out=sgbuf, in_=pA[:], func=AF.Silu),
                    waits=[(sPe, mmg), (sV, c_h1_prev)], inc=sA)
                ca2 = EM("scalar", lambda e: e.activation(
                    out=upbuf, in_=pB[:], func=AF.Copy),
                    waits=[(sPe, mmu)], inc=sA)
                evac_done(pA, sA, ca1)
                evac_done(pB, sA, ca2)
                c_h1_prev = EM("vector", lambda e, mh=mh: e.tensor_tensor(
                    out=h1b[:, mh * FS:(mh + 1) * FS], in0=sgbuf,
                    in1=upbuf, op=ALU.mult),
                    waits=[(sA, ca2), (sPe, P.cnt[sPe.name])], inc=sV)
            c_h1 = NOW(sV)
            ldg = DMA("scalar", lambda e, su=su: e.dma_start(
                out=gbuf.rearrange("p (dt t) -> p dt t", t=FS),
                in_=gdram.rearrange("(dt p) t -> p dt t", p=128)[
                    :, :, su * FS:(su + 1) * FS]),
                waits=qsnap("gpsimd") + [(sV, c_h1)])
            for md in range(DT):
                pp = (pC, pDm)[md % 2]
                mm = mm_chain("wdn", md, ET,
                              lambda k: h1b[:, k * FS:(k + 1) * FS], pp, md % 2,
                              extra_waits=[(sV, c_h1)])
                cs = EM("vector", lambda e, pp=pp, md=md: e.tensor_tensor(
                    out=sfin[:, md * FS:(md + 1) * FS], in0=pp[:],
                    in1=x32b[:, md * FS:(md + 1) * FS], op=ALU.add),
                    waits=[(sPe, mm)], inc=sV)
                evac_done(pp, sV, cs)
                EM("vector", lambda e, md=md: e.tensor_tensor(
                    out=spt[:, 0:FS], in0=sfin[:, md * FS:(md + 1) * FS],
                    in1=gbuf[:, md * FS:(md + 1) * FS], op=ALU.mult),
                    waits=[ldg], inc=sV)
                cfo = EM("vector", lambda e, md=md: e.tensor_tensor(
                    out=x32a[:, md * FS:(md + 1) * FS],
                    in0=xsub[:, md * FS:(md + 1) * FS], in1=spt[:, 0:FS],
                    op=ALU.add), inc=sV)
            c_fin = NOW(sV)
            DMA("gpsimd", lambda e, su=su: e.dma_start(
                out=out.rearrange("(dt p) t -> p dt t", p=128)[
                    :, :, su * FS:(su + 1) * FS],
                in_=x32a[:].rearrange("p (dt t) -> p dt t", t=FS)),
                waits=[(sV, c_fin)])
        EM("gpsimd", lambda e: e.nop(),
           waits=qsnap("gpsimd") + qsnap("sync") + qsnap("scalar"))

        nc._dumps = dumps
        qs = {e: [] for e in Prog.ENG}
        for op in P.ops:
            qs[op.eng].append(op)

        def runner(ops):
            def run(eng):
                for op in ops:
                    for (s, c) in op.waits:
                        eng.wait_ge(s, c)
                    ins = op.fn(eng)
                    if op.inc is not None:
                        ins.then_inc(op.inc, op.amt)
            return run

        with nc.Block() as block:
            block.sync(runner(qs["sync"]))
            block.tensor(runner(qs["tensor"]))
            block.vector(runner(qs["vector"]))
            block.scalar(runner(qs["scalar"]))
            block.gpsimd(runner(qs["gpsimd"]))

    return nc


def kernel(**inputs):
    T = int(np.asarray(inputs["L0_attn_x"]).shape[1])
    nc = build_nc(T=T)
    in_maps = []
    for core in range(8):
        b, j = core // 2, core % 2
        in_maps.append(pack_core(inputs, b, j))
    res = run_bass_kernel_spmd(nc, in_maps, core_ids=list(range(8)))
    x = np.asarray(inputs["L0_attn_x"])
    B, T_, D_ = x.shape
    out = np.empty((B, T_, D_), np.float32)
    for core in range(8):
        b, j = core // 2, core % 2
        o = res.results[core]["out"]
        out[b, j * (T_ // 2):(j + 1) * (T_ // 2), :] = o.T
    return out
